# version 11
# baseline (speedup 1.0000x reference)
from contextlib import ExitStack

import numpy as np
import concourse.bass as bass
import concourse.mybir as mybir
from concourse.bass_utils import run_bass_kernel_spmd

F32 = mybir.dt.float32
BF16 = mybir.dt.bfloat16
I32 = mybir.dt.int32
AF = mybir.ActivationFunctionType
OP = mybir.AluOpType
AX = mybir.AxisListType

NCORE = 8
S = 8192
D = 2048
H = 16
DH = 128
DFF = 8192
T = 1024
NT = 8
CH = 512

KVW = 8192 + 8448
KV_AREA = 128 * KVW
STATS = 1024
ROW = KV_AREA + STATS

NPAIR = 17
TAB_K, TAB_V, TAB_Q, TAB_O = 0, 17, 34, 51

MAGIC = 12582912.0
SC_QK = np.float32(1.0 / np.sqrt(128.0))
EXPC = np.float32(-15.0)
MASKVAL = -1.0e6
LN_EPS = 1.0e-5
RMS_EPS = 1.0e-6
Q_EPS = 1.0e-5

ENGS = ["sync", "gpsimd", "tensor", "vector", "scalar"]


class Sem:
    __slots__ = ("name", "v")

    def __init__(self, name):
        self.name = name
        self.v = 0

    def inc(self, n=1):
        self.v += n
        return self.v


FCNT = {e: 0 for e in ENGS}
FSEM = {"vector": "Fv", "scalar": "Fa", "gpsimd": "Fg", "tensor": "Ft",
        "sync": "Fs"}
FENCE_ALL = {"A": False, "B": False, "C": False, "D": True, "E1": False,
             "E": True, "F": True}
E1VN = True
E1CP = False


class Phase:

    def __init__(self, fence_all=False):
        self.fence_all = fence_all
        self.plan = {e: [] for e in ENGS}
        self.regs = {e: [] for e in ENGS}

    def want_regs(self, eng, names):
        self.regs[eng] = list(names)

    def step(self, eng, act, waits=(), inc=None, fence=False):
        if self.fence_all and eng in ("vector", "scalar"):
            fence = True
        inc_val = inc[0].inc(inc[1]) if inc is not None else None
        self.plan[eng].append((list(waits), act, inc, fence, inc_val))
        return inc_val

    def dma(self, eng, sem, out_ap, in_ap, waits=(), **kw):
        return self.step(
            eng,
            (lambda e, r, o=out_ap, i=in_ap, kw=kw: e.dma_start(out=o, in_=i, **kw)),
            waits=waits, inc=(sem, 16), fence=False)

    def run(self, nc, sems):
        def mk(engname):
            steps = self.plan[engname]
            regnames = self.regs[engname]
            fsem = sems[FSEM[engname]]

            def closure(eng):
                def F(inst):
                    FCNT[engname] += 1
                    inst.then_inc(fsem, 1)
                    eng.wait_ge(fsem, FCNT[engname])
                    return inst

                with ExitStack() as rs:
                    regs = {n: rs.enter_context(eng.register(n)) for n in regnames}
                    last = {}
                    for si, (waits, act, inc, fence, inc_val) in enumerate(steps):
                        for s, v in waits:
                            if v > 0 and last.get(s.name, -1) < v:
                                eng.wait_ge(sems[s.name], v)
                                last[s.name] = v
                        try:
                            if act is None:
                                inst = None
                            elif (act.__code__.co_argcount >= 3
                                  and act.__code__.co_varnames[2] == "F"):
                                inst = act(eng, regs, F)
                            else:
                                inst = act(eng, regs)
                        except Exception:
                            import traceback as _tb
                            print(f"STEP FAIL eng={engname} idx={si}")
                            _tb.print_exc()
                            raise
                        if inc is not None:
                            inst.then_inc(sems[inc[0].name], inc[1])
                            if fence:
                                eng.wait_ge(sems[inc[0].name], inc_val)
                        elif fence and inst is not None:
                            F(inst)
            return closure

        with nc.Block() as block:
            for engname in ENGS:
                if self.plan[engname]:
                    getattr(block, engname)(mk(engname))


def build_program():
    for k in FCNT:
        FCNT[k] = 0
    nc = bass.Bass("TRN2", target_bir_lowering=False, detect_race_conditions=False)

    xin = nc.dram_tensor("xin", [T, D], F32, kind="ExternalInput")
    wqkv = nc.dram_tensor("wqkv", [3, H, D, DH], F32, kind="ExternalInput")
    woutT = nc.dram_tensor("woutT", [16, 128, D], F32, kind="ExternalInput")
    w1c = nc.dram_tensor("w1c", [32, D, 256], F32, kind="ExternalInput")
    w2c = nc.dram_tensor("w2c", [32, D, 256], F32, kind="ExternalInput")
    w3c = nc.dram_tensor("w3c", [64, 128, D], F32, kind="ExternalInput")
    w1s = nc.dram_tensor("w1s", [2, 128, 8192], F32, kind="ExternalInput")
    w2s = nc.dram_tensor("w2s", [2, 128, 8192], F32, kind="ExternalInput")
    w3s = nc.dram_tensor("w3s", [2, 128, 8192], F32, kind="ExternalInput")
    masks = nc.dram_tensor("masks", [4, 128, 512], F32, kind="ExternalInput")
    ident = nc.dram_tensor("ident", [128, 128], F32, kind="ExternalInput")
    tab = nc.dram_tensor("tabi", [1, 128], I32, kind="ExternalInput")
    tabm = nc.dram_tensor("tabm", [128, 64], F32, kind="ExternalInput")
    out = nc.dram_tensor("outt", [T, D], F32, kind="ExternalOutput")

    ag_in = nc.dram_tensor("ag_in", [2, ROW], BF16, kind="Internal")
    ag_out = nc.dram_tensor("ag_out", [16, ROW], BF16, kind="Internal")
    q_dram = nc.dram_tensor("q_dram", [2, 128, H * 512], BF16, kind="Internal")
    h_dram = nc.dram_tensor("h_dram", [NT, 128, DFF], F32, kind="Internal")
    w3q_dram = nc.dram_tensor("w3q_dram", [64, 128, D], BF16, kind="Internal")

    sems = {}
    S_ = {}
    gctx = ExitStack()
    with gctx:
        for nm in ["sL", "sS", "sG", "cc", "tm", "vv", "aa", "gg",
                   "Fv", "Fa", "Fg", "Ft", "Fs"]:
            sems[nm] = gctx.enter_context(nc.semaphore(nm))
            S_[nm] = Sem(nm)
        sL, sS, sG, cc, tm, vv, aa, gg = (
            S_[k] for k in ["sL", "sS", "sG", "cc", "tm", "vv", "aa", "gg"])

        def sb(name, shape, dtype=F32, ctx=gctx):
            return ctx.enter_context(nc.sbuf_tensor(name, shape, dtype))

        def psum(name, shape, dtype=F32, ctx=gctx):
            return ctx.enter_context(nc.psum_tensor(name, shape, dtype))

        tab_sb = sb("tab_sb", [1, 128], I32)
        tabm_sb = sb("tabm_sb", [128, 64], F32)
        mask_sb = sb("mask_sb", [128, 4 * 512], BF16)
        id_sb = sb("id_sb", [128, 128], BF16)
        ones_row = sb("ones_row", [1, 128], F32)
        onesc = sb("onesc", [128, 1], F32)
        rsx = sb("rsx", [128, NT], F32)
        inv_sx = sb("inv_sx", [128, NT], F32)
        f1t = sb("f1t", [128, NT], F32)
        f2t = sb("f2t", [128, NT], F32)
        b1t = sb("b1t", [128, NT], F32)
        b2t = sb("b2t", [128, NT], F32)
        f3t = sb("f3t", [128, NT], F32)
        b3t = sb("b3t", [128, NT], F32)
        rsh = sb("rsh", [128, NT], F32)
        sw_b = sb("sw_b", [128, 3], F32)
        mh_b = sb("mh_b", [128, 3], F32)
        stats_f = sb("stats_f", [1, 16], F32)
        stats_bf = sb("stats_bf", [1, 16], BF16)
        tmp1 = sb("tmp1", [128, 8], F32)
        tmp2 = sb("tmp2", [128, 8], F32)

        ctxAC = ExitStack()
        ctxAC.__enter__()
        xq_fm = sb("xq_fm", [128, 16 * 1024], BF16, ctxAC)
        ctxAB = ExitStack()
        with ctxAB:
            xn_fm = sb("xn_fm", [128, 16 * 1024], BF16, ctxAB)

            ctxA = ExitStack()
            with ctxA:
                x_sb = sb("x_sb", [128, NT * D], F32, ctxA)
                xn_f = sb("xn_f", [128, D], F32, ctxA)
                xn_b = sb("xn_b", [128, D], BF16, ctxA)
                xq_b = sb("xq_b", [128, D], BF16, ctxA)
                i32b = sb("i32b", [128, D], I32, ctxA)
                strb = sb("strb", [128, 8192], F32, ctxA)
                bnst = sb("bnst", [128, 6], F32, ctxA)
                bnag = sb("bnag", [128, 2], F32, ctxA)
                strp = sb("strp", [128, 3], F32, ctxA)
                ztail = sb("ztail", [1, 1024], BF16, ctxA)
                pT = psum("pT", [128, 128], BF16, ctxA)
                pT2 = psum("pT2", [128, 128], BF16, ctxA)
                pR = psum("pR", [1, 8], F32, ctxA)

                P = Phase(fence_all=FENCE_ALL["A"])
                v_x = P.dma("sync", sL,
                            bass.AP(x_sb, 0, [[NT * D, 128], [D, NT], [1, D]]),
                            xin.ap().rearrange("(t p) d -> p t d", p=128))
                v_tab = P.dma("sync", sL, tab_sb[:, :], tab.ap())
                v_tabm = P.dma("sync", sL, tabm_sb[:, :], tabm.ap())
                v_mask = P.dma("gpsimd", sG,
                               bass.AP(mask_sb, 0, [[4 * 512, 128], [512, 4], [1, 512]]),
                               masks.ap().transpose([1, 0, 2]))
                v_id = P.dma("gpsimd", sG, id_sb[:, :], ident.ap())
                P.step("gpsimd", lambda e, r: e.memset(ones_row[:, :], 1.0))
                g_ones = P.step("gpsimd", lambda e, r: e.memset(onesc[:, :], 1.0),
                                inc=(gg, 1))

                P.step("vector", lambda e, r: e.memset(strp[:, :], 0.0))
                v_red = 0
                first = True
                for (wsrc, j) in [(w1s, 0), (w2s, 1), (w3s, 2)]:
                    for b in range(2):
                        waits = [] if first else [(vv, v_red)]
                        first = False
                        t_l = P.dma("sync", sL, strb[:, :], wsrc.ap()[b], waits=waits)
                        P.step("vector",
                               lambda e, r: e.reduce_sum(tmp1[:, 0:1], strb[:, :],
                                                         axis=AX.X,
                                                         apply_absolute_value=True),
                               waits=[(sL, t_l)], fence=True)
                        v_red = P.step(
                            "vector",
                            (lambda e, r, j=j: e.tensor_add(strp[:, j:j + 1],
                                                            strp[:, j:j + 1],
                                                            tmp1[:, 0:1])),
                            inc=(vv, 1), fence=True)
                t_red = None
                for j in range(3):
                    t_red = P.step(
                        "tensor",
                        (lambda e, r, j=j: e.matmul(pR[0:1, j:j + 1], strp[:, j:j + 1],
                                                    onesc[:, :], start=(j == 0),
                                                    stop=(j == 2),
                                                    skip_group_check=True)),
                        waits=[(vv, v_red), (gg, g_ones)],
                        inc=(tm, 1) if j == 2 else None)
                a_st = P.step("scalar", lambda e, r: e.copy(stats_f[:, 0:3], pR[0:1, 0:3]),
                              waits=[(tm, t_red)], inc=(aa, 1))

                def enc(e, r, F):
                    e.memset(stats_bf[:, :], 0.0)
                    F(e.tensor_copy(stats_bf[:, 0:3], stats_f[:, 0:3]))
                    F(e.tensor_copy(stats_f[:, 4:7], stats_bf[:, 0:3]))
                    F(e.tensor_sub(stats_f[:, 8:11], stats_f[:, 0:3], stats_f[:, 4:7]))
                    F(e.tensor_copy(stats_bf[:, 3:6], stats_f[:, 8:11]))
                    F(e.tensor_copy(stats_f[:, 4:7], stats_bf[:, 3:6]))
                    F(e.tensor_sub(stats_f[:, 8:11], stats_f[:, 8:11],
                                   stats_f[:, 4:7]))
                    return e.tensor_copy(stats_bf[:, 6:9], stats_f[:, 8:11])
                v_enc = P.step("vector", enc, waits=[(aa, a_st)], inc=(vv, 1))
                v_zt = P.step("vector", lambda e, r: e.memset(ztail[:, :], 0.0),
                              inc=(vv, 1), fence=True)
                z_t = 0
                for half in range(2):
                    z_t = P.dma("sync", sL,
                                bass.AP(ag_in, half * ROW + KV_AREA,
                                        [[ROW, 1], [1, 1024]]),
                                ztail[0:1, :], waits=[(vv, v_zt)])
                P.dma("gpsimd", sG,
                      bass.AP(ag_in, KV_AREA, [[ROW, 1], [1, 16]]),
                      stats_bf[0:1, 0:16], waits=[(vv, v_enc), (sL, z_t)])
                g_stats = sG.v

                def newton(e, F, src_ap, eps):
                    F(e.reciprocal(tmp1[:, 1:2], tmp1[:, 0:1]))
                    e.tensor_mul(tmp1[:, 2:3], tmp1[:, 1:2], tmp1[:, 1:2])
                    F(e.tensor_scalar(tmp1[:, 3:4], src_ap, 1.0, float(eps),
                                      OP.mult, OP.add))
                    F(e.tensor_mul(tmp1[:, 2:3], tmp1[:, 2:3], tmp1[:, 3:4]))
                    F(e.tensor_scalar(tmp1[:, 2:3], tmp1[:, 2:3], -0.5, 1.5,
                                      OP.mult, OP.add))
                    return e.tensor_mul(tmp1[:, 1:2], tmp1[:, 1:2], tmp1[:, 2:3])

                a_xn_copy = 0
                v_xq_copy = 0
                t_xnb_used = 0
                t_xqb_used = 0
                a_xq_prev = 0
                for tt in range(NT):
                    xs_ap = x_sb[:, tt * D:(tt + 1) * D]
                    P.step("vector", (lambda e, r, xs=xs_ap: e.tensor_mul(
                        xn_f[:, :], xs, xs)),
                        waits=[(sL, v_x), (tm, t_xnb_used), (aa, a_xq_prev)])
                    P.step("vector", lambda e, r: e.reduce_sum(
                        tmp2[:, 5:6], xn_f[:, :], axis=AX.X), fence=True)
                    P.step("vector", (lambda e, r, xs=xs_ap: e.reduce_sum(
                        tmp1[:, 4:5], xs, axis=AX.X)), fence=True)
                    def vbn(e, r, F):
                        e.tensor_scalar(tmp1[:, 4:5], tmp1[:, 4:5], 1.0 / D, None,
                                        OP.mult)
                        F(e.tensor_scalar(tmp1[:, 5:6], tmp2[:, 5:6], 1.0 / D, None,
                                          OP.mult))
                        F(e.tensor_mul(tmp1[:, 6:7], tmp1[:, 4:5], tmp1[:, 4:5]))
                        return e.tensor_sub(tmp2[:, 0:1], tmp1[:, 5:6], tmp1[:, 6:7])
                    v_bn = P.step("vector", vbn, inc=(vv, 1))
                    a1 = P.step("scalar", lambda e, r: e.activation(
                        tmp1[:, 0:1], tmp2[:, 0:1], AF.Sqrt, bias=np.float32(LN_EPS)),
                        waits=[(vv, v_bn)], inc=(aa, 1))
                    P.step("vector", (lambda e, r, F: newton(e, F, tmp2[:, 0:1],
                                                             LN_EPS)),
                           waits=[(aa, a1)])
                    P.step("vector", lambda e, r: e.tensor_scalar(
                        tmp1[:, 4:5], tmp1[:, 4:5], -1.0, None, OP.mult), fence=True)
                    P.step("vector", (lambda e, r, xs=xs_ap: e.tensor_scalar(
                        xn_f[:, :], xs, tmp1[:, 4:5], tmp1[:, 1:2], OP.add, OP.mult)))
                    v_xnb = P.step("vector",
                                   lambda e, r: e.tensor_copy(xn_b[:, :], xn_f[:, :]),
                                   inc=(vv, 1))
                    P.step("vector", (lambda e, r, xs=xs_ap: e.tensor_mul(
                        xs, xn_f[:, :], xn_f[:, :])))
                    P.step("vector", (lambda e, r, xs=xs_ap: e.reduce_sum(
                        tmp2[:, 5:6], xs, axis=AX.X)), fence=True)
                    v_ms = P.step("vector", lambda e, r: e.tensor_scalar(
                        tmp2[:, 0:1], tmp2[:, 5:6], 1.0 / D, None, OP.mult),
                        inc=(vv, 1), fence=True)
                    a2 = P.step("scalar", lambda e, r: e.activation(
                        tmp1[:, 0:1], tmp2[:, 0:1], AF.Sqrt, bias=np.float32(RMS_EPS)),
                        waits=[(vv, v_ms)], inc=(aa, 1))
                    P.step("vector", (lambda e, r, F: newton(e, F, tmp2[:, 0:1],
                                                             RMS_EPS)),
                           waits=[(aa, a2)])
                    P.step("vector", lambda e, r: e.reduce_max(
                        tmp1[:, 6:7], xn_f[:, :], axis=AX.X,
                        apply_absolute_value=True), fence=True)

                    def scales(e, r, F, tt=tt):
                        F(e.tensor_mul(tmp1[:, 7:8], tmp1[:, 1:2], tmp1[:, 6:7]))
                        F(e.tensor_scalar(tmp1[:, 7:8], tmp1[:, 7:8], float(Q_EPS),
                                          None, OP.max))
                        e.tensor_scalar(inv_sx[:, tt:tt + 1], tmp1[:, 7:8], 1.0 / 127.0,
                                        None, OP.mult)
                        F(e.reciprocal(tmp2[:, 1:2], tmp1[:, 7:8]))
                        F(e.tensor_scalar(tmp2[:, 1:2], tmp2[:, 1:2], 127.0, None,
                                          OP.mult))
                        return e.tensor_mul(tmp2[:, 2:3], tmp2[:, 1:2], tmp1[:, 1:2])
                    P.step("vector", scales, fence=True)
                    P.step("vector", lambda e, r: e.tensor_scalar(
                        xn_f[:, :], xn_f[:, :], tmp2[:, 2:3], MAGIC, OP.mult,
                        OP.add), fence=True)
                    v_u = P.step("vector", (lambda e, r, tt=tt: e.tensor_scalar(
                        xn_f[:, :], xn_f[:, :], MAGIC, 0.0, OP.subtract, OP.add,
                        accum_out=rsx[:, tt:tt + 1])), inc=(vv, 1))
                    P.step("scalar", lambda e, r: e.copy(i32b[:, :], xn_f[:, :]),
                           waits=[(vv, v_u), (tm, t_xqb_used)], fence=True)
                    a_xq = P.step("scalar", lambda e, r: e.copy(xq_b[:, :], i32b[:, :]),
                                  inc=(aa, 1))
                    a_xq_prev = a_xq
                    for s_ in range(16):
                        t1 = P.step("tensor", (lambda e, r, s_=s_: e.transpose(
                            pT[:, :], xn_b[:, s_ * 128:(s_ + 1) * 128], id_sb[:, :])),
                            waits=[(vv, v_xnb), (sG, v_id), (aa, a_xn_copy)],
                            inc=(tm, 1))
                        if s_ == 15:
                            t_xnb_used = t1
                        a_xn_copy = P.step("scalar", (lambda e, r, s_=s_, tt=tt: e.copy(
                            xn_fm[:, s_ * 1024 + tt * 128: s_ * 1024 + (tt + 1) * 128],
                            pT[:, :])), waits=[(tm, t1)], inc=(aa, 1))
                        t2 = P.step("tensor", (lambda e, r, s_=s_: e.transpose(
                            pT2[:, :], xq_b[:, s_ * 128:(s_ + 1) * 128], id_sb[:, :])),
                            waits=[(aa, a_xq), (vv, v_xq_copy)], inc=(tm, 1))
                        if s_ == 15:
                            t_xqb_used = t2
                        v_xq_copy = P.step("vector", (lambda e, r, s_=s_, tt=tt:
                            e.tensor_copy(
                                xq_fm[:, s_ * 1024 + tt * 128: s_ * 1024 + (tt + 1) * 128],
                                pT2[:, :])), waits=[(tm, t2)], inc=(vv, 1))
                P.run(nc, sems)

            ctxB = ExitStack()
            with ctxB:
                wbuf = [sb(f"wbuf{i}", [128, 3 * 16 * 128], BF16, ctxB) for i in range(2)]
                kst = [sb(f"kst{i}", [128, H * 512], BF16, ctxB) for i in range(2)]
                vst = [sb(f"vst{i}", [128, H * 4 * 132], BF16, ctxB) for i in range(2)]
                qev = sb("qev", [128, 512], BF16, ctxB)
                vtm = sb("vtm", [128, 512], BF16, ctxB)
                pQ = psum("pQ", [128, 512], F32, ctxB)
                pK = psum("pK", [128, 512], F32, ctxB)
                pV = psum("pV", [128, 512], F32, ctxB)
                pVT = psum("pVT", [128, 128], BF16, ctxB)

                P = Phase(fence_all=FENCE_ALL["B"])
                for half in range(2):
                    P.step("vector", (lambda e, r, half=half: e.memset(
                        bass.AP(vst[half], 128, [[H * 4 * 132, 128], [132, 64], [1, 4]]),
                        1.0)))
                w_done = []
                tmB0 = tm.v
                head_mm_done = [tmB0 + 14 * (h + 1) for h in range(H)]
                for h in range(H):
                    par = h % 2
                    waits = [(tm, head_mm_done[h - 2])] if h >= 2 else []
                    tgt = None
                    for p_ in range(3):
                        tgt = P.dma(
                            "gpsimd", sG,
                            bass.AP(wbuf[par], p_ * 2048,
                                    [[3 * 16 * 128, 128], [128, 16], [1, 128]]),
                            wqkv.ap()[p_, h].rearrange("(s p) c -> p s c", p=128),
                            waits=waits if p_ == 0 else [])
                    w_done.append(tgt)

                a_qe = 0
                v_ke = 0
                v_ve = 0
                t_vt_used = 0
                st_q_last = 0
                for h in range(H):
                    par = h % 2
                    for half in range(2):
                        t_q = None
                        for s_ in range(16):
                            waits = [(sG, w_done[h])]
                            if s_ == 0:
                                waits.append((aa, a_qe))
                            t_q = P.step("tensor", (lambda e, r, s_=s_, half=half, par=par:
                                e.matmul(pQ[:, :],
                                         wbuf[par][:, 0 * 2048 + s_ * 128:(s_ + 1) * 128],
                                         xn_fm[:, s_ * 1024 + half * 512:
                                               s_ * 1024 + half * 512 + 512],
                                         start=(s_ == 0), stop=(s_ == 15))),
                                waits=waits, inc=(tm, 1) if s_ == 15 else None)
                        a_qe = P.step("scalar", lambda e, r: e.copy(qev[:, :], pQ[:, :]),
                                      waits=[(tm, t_q)], inc=(aa, 1))
                        st_q_last = P.dma(
                            "scalar", sS,
                            bass.AP(q_dram, half * 128 * H * 512 + h * 512,
                                    [[H * 512, 128], [1, 512]]),
                            qev[:, :], waits=[(aa, a_qe)])
                        t_k = None
                        for s_ in range(16):
                            waits = [(vv, v_ke)] if s_ == 0 else []
                            t_k = P.step("tensor", (lambda e, r, s_=s_, half=half, par=par:
                                e.matmul(pK[:, :],
                                         wbuf[par][:, 1 * 2048 + s_ * 128:
                                                   1 * 2048 + (s_ + 1) * 128],
                                         xn_fm[:, s_ * 1024 + half * 512:
                                               s_ * 1024 + half * 512 + 512],
                                         start=(s_ == 0), stop=(s_ == 15))),
                                waits=waits, inc=(tm, 1) if s_ == 15 else None)
                        v_ke = P.step("vector", (lambda e, r, h=h, half=half:
                            e.tensor_copy(kst[half][:, h * 512:(h + 1) * 512], pK[:, :])),
                            waits=[(tm, t_k)], inc=(vv, 1))
                        t_v = None
                        for s_ in range(16):
                            waits = [(vv, v_ve)] if s_ == 0 else []
                            t_v = P.step("tensor", (lambda e, r, s_=s_, half=half, par=par:
                                e.matmul(pV[:, :],
                                         wbuf[par][:, 2 * 2048 + s_ * 128:
                                                   2 * 2048 + (s_ + 1) * 128],
                                         xn_fm[:, s_ * 1024 + half * 512:
                                               s_ * 1024 + half * 512 + 512],
                                         start=(s_ == 0), stop=(s_ == 15))),
                                waits=waits, inc=(tm, 1) if s_ == 15 else None)
                        v_vt = P.step("vector", lambda e, r: e.tensor_copy(vtm[:, :],
                                                                           pV[:, :]),
                                      waits=[(tm, t_v), (tm, t_vt_used)], inc=(vv, 1))
                        for kts in range(4):
                            t_tr = P.step("tensor", (lambda e, r, kts=kts: e.transpose(
                                pVT[:, :], vtm[:, kts * 128:(kts + 1) * 128], id_sb[:, :])),
                                waits=[(vv, v_vt), (vv, v_ve)], inc=(tm, 1))
                            if kts == 3:
                                t_vt_used = t_tr
                            v_ve = P.step("vector", (lambda e, r, h=h, half=half, kts=kts:
                                e.tensor_copy(
                                    vst[half][:, (h * 4 + kts) * 132:
                                              (h * 4 + kts) * 132 + 128],
                                    pVT[:, :])), waits=[(tm, t_tr)], inc=(vv, 1))
                    assert tm.v == head_mm_done[h], (h, tm.v, head_mm_done[h])

                st_kv = 0
                for half in range(2):
                    for (srcb, base, width) in [(kst[half], 0, 8192),
                                                (vst[half], 8192, 8448)]:
                        st_kv = P.dma(
                            "scalar", sS,
                            bass.AP(ag_in, half * ROW + base,
                                    [[KVW, 128], [1, width]]),
                            srcb[:, :], waits=[(vv, v_ve), (vv, v_ke)])
                P.step("gpsimd", lambda e, r: e.collective_compute(
                    "AllGather", OP.bypass, replica_groups=[list(range(NCORE))],
                    ins=[ag_in.ap().opt()], outs=[ag_out.ap().opt()]),
                    waits=[(sS, st_kv), (sG, g_stats)], inc=(cc, 1))
                P.run(nc, sems)

        ctxC = ExitStack()
        with ctxC:
            w1st = [sb(f"w1st{i}", [128, 16 * 256], F32, ctxC) for i in range(2)]
            w2st = [sb(f"w2st{i}", [128, 16 * 256], F32, ctxC) for i in range(2)]
            wq1 = [sb(f"wq1_{i}", [128, 16 * 256], BF16, ctxC) for i in range(2)]
            wq2 = [sb(f"wq2_{i}", [128, 16 * 256], BF16, ctxC) for i in range(2)]
            w3st = [sb(f"w3st{i}", [128, D], F32, ctxC) for i in range(2)]
            wq3 = [sb(f"wq3_{i}", [128, D], BF16, ctxC) for i in range(2)]
            wgtb = sb("wgtb", [128, 16 * 256], BF16, ctxC)
            silb = sb("silb", [128, 256], F32, ctxC)
            h2b = sb("h2b", [128, 256], F32, ctxC)
            hbuf = [sb(f"hbuf{i}", [128, 256], F32, ctxC) for i in range(2)]
            gath = sb("gath", [1, 256], F32, ctxC)
            gath_b = sb("gath_b", [1, 128], BF16, ctxC)
            pH1 = [psum(f"pH1_{i}", [128, 256], F32, ctxC) for i in range(2)]
            pH2 = [psum(f"pH2_{i}", [128, 256], F32, ctxC) for i in range(2)]
            pB = psum("pBb", [128, 8], F32, ctxC)

            P = Phase(fence_all=FENCE_ALL["C"])
            g_st = P.dma("gpsimd", sG,
                         bass.AP(gath_b, 0, [[128, 1], [16, 8], [1, 16]]),
                         bass.AP(ag_out, KV_AREA, [[1, 1], [2 * ROW, 8], [1, 16]]),
                         waits=[(cc, 1)])
            P.step("vector", lambda e, r: e.tensor_copy(gath[0:1, 0:128],
                                                        gath_b[0:1, :]),
                   waits=[(sG, g_st)], fence=True)

            def dec(e, r, F):
                last = None
                for j in range(3):
                    F(e.tensor_reduce(gath[0:1, 160:168],
                                      bass.AP(gath, j, [[256, 1], [16, 8], [3, 3]]),
                                      AX.X, OP.add))
                    last = e.reduce_sum(stats_f[0:1, 8 + j:9 + j],
                                        gath[0:1, 160:168], axis=AX.X)
                    if j < 2:
                        F(last)
                return last
            P.step("vector", dec)
            NW = [float(DFF * D), float(DFF * D), float(D * DFF)]

            def scl(e, r, F):
                last = None
                for j in range(3):
                    F(e.tensor_scalar(stats_f[0:1, 8 + j:9 + j],
                                      stats_f[0:1, 8 + j:9 + j],
                                      1.0 / NW[j], None, OP.mult))
                    F(e.tensor_scalar(stats_f[0:1, 8 + j:9 + j],
                                      stats_f[0:1, 8 + j:9 + j],
                                      float(Q_EPS), None, OP.max))
                    e.tensor_scalar(stats_f[0:1, 12 + j:13 + j],
                                    stats_f[0:1, 8 + j:9 + j], 0.5, None, OP.mult)
                    last = e.reciprocal(stats_f[0:1, j:j + 1],
                                        stats_f[0:1, 8 + j:9 + j])
                return last
            v_scl = P.step("vector", scl, inc=(vv, 1))
            t_bc = None
            for j in range(6):
                so = j if j < 3 else (12 + j - 3)
                t_bc = P.step("tensor", (lambda e, r, j=j, so=so: e.matmul(
                    pB[:, j:j + 1], ones_row[0:1, :], stats_f[0:1, so:so + 1],
                    start=(j == 0), stop=(j == 5), skip_group_check=True)),
                    waits=[(vv, v_scl)], inc=(tm, 1) if j == 5 else None)
            a_bc = P.step("scalar", lambda e, r: (
                e.copy(sw_b[:, 0:3], pB[:, 0:3]),
                e.copy(mh_b[:, 0:3], pB[:, 3:6]))[-1], waits=[(tm, t_bc)], inc=(aa, 1))

            def ffd(e, r, F):
                e.tensor_scalar(f1t[:, :], inv_sx[:, :], mh_b[:, 0:1], None, OP.mult)
                F(e.tensor_scalar(f2t[:, :], inv_sx[:, :], mh_b[:, 1:2], None,
                                  OP.mult))
                e.tensor_mul(b1t[:, :], rsx[:, :], f1t[:, :])
                F(e.tensor_mul(b2t[:, :], rsx[:, :], f2t[:, :]))
                e.tensor_scalar(b1t[:, :], b1t[:, :], -1.0, None, OP.mult)
                return e.tensor_scalar(b2t[:, :], b2t[:, :], -1.0, None, OP.mult)
            v_ffd = P.step("vector", ffd, waits=[(aa, a_bc)], inc=(vv, 1))

            w3q_done = [0] * 64
            w3_tl = [0] * 64
            w3q_store = 0

            def emit_w3_load(P, j3):
                par = j3 % 2
                waits = [(vv, w3q_done[j3 - 2])] if j3 >= 2 else []
                w3_tl[j3] = P.dma("sync", sL, w3st[par][:, :], w3c.ap()[j3],
                                  waits=waits)

            def emit_w3_quant(P, j3):
                nonlocal w3q_store
                par = j3 % 2
                a_s3 = P.step("scalar", (lambda e, r, par=par: e.activation(
                    wq3[par][:, :], w3st[par][:, :], AF.Sign, bias=np.float32(0.5),
                    scale=sw_b[:, 2:3])), waits=[(sL, w3_tl[j3]), (aa, a_bc)],
                    inc=(aa, 1))
                P.step("vector", (lambda e, r, par=par: e.tensor_scalar(
                    wgtb[:, 0:D], w3st[par][:, :], sw_b[:, 2:3], 0.5, OP.mult,
                    OP.is_gt)), waits=[(aa, a_s3)], fence=True)
                w3q_done[j3] = P.step("vector", (lambda e, r, par=par:
                    e.scalar_tensor_tensor(wq3[par][:, :], wgtb[:, 0:D], 2.0,
                                           wq3[par][:, :], OP.mult, OP.add)),
                    inc=(vv, 1))
                w3q_store = P.dma("scalar", sS, w3q_dram.ap()[j3], wq3[par][:, :],
                                  waits=[(vv, w3q_done[j3])])
                if j3 + 2 < 64:
                    emit_w3_load(P, j3 + 2)

            w1done = [0] * 32
            w2done = [0] * 32
            wq_consume = [0] * 32
            h_mm_done = [0] * 32
            evac_done = [0] * (32 * NT)
            h_dma_t = [0] * (32 * NT)
            h_store = 0

            def emit_wload(P, j):
                par = j % 2
                waits = [(vv, wq_consume[j - 2])] if j >= 2 else []
                w1done[j] = P.dma(
                    "sync", sL,
                    bass.AP(w1st[par], 0, [[16 * 256, 128], [256, 16], [1, 256]]),
                    w1c.ap()[j].rearrange("(s p) c -> p s c", p=128), waits=waits)
                w2done[j] = P.dma(
                    "scalar", sS,
                    bass.AP(w2st[par], 0, [[16 * 256, 128], [256, 16], [1, 256]]),
                    w2c.ap()[j].rearrange("(s p) c -> p s c", p=128), waits=waits)

            emit_wload(P, 0)
            emit_wload(P, 1)
            emit_w3_load(P, 0)
            emit_w3_load(P, 1)
            for j in range(32):
                par = j % 2
                waits_q = [(sL, w1done[j]), (aa, a_bc)]
                if j >= 2:
                    waits_q.append((tm, h_mm_done[j - 2]))
                a_s1 = P.step("scalar", (lambda e, r, par=par: e.activation(
                    wq1[par][:, :], w1st[par][:, :], AF.Sign, bias=np.float32(0.5),
                    scale=sw_b[:, 0:1])), waits=waits_q, inc=(aa, 1))
                P.step("vector", (lambda e, r, par=par: e.tensor_scalar(
                    wgtb[:, :], w1st[par][:, :], sw_b[:, 0:1], 0.5, OP.mult,
                    OP.is_gt)), waits=[(aa, a_s1)], fence=True)
                P.step("vector", (lambda e, r, par=par: e.scalar_tensor_tensor(
                    wq1[par][:, :], wgtb[:, :], 2.0, wq1[par][:, :], OP.mult,
                    OP.add)))
                waits_q2 = [(sS, w2done[j]), (aa, a_bc)]
                if j >= 2:
                    waits_q2.append((tm, h_mm_done[j - 2]))
                a_s2 = P.step("scalar", (lambda e, r, par=par: e.activation(
                    wq2[par][:, :], w2st[par][:, :], AF.Sign, bias=np.float32(0.5),
                    scale=sw_b[:, 1:2])), waits=waits_q2, inc=(aa, 1))
                emit_w3_quant(P, 2 * j)
                P.step("vector", (lambda e, r, par=par: e.tensor_scalar(
                    wgtb[:, :], w2st[par][:, :], sw_b[:, 1:2], 0.5, OP.mult,
                    OP.is_gt)), waits=[(aa, a_s2)], fence=True)
                v_qw = P.step("vector", (lambda e, r, par=par: e.scalar_tensor_tensor(
                    wq2[par][:, :], wgtb[:, :], 2.0, wq2[par][:, :], OP.mult,
                    OP.add)), inc=(vv, 1))
                wq_consume[j] = v_qw
                emit_w3_quant(P, 2 * j + 1)
                if j + 2 < 32:
                    emit_wload(P, j + 2)

                t_h = None
                for tt in range(NT):
                    ppar = tt % 2
                    idx = j * NT + tt
                    for s_ in range(16):
                        waits = [(vv, v_qw)]
                        if s_ == 0 and idx >= 2:
                            waits.append((vv, evac_done[idx - 2]))
                        P.step("tensor", (lambda e, r, s_=s_, tt=tt, par=par, ppar=ppar:
                            e.matmul(pH1[ppar][:, :],
                                     xq_fm[:, s_ * 1024 + tt * 128:
                                           s_ * 1024 + (tt + 1) * 128],
                                     wq1[par][:, s_ * 256:(s_ + 1) * 256],
                                     start=(s_ == 0), stop=(s_ == 15))),
                            waits=waits)
                    for s_ in range(16):
                        t_h = P.step("tensor", (lambda e, r, s_=s_, tt=tt, par=par,
                                                ppar=ppar:
                            e.matmul(pH2[ppar][:, :],
                                     xq_fm[:, s_ * 1024 + tt * 128:
                                           s_ * 1024 + (tt + 1) * 128],
                                     wq2[par][:, s_ * 256:(s_ + 1) * 256],
                                     start=(s_ == 0), stop=(s_ == 15))),
                            inc=(tm, 1) if s_ == 15 else None)
                    sil_waits = [(tm, t_h), (vv, v_ffd)]
                    if idx >= 1:
                        sil_waits.append((vv, evac_done[idx - 1]))
                    a_sil = P.step("scalar", (lambda e, r, tt=tt, ppar=ppar:
                        e.activation(silb[:, :], pH1[ppar][:, :], AF.Silu,
                                     bias=b1t[:, tt:tt + 1], scale=f1t[:, tt:tt + 1])),
                        waits=sil_waits, inc=(aa, 1))
                    P.step("vector", (lambda e, r, tt=tt, ppar=ppar: e.tensor_scalar(
                        h2b[:, :], pH2[ppar][:, :], f2t[:, tt:tt + 1], b2t[:, tt:tt + 1],
                        OP.mult, OP.add)), waits=[(tm, t_h), (vv, v_ffd)],
                        fence=True)
                    hb = hbuf[idx % 2]
                    ev_waits = [(aa, a_sil)]
                    if idx >= 2:
                        ev_waits.append((sL, h_dma_t[idx - 2]))
                    evac_done[idx] = P.step("vector", (lambda e, r, hb=hb: e.tensor_mul(
                        hb[:, :], silb[:, :], h2b[:, :])), waits=ev_waits,
                        inc=(vv, 1))
                    h_mm_done[j] = t_h
                    h_dma_t[idx] = P.dma(
                        "sync", sL,
                        bass.AP(h_dram, tt * 128 * DFF + j * 256, [[DFF, 128], [1, 256]]),
                        hb[:, :], waits=[(vv, evac_done[idx])])
                    h_store = h_dma_t[idx]
            P.run(nc, sems)

        ctxAC.close()

        s_sb = sb("s_sb", [128, NT * D], BF16)
        ctxDE = ExitStack()
        ctxDE.__enter__()
        q_stage = sb("q_stage", [128, H * 512], BF16, ctxDE)
        ctxD = ExitStack()
        with ctxD:
            O_all = sb("O_all", [128, 2 * 16 * 528], F32, ctxD)
            kv_sb = [sb(f"kv_sb{i}", [128, KVW], BF16, ctxD) for i in range(2)]
            P_sb = [sb(f"P_sb{i}", [128, 512], BF16, ctxD) for i in range(3)]
            ctxDP = ExitStack()
            ctxDP.__enter__()
            pS = [psum(f"pS{i}", [128, 512], F32, ctxDP) for i in range(3)]
            pO = psum("pO", [128, 2048], F32, ctxDP)

            P = Phase(fence_all=FENCE_ALL["D"])
            P.want_regs("gpsimd", ["kr"])
            P.want_regs("sync", ["qr"])

            v_init = P.step("vector", lambda e, r: e.memset(O_all[:, :], 0.0),
                            inc=(vv, 1))

            tmD0 = tm.v
            pair_mm = [tmD0 + 320 * (i + 1) for i in range(NPAIR)]
            kv_done = [0] * NPAIR
            last_exp = [0, 0, 0]
            last_av = [0, 0, 0]
            last_oacc = v_init

            pair_last_st = [tmD0 + 320 * i + 15 * 20 + 16 for i in range(NPAIR)]
            q_done = [0] * NPAIR
            for i in range(NPAIR):
                par = i % 2
                waits = [(sL, v_tab), (cc, 1)]
                if i >= 2:
                    waits.append((tm, pair_mm[i - 2]))
                P.step("gpsimd", (lambda e, r, i=i: e.reg_load(
                    r["kr"], bass.AP(tab_sb, TAB_K + i, [[128, 1], [1, 1]]))),
                    waits=waits)
                kv_done[i] = P.step("gpsimd", (lambda e, r, par=par: e.dma_start(
                    out=kv_sb[par][:, :],
                    in_=bass.AP(ag_out, r["kr"], [[KVW, 128], [1, KVW]]))),
                    inc=(sG, 16))
            for i in range(NPAIR):
                q_waits = [(sL, v_tab), (sS, st_q_last), (cc, 1)]
                if i >= 1:
                    q_waits.append((tm, pair_last_st[i - 1]))
                if i < 2:
                    q_done[i] = P.dma("sync", sL, q_stage[:, :], q_dram.ap()[i],
                                      waits=q_waits)
                else:
                    P.step("sync", (lambda e, r, i=i: e.reg_load(
                        r["qr"], bass.AP(tab_sb, TAB_Q + i, [[128, 1], [1, 1]]))),
                        waits=q_waits)
                    q_done[i] = P.step("sync", (lambda e, r: e.dma_start(
                        out=q_stage[:, :],
                        in_=bass.AP(q_dram, r["qr"], [[H * 512, 128], [1, H * 512]]))),
                        inc=(sL, 16))

            for i in range(NPAIR):
                par = i % 2
                for h in range(H):
                    for kts in range(4):
                        b = (h * 4 + kts) % 3
                        waits = [(sG, kv_done[i]), (sL, q_done[i]),
                                 (aa, last_exp[b]), (vv, v_init)]
                        t_st = P.step("tensor", (lambda e, r, par=par, h=h, kts=kts, b=b:
                            e.matmul(pS[b][:, :],
                                     kv_sb[par][:, h * 512 + kts * 128:
                                                h * 512 + (kts + 1) * 128],
                                     q_stage[:, h * 512:(h + 1) * 512],
                                     start=True, stop=True)),
                            waits=waits, inc=(tm, 1))
                        if i < 2:
                            v_m = P.step("vector", (lambda e, r, b=b, kts=kts:
                                e.tensor_tensor(pS[b][:, :], pS[b][:, :],
                                                mask_sb[:, kts * 512:(kts + 1) * 512],
                                                OP.add)),
                                waits=[(tm, t_st), (sG, v_mask)], inc=(vv, 1))
                            exp_waits = [(vv, v_m)]
                        else:
                            exp_waits = [(tm, t_st)]
                        exp_waits.append((tm, last_av[b]))
                        a_e = P.step("scalar", (lambda e, r, b=b: e.activation(
                            P_sb[b][:, :], pS[b][:, :], AF.Exp, bias=EXPC,
                            scale=SC_QK)), waits=exp_waits, inc=(aa, 1))
                        last_exp[b] = a_e
                        av_waits = [(aa, a_e)]
                        if kts == 0:
                            av_waits.append((vv, last_oacc))
                        t_av = None
                        for qs in range(4):
                            t_av = P.step("tensor", (lambda e, r, b=b, qs=qs, par=par,
                                                     h=h, kts=kts:
                                e.matmul(pO[:, qs * 512:qs * 512 + 129],
                                         P_sb[b][:, qs * 128:(qs + 1) * 128],
                                         kv_sb[par][:, 8192 + (h * 4 + kts) * 132:
                                                    8192 + (h * 4 + kts) * 132 + 129],
                                         start=(kts == 0), stop=(kts == 3),
                                         skip_group_check=True)),
                                waits=av_waits if qs == 0 else [], inc=(tm, 1))
                        last_av[b] = t_av
                    pO_ap = bass.AP(pO, 0, [[2048, 128], [512, 4], [1, 129]])
                    for half in range(2):
                        oap = bass.AP(O_all, half * 8448 + h * 528,
                                      [[2 * 16 * 528, 128], [132, 4], [1, 129]])
                        st_ = P.step("vector", (lambda e, r, oap=oap, i=i, half=half:
                            e.scalar_tensor_tensor(
                                oap, pO_ap, tabm_sb[:, 2 * i + half:2 * i + half + 1],
                                oap, OP.mult, OP.add)),
                            waits=[(tm, last_av[0]), (tm, last_av[1]),
                                   (tm, last_av[2]), (sL, v_tabm)],
                            inc=(vv, 1) if half == 1 else None)
                    last_oacc = st_
                assert tm.v == pair_mm[i], (tm.v, pair_mm[i])
            P.run(nc, sems)
            ctxDP.close()

            o_fm = kv_sb[0]
            pE1ctx = ExitStack()
            with pE1ctx:
                o_nrm = [sb(f"o_nrm{i}", [128, 128], BF16, pE1ctx) for i in range(2)]
                pOT = [psum(f"pOT{i}", [128, 128], BF16, pE1ctx) for i in range(2)]
                P = Phase(fence_all=FENCE_ALL["E1"])
                P.step("vector", lambda e, r: e.reciprocal(
                    bass.AP(O_all, 128, [[2 * 16 * 528, 128], [132, 128], [1, 1]]),
                    bass.AP(O_all, 128, [[2 * 16 * 528, 128], [132, 128], [1, 1]])),
                    fence=True)
                last_tr = [0, 0]
                last_cp = [0, 0]
                for h in range(H):
                    for qt in range(8):
                        b = (h * 8 + qt) % 2
                        off = (qt // 4) * 8448 + h * 528 + (qt % 4) * 132
                        v_n = P.step("vector", (lambda e, r, off=off, b=b:
                            e.tensor_scalar(o_nrm[b][:, :],
                                            O_all[:, off:off + 128],
                                            O_all[:, off + 128:off + 129],
                                            None, OP.mult)),
                            waits=[(tm, last_tr[b])], inc=(vv, 1), fence=E1VN)
                        t_t = P.step("tensor", (lambda e, r, b=b: e.transpose(
                            pOT[b][:, :], o_nrm[b][:, :], id_sb[:, :])),
                            waits=[(vv, v_n), (aa, last_cp[b])], inc=(tm, 1))
                        last_tr[b] = t_t
                        off2 = h * 1024 + qt * 128
                        last_cp[b] = P.step("scalar", (lambda e, r, off2=off2, b=b:
                            e.copy(o_fm[:, off2:off2 + 128], pOT[b][:, :])),
                            waits=[(tm, t_t)], inc=(aa, 1), fence=E1CP)
                P.run(nc, sems)

            ctxE = ExitStack()
            with ctxE:
                woc = [sb(f"woc{i}", [128, 512], BF16, ctxE) for i in range(2)]
                pSP = [psum(f"pSP{i}", [128, 512], F32, ctxE) for i in range(8)]
                P = Phase(fence_all=FENCE_ALL["E"])
                v_se = [0] * 8
                wo_last = [0, 0]
                for dc in range(4):
                    t_s = [None] * 8
                    for s_ in range(16):
                        wpar = s_ % 2
                        wo_waits = [(tm, wo_last[wpar])] if (dc > 0 or s_ >= 2) else []
                        wo_done = P.dma(
                            "gpsimd", sG, woc[wpar][:, :],
                            bass.AP(woutT, s_ * 128 * D + dc * 512,
                                    [[D, 128], [1, 512]]),
                            waits=wo_waits)
                        for qt in range(8):
                            waits = [(sG, wo_done)]
                            if s_ == 0:
                                waits.append((vv, v_se[qt]))
                            t_s[qt] = P.step("tensor", (lambda e, r, s_=s_, qt=qt,
                                                        dc=dc, wpar=wpar:
                                e.matmul(pSP[qt][:, :],
                                         o_fm[:, s_ * 1024 + qt * 128:
                                              s_ * 1024 + (qt + 1) * 128],
                                         woc[wpar][:, :],
                                         start=(s_ == 0), stop=(s_ == 15))),
                                waits=waits,
                                inc=(tm, 1) if qt == 7 else None)
                        wo_last[wpar] = t_s[7]
                    for qt in range(8):
                        v_se[qt] = P.step("vector", (lambda e, r, qt=qt, dc=dc:
                            e.tensor_copy(s_sb[:, qt * D + dc * 512:
                                               qt * D + (dc + 1) * 512],
                                          pSP[qt][:, :])),
                            waits=[(tm, t_s[7])], inc=(vv, 1))
                P.run(nc, sems)

        ctxDE.close()

        ctxF = ExitStack()
        with ctxF:
            hst = [sb(f"hst{i}", [128, DFF], F32, ctxF) for i in range(2)]
            hq_fm = sb("hq_fm", [128, 64 * 512], BF16, ctxF)
            hq_tm = sb("hq_tm", [128, 2048], BF16, ctxF)
            i32f = sb("i32f", [128, 2048], I32, ctxF)
            w3t = [sb(f"w3t{i}", [128, 4 * 512], BF16, ctxF) for i in range(2)]
            g_b = sb("g_b", [128, 512], BF16, ctxF)
            xre = [sb(f"xre{i}", [128, 512], F32, ctxF) for i in range(2)]
            ob = [sb(f"ob{i}", [128, 512], F32, ctxF) for i in range(2)]
            hsq = sb("hsq", [128, 2048], F32, ctxF)
            acc4 = sb("acc4", [128, 4], F32, ctxF)
            pG = [psum(f"pG{i}", [128, 512], F32, ctxF) for i in range(4)]
            pFT = [psum(f"pFT{i}", [128, 128], BF16, ctxF) for i in range(2)]

            P = Phase(fence_all=FENCE_ALL["F"])

            def newtonF(e, src_ap, eps):
                e.reciprocal(tmp1[:, 1:2], tmp1[:, 0:1])
                e.tensor_mul(tmp1[:, 2:3], tmp1[:, 1:2], tmp1[:, 1:2])
                e.tensor_scalar(tmp1[:, 3:4], src_ap, 1.0, float(eps), OP.mult, OP.add)
                e.tensor_mul(tmp1[:, 2:3], tmp1[:, 2:3], tmp1[:, 3:4])
                e.tensor_scalar(tmp1[:, 2:3], tmp1[:, 2:3], -0.5, 1.5, OP.mult, OP.add)
                return e.tensor_mul(tmp1[:, 1:2], tmp1[:, 1:2], tmp1[:, 2:3])

            h_loaded = [0] * NT
            h_loaded[0] = P.dma("sync", sL, hst[0][:, :], h_dram.ap()[0],
                                waits=[(sL, h_store)])
            h_loaded[1] = P.dma("sync", sL, hst[1][:, :], h_dram.ap()[1])

            t_trF_used = [0, 0]
            a_cpF = [0, 0]
            storm_done = 0
            st_out = 0
            x_loaded = {}
            g_done = [0] * 4
            w3t_last = [0, 0]
            last_vo = 0
            last_vo_bo = [0, 0]
            st_out_bo = [0, 0]
            a_hq = 0
            for tg in range(2):
                for tl in range(4):
                    tt = tg * 4 + tl
                    par = tt % 2
                    for q4 in range(4):
                        sl = slice(q4 * 2048, (q4 + 1) * 2048)
                        P.step("vector", (lambda e, r, par=par, sl=sl:
                            e.tensor_mul(hsq[:, :], hst[par][:, sl], hst[par][:, sl])),
                            waits=[(sL, h_loaded[tt])])
                        P.step("vector", (lambda e, r, q4=q4: e.reduce_sum(
                            acc4[:, q4:q4 + 1], hsq[:, :], axis=AX.X)), fence=True)
                    def vbnF(e, r, F):
                        F(e.reduce_sum(tmp2[:, 5:6], acc4[:, :], axis=AX.X))
                        return e.tensor_scalar(tmp2[:, 0:1], tmp2[:, 5:6], 1.0 / DFF,
                                               None, OP.mult)
                    v_bn = P.step("vector", vbnF, inc=(vv, 1))
                    a1 = P.step("scalar", lambda e, r: e.activation(
                        tmp1[:, 0:1], tmp2[:, 0:1], AF.Sqrt, bias=np.float32(RMS_EPS)),
                        waits=[(vv, v_bn)], inc=(aa, 1))
                    P.step("vector", (lambda e, r: newtonF(e, tmp2[:, 0:1], RMS_EPS)),
                           waits=[(aa, a1)])
                    P.step("vector", (lambda e, r, par=par: e.reduce_max(
                        tmp1[:, 6:7], hst[par][:, :], axis=AX.X,
                        apply_absolute_value=True)), fence=True)

                    def scalesF(e, r, F, tt=tt):
                        F(e.tensor_mul(tmp1[:, 7:8], tmp1[:, 1:2], tmp1[:, 6:7]))
                        F(e.tensor_scalar(tmp1[:, 7:8], tmp1[:, 7:8], float(Q_EPS),
                                          None, OP.max))
                        F(e.tensor_scalar(tmp2[:, 3:4], tmp1[:, 7:8], 1.0 / 127.0,
                                          None, OP.mult))
                        e.tensor_scalar(f3t[:, tt:tt + 1], tmp2[:, 3:4], mh_b[:, 2:3],
                                        None, OP.mult)
                        F(e.reciprocal(tmp2[:, 1:2], tmp1[:, 7:8]))
                        F(e.tensor_scalar(tmp2[:, 1:2], tmp2[:, 1:2], 127.0, None,
                                          OP.mult))
                        return e.tensor_mul(tmp2[:, 2:3], tmp2[:, 1:2], tmp1[:, 1:2])
                    P.step("vector", scalesF, fence=True)
                    for q4 in range(4):
                        sl = slice(q4 * 2048, (q4 + 1) * 2048)
                        P.step("vector", (lambda e, r, par=par, sl=sl: e.tensor_scalar(
                            hst[par][:, sl], hst[par][:, sl], tmp2[:, 2:3], MAGIC,
                            OP.mult, OP.add)), fence=True)
                        v_u = P.step("vector", (lambda e, r, par=par, sl=sl, q4=q4:
                            e.tensor_scalar(hst[par][:, sl], hst[par][:, sl], MAGIC,
                                            0.0, OP.subtract, OP.add,
                                            accum_out=acc4[:, q4:q4 + 1])),
                            inc=(vv, 1))
                        P.step("scalar", (lambda e, r, par=par, sl=sl: e.copy(
                            i32f[:, :], hst[par][:, sl])),
                            waits=[(vv, v_u), (tm, t_trF_used[0]),
                                   (tm, t_trF_used[1])], fence=True)
                        a_hq = P.step("scalar", lambda e, r: e.copy(hq_tm[:, :],
                                                                    i32f[:, :]),
                                      inc=(aa, 1))
                        for s2 in range(16):
                            ds = q4 * 16 + s2
                            btr = ds % 2
                            t_t = P.step("tensor", (lambda e, r, s2=s2, btr=btr:
                                e.transpose(pFT[btr][:, :],
                                            hq_tm[:, s2 * 128:(s2 + 1) * 128],
                                            id_sb[:, :])),
                                waits=[(aa, a_hq), (vv, a_cpF[btr]),
                                       (tm, storm_done)], inc=(tm, 1))
                            t_trF_used[btr] = t_t
                            a_cpF[btr] = P.step("vector", (lambda e, r, ds=ds, tl=tl,
                                                           btr=btr:
                                e.tensor_copy(hq_fm[:, ds * 512 + tl * 128:
                                                    ds * 512 + (tl + 1) * 128],
                                              pFT[btr][:, :])),
                                waits=[(tm, t_t)], inc=(vv, 1))
                    def vrsF(e, r, F, tt=tt):
                        F(e.reduce_sum(rsh[:, tt:tt + 1], acc4[:, :], axis=AX.X))
                        F(e.tensor_mul(tmp2[:, 4:5], rsh[:, tt:tt + 1],
                                       f3t[:, tt:tt + 1]))
                        return e.tensor_scalar(b3t[:, tt:tt + 1], tmp2[:, 4:5], -1.0,
                                               None, OP.mult)
                    v_rs = P.step("vector", vrsF, inc=(vv, 1))
                    nxt = tt + 2
                    if nxt < NT:
                        h_loaded[nxt] = P.dma("sync", sL, hst[par][:, :],
                                              h_dram.ap()[nxt],
                                              waits=[(vv, v_rs), (aa, a_hq)])
                hq_ready = vv.v

                for dq in range(4):
                    w3_done = [0] * 16
                    for s4 in range(2):
                        w3_done[s4] = P.dma(
                            "scalar", sS,
                            bass.AP(w3t[s4 % 2], 0, [[4 * 512, 128], [512, 4],
                                                     [1, 512]]),
                            bass.AP(w3q_dram, s4 * 4 * 128 * D + dq * 512,
                                    [[D, 128], [128 * D, 4], [1, 512]]),
                            waits=[(sS, w3q_store), (tm, w3t_last[s4 % 2])])
                    ds_mm = [0] * 64
                    for s4 in range(16):
                        wpar = s4 % 2
                        for dsl in range(4):
                            ds = s4 * 4 + dsl
                            for tl in range(4):
                                waits = [(sS, w3_done[s4]), (vv, hq_ready)]
                                if ds == 0:
                                    waits.append((aa, g_done[tl]))
                                ds_mm[ds] = P.step("tensor", (lambda e, r, ds=ds,
                                                              tl=tl, wpar=wpar,
                                                              dsl=dsl:
                                    e.matmul(pG[tl][:, :],
                                             hq_fm[:, ds * 512 + tl * 128:
                                                   ds * 512 + (tl + 1) * 128],
                                             w3t[wpar][:, dsl * 512:
                                                       (dsl + 1) * 512],
                                             start=(ds == 0), stop=(ds == 63))),
                                    waits=waits,
                                    inc=(tm, 1) if tl == 3 else None)
                        w3t_last[wpar] = ds_mm[s4 * 4 + 3]
                        if s4 + 2 < 16:
                            w3_done[s4 + 2] = P.dma(
                                "scalar", sS,
                                bass.AP(w3t[wpar], 0, [[4 * 512, 128], [512, 4],
                                                       [1, 512]]),
                                bass.AP(w3q_dram, (s4 + 2) * 4 * 128 * D + dq * 512,
                                        [[D, 128], [128 * D, 4], [1, 512]]),
                                waits=[(tm, ds_mm[s4 * 4 + 3])])
                    storm_done = ds_mm[63]
                    for tl in range(4):
                        tt = tg * 4 + tl
                        bo = tl % 2
                        xk = (tg, dq, tl)
                        x_loaded[xk] = P.dma(
                            "sync", sL, xre[bo][:, :],
                            bass.AP(xin, (tt * 128) * D + dq * 512,
                                    [[D, 128], [1, 512]]),
                            waits=[(vv, last_vo_bo[bo])])
                        a_g = P.step("scalar", (lambda e, r, tl=tl, tt=tt:
                            e.activation(g_b[:, :], pG[tl][:, :], AF.Silu,
                                         bias=b3t[:, tt:tt + 1],
                                         scale=f3t[:, tt:tt + 1])),
                            waits=[(tm, storm_done), (vv, last_vo)], inc=(aa, 1))
                        g_done[tl] = a_g
                        def vcomb(e, r, F, tt=tt, dq=dq, bo=bo):
                            F(e.tensor_mul(g_b[:, :],
                                           s_sb[:, tt * D + dq * 512:
                                                tt * D + (dq + 1) * 512],
                                           g_b[:, :]))
                            return e.tensor_add(ob[bo][:, :], xre[bo][:, :],
                                                g_b[:, :])
                        v_o = P.step("vector", vcomb,
                            waits=[(aa, a_g), (sL, x_loaded[xk]),
                                   (sL, st_out_bo[bo])], inc=(vv, 1))
                        last_vo = v_o
                        last_vo_bo[bo] = v_o
                        st_out_bo[bo] = P.dma(
                            "sync", sL,
                            bass.AP(out, (tt * 128) * D + dq * 512,
                                    [[D, 128], [1, 512]]),
                            ob[bo][:, :], waits=[(vv, v_o)])
                        st_out = st_out_bo[bo]
            P.step("sync", None, waits=[(sL, st_out)])
            P.run(nc, sems)

    return nc



def _host_prep(L0_attn_x, L0_attn_w_in, L0_attn_w_out, L0_ffn_w1, L0_ffn_w2,
               L0_ffn_w3):
    c = np.ascontiguousarray
    x = np.asarray(L0_attn_x, np.float32).reshape(S, D)
    w_in = np.asarray(L0_attn_w_in, np.float32)
    w_out = np.asarray(L0_attn_w_out, np.float32)
    w1 = np.asarray(L0_ffn_w1, np.float32)
    w2 = np.asarray(L0_ffn_w2, np.float32)
    w3 = np.asarray(L0_ffn_w3, np.float32)

    wqkv = c(w_in.reshape(H, 3, DH, D).transpose(1, 0, 3, 2))
    woutT = c(w_out.T.reshape(16, 128, D))
    w1c = c(w1.T.reshape(D, 32, 256).transpose(1, 0, 2))
    w2c = c(w2.T.reshape(D, 32, 256).transpose(1, 0, 2))
    w3c = c(w3.T.reshape(64, 128, D))

    kk = np.arange(512).reshape(4, 128)
    qq = np.arange(512)
    masks = np.where(kk[:, :, None] <= qq[None, None, :], 0.0, MASKVAL
                     ).astype(np.float32)
    ident = np.eye(128, dtype=np.float32)

    shared = dict(wqkv=wqkv, woutT=woutT, w1c=w1c, w2c=w2c, w3c=w3c,
                  masks=masks, ident=ident)

    def rowof(g):
        return 2 * g if g <= 7 else 2 * (15 - g) + 1

    in_maps = []
    for core in range(NCORE):
        chA, chB = core, 15 - core
        xc = c(np.concatenate([x[CH * chA:CH * (chA + 1)],
                               x[CH * chB:CH * (chB + 1)]], axis=0))
        w1s_ = c(w1[1024 * core:1024 * (core + 1)].reshape(2, 128, 8192))
        w2s_ = c(w2[1024 * core:1024 * (core + 1)].reshape(2, 128, 8192))
        w3s_ = c(w3[256 * core:256 * (core + 1)].reshape(2, 128, 8192))
        pairs = [(0, chA), (1, chB)]
        pairs += [(0, j) for j in range(chA)]
        pairs += [(1, j) for j in range(chB)]
        assert len(pairs) == NPAIR
        tabv = np.zeros(128, np.int32)
        tabmv = np.zeros((128, 64), np.float32)
        for i, (qsel, kch) in enumerate(pairs):
            tabv[TAB_K + i] = rowof(kch) * ROW
            tabv[TAB_Q + i] = qsel * (128 * H * 512)
            tabmv[:, 2 * i + qsel] = 1.0
        in_maps.append(dict(shared, xin=xc, w1s=w1s_, w2s=w2s_, w3s=w3s_,
                            tabi=tabv.reshape(1, 128), tabm=tabmv))
    return in_maps


def kernel(L0_attn_x, L0_attn_w_in, L0_attn_w_out, L0_ffn_w1, L0_ffn_w2,
           L0_ffn_w3):
    in_maps = _host_prep(L0_attn_x, L0_attn_w_in, L0_attn_w_out, L0_ffn_w1,
                         L0_ffn_w2, L0_ffn_w3)
    nc = build_program()
    res = run_bass_kernel_spmd(nc, in_maps, core_ids=list(range(NCORE)))
    full = np.zeros((1, S, D), np.float32)
    for core in range(NCORE):
        o = np.asarray(res.results[core]["outt"], np.float32)
        chA, chB = core, 15 - core
        full[0, CH * chA:CH * (chA + 1)] = o[0:CH]
        full[0, CH * chB:CH * (chB + 1)] = o[CH:2 * CH]
    return full


if __name__ == "__main__":
    nc = build_program()
    print("build OK; instructions:", nc.n_instructions())


# version 16
# speedup vs baseline: 1.0073x; 1.0073x over previous
from contextlib import ExitStack

import numpy as np
import concourse.bass as bass
import concourse.mybir as mybir
from concourse.bass_utils import run_bass_kernel_spmd

F32 = mybir.dt.float32
BF16 = mybir.dt.bfloat16
I32 = mybir.dt.int32
AF = mybir.ActivationFunctionType
OP = mybir.AluOpType
AX = mybir.AxisListType

NCORE = 8
S = 8192
D = 2048
H = 16
DH = 128
DFF = 8192
T = 1024
NT = 8
CH = 512

KVW = 8192 + 8448
KV_AREA = 128 * KVW
STATS = 1024
ROW = KV_AREA + STATS

NPAIR = 17
TAB_K, TAB_V, TAB_Q, TAB_O = 0, 17, 34, 51

MAGIC = 12582912.0
SC_QK = np.float32(1.0 / np.sqrt(128.0))
EXPC = np.float32(-15.0)
MASKVAL = -1.0e6
LN_EPS = 1.0e-5
RMS_EPS = 1.0e-6
Q_EPS = 1.0e-5

ENGS = ["sync", "gpsimd", "tensor", "vector", "scalar"]


class Sem:
    __slots__ = ("name", "v")

    def __init__(self, name):
        self.name = name
        self.v = 0

    def inc(self, n=1):
        self.v += n
        return self.v


FCNT = {e: 0 for e in ENGS}
FSEM = {"vector": "Fv", "scalar": "Fa", "gpsimd": "Fg", "tensor": "Ft",
        "sync": "Fs"}
FENCE_ALL = {"A": False, "B": False, "C": False, "D": False, "E1": False,
             "E": True, "F": False}
E1VN = True
E1CP = False
DEXP = False
DSTT = True


class Phase:

    def __init__(self, fence_all=False):
        self.fence_all = fence_all
        self.plan = {e: [] for e in ENGS}
        self.regs = {e: [] for e in ENGS}

    def want_regs(self, eng, names):
        self.regs[eng] = list(names)

    def step(self, eng, act, waits=(), inc=None, fence=False):
        if self.fence_all and eng in ("vector", "scalar"):
            fence = True
        inc_val = inc[0].inc(inc[1]) if inc is not None else None
        self.plan[eng].append((list(waits), act, inc, fence, inc_val))
        return inc_val

    def dma(self, eng, sem, out_ap, in_ap, waits=(), **kw):
        return self.step(
            eng,
            (lambda e, r, o=out_ap, i=in_ap, kw=kw: e.dma_start(out=o, in_=i, **kw)),
            waits=waits, inc=(sem, 16), fence=False)

    def run(self, nc, sems):
        def mk(engname):
            steps = self.plan[engname]
            regnames = self.regs[engname]
            fsem = sems[FSEM[engname]]

            def closure(eng):
                def F(inst):
                    FCNT[engname] += 1
                    inst.then_inc(fsem, 1)
                    eng.wait_ge(fsem, FCNT[engname])
                    return inst

                with ExitStack() as rs:
                    regs = {n: rs.enter_context(eng.register(n)) for n in regnames}
                    last = {}
                    for si, (waits, act, inc, fence, inc_val) in enumerate(steps):
                        for s, v in waits:
                            if v > 0 and last.get(s.name, -1) < v:
                                eng.wait_ge(sems[s.name], v)
                                last[s.name] = v
                        try:
                            if act is None:
                                inst = None
                            elif (act.__code__.co_argcount >= 3
                                  and act.__code__.co_varnames[2] == "F"):
                                inst = act(eng, regs, F)
                            else:
                                inst = act(eng, regs)
                        except Exception:
                            import traceback as _tb
                            print(f"STEP FAIL eng={engname} idx={si}")
                            _tb.print_exc()
                            raise
                        if inc is not None:
                            inst.then_inc(sems[inc[0].name], inc[1])
                            if fence:
                                eng.wait_ge(sems[inc[0].name], inc_val)
                        elif fence and inst is not None:
                            F(inst)
            return closure

        with nc.Block() as block:
            for engname in ENGS:
                if self.plan[engname]:
                    getattr(block, engname)(mk(engname))


def build_program():
    for k in FCNT:
        FCNT[k] = 0
    nc = bass.Bass("TRN2", target_bir_lowering=False, detect_race_conditions=False)

    xin = nc.dram_tensor("xin", [T, D], F32, kind="ExternalInput")
    wqkv = nc.dram_tensor("wqkv", [3, H, D, DH], F32, kind="ExternalInput")
    woutT = nc.dram_tensor("woutT", [16, 128, D], F32, kind="ExternalInput")
    w1c = nc.dram_tensor("w1c", [32, D, 256], F32, kind="ExternalInput")
    w2c = nc.dram_tensor("w2c", [32, D, 256], F32, kind="ExternalInput")
    w3c = nc.dram_tensor("w3c", [64, 128, D], F32, kind="ExternalInput")
    w1s = nc.dram_tensor("w1s", [2, 128, 8192], F32, kind="ExternalInput")
    w2s = nc.dram_tensor("w2s", [2, 128, 8192], F32, kind="ExternalInput")
    w3s = nc.dram_tensor("w3s", [2, 128, 8192], F32, kind="ExternalInput")
    masks = nc.dram_tensor("masks", [4, 128, 512], F32, kind="ExternalInput")
    ident = nc.dram_tensor("ident", [128, 128], F32, kind="ExternalInput")
    tab = nc.dram_tensor("tabi", [1, 128], I32, kind="ExternalInput")
    tabm = nc.dram_tensor("tabm", [128, 64], F32, kind="ExternalInput")
    out = nc.dram_tensor("outt", [T, D], F32, kind="ExternalOutput")

    ag_in = nc.dram_tensor("ag_in", [2, ROW], BF16, kind="Internal")
    ag_out = nc.dram_tensor("ag_out", [16, ROW], BF16, kind="Internal")
    q_dram = nc.dram_tensor("q_dram", [2, 128, H * 512], BF16, kind="Internal")
    h_dram = nc.dram_tensor("h_dram", [NT, 128, DFF], F32, kind="Internal")
    w3q_dram = nc.dram_tensor("w3q_dram", [64, 128, D], BF16, kind="Internal")

    sems = {}
    S_ = {}
    gctx = ExitStack()
    with gctx:
        for nm in ["sL", "sS", "sG", "cc", "tm", "vv", "aa", "gg",
                   "Fv", "Fa", "Fg", "Ft", "Fs"]:
            sems[nm] = gctx.enter_context(nc.semaphore(nm))
            S_[nm] = Sem(nm)
        sL, sS, sG, cc, tm, vv, aa, gg = (
            S_[k] for k in ["sL", "sS", "sG", "cc", "tm", "vv", "aa", "gg"])

        def sb(name, shape, dtype=F32, ctx=gctx):
            return ctx.enter_context(nc.sbuf_tensor(name, shape, dtype))

        def psum(name, shape, dtype=F32, ctx=gctx):
            return ctx.enter_context(nc.psum_tensor(name, shape, dtype))

        tab_sb = sb("tab_sb", [1, 128], I32)
        tabm_sb = sb("tabm_sb", [128, 64], F32)
        mask_sb = sb("mask_sb", [128, 4 * 512], BF16)
        id_sb = sb("id_sb", [128, 128], BF16)
        ones_row = sb("ones_row", [1, 128], F32)
        onesc = sb("onesc", [128, 1], F32)
        rsx = sb("rsx", [128, NT], F32)
        inv_sx = sb("inv_sx", [128, NT], F32)
        f1t = sb("f1t", [128, NT], F32)
        f2t = sb("f2t", [128, NT], F32)
        b1t = sb("b1t", [128, NT], F32)
        b2t = sb("b2t", [128, NT], F32)
        f3t = sb("f3t", [128, NT], F32)
        b3t = sb("b3t", [128, NT], F32)
        rsh = sb("rsh", [128, NT], F32)
        sw_b = sb("sw_b", [128, 3], F32)
        mh_b = sb("mh_b", [128, 3], F32)
        stats_f = sb("stats_f", [1, 16], F32)
        stats_bf = sb("stats_bf", [1, 16], BF16)
        tmp1 = sb("tmp1", [128, 8], F32)
        tmp2 = sb("tmp2", [128, 8], F32)

        ctxAC = ExitStack()
        ctxAC.__enter__()
        xq_fm = sb("xq_fm", [128, 16 * 1024], BF16, ctxAC)
        ctxAB = ExitStack()
        with ctxAB:
            xn_fm = sb("xn_fm", [128, 16 * 1024], BF16, ctxAB)

            ctxA = ExitStack()
            with ctxA:
                x_sb = sb("x_sb", [128, NT * D], F32, ctxA)
                xn_f = sb("xn_f", [128, D], F32, ctxA)
                xn_b = sb("xn_b", [128, D], BF16, ctxA)
                xq_b = sb("xq_b", [128, D], BF16, ctxA)
                i32b = sb("i32b", [128, D], I32, ctxA)
                strb = sb("strb", [128, 8192], F32, ctxA)
                bnst = sb("bnst", [128, 6], F32, ctxA)
                bnag = sb("bnag", [128, 2], F32, ctxA)
                strp = sb("strp", [128, 3], F32, ctxA)
                ztail = sb("ztail", [1, 1024], BF16, ctxA)
                pT = psum("pT", [128, 128], BF16, ctxA)
                pT2 = psum("pT2", [128, 128], BF16, ctxA)
                pR = psum("pR", [1, 8], F32, ctxA)

                P = Phase(fence_all=FENCE_ALL["A"])
                v_x = P.dma("sync", sL,
                            bass.AP(x_sb, 0, [[NT * D, 128], [D, NT], [1, D]]),
                            xin.ap().rearrange("(t p) d -> p t d", p=128))
                v_tab = P.dma("sync", sL, tab_sb[:, :], tab.ap())
                v_tabm = P.dma("sync", sL, tabm_sb[:, :], tabm.ap())
                v_mask = P.dma("gpsimd", sG,
                               bass.AP(mask_sb, 0, [[4 * 512, 128], [512, 4], [1, 512]]),
                               masks.ap().transpose([1, 0, 2]))
                v_id = P.dma("gpsimd", sG, id_sb[:, :], ident.ap())
                P.step("gpsimd", lambda e, r: e.memset(ones_row[:, :], 1.0))
                g_ones = P.step("gpsimd", lambda e, r: e.memset(onesc[:, :], 1.0),
                                inc=(gg, 1))

                P.step("vector", lambda e, r: e.memset(strp[:, :], 0.0))
                v_red = 0
                first = True
                for (wsrc, j) in [(w1s, 0), (w2s, 1), (w3s, 2)]:
                    for b in range(2):
                        waits = [] if first else [(vv, v_red)]
                        first = False
                        t_l = P.dma("sync", sL, strb[:, :], wsrc.ap()[b], waits=waits)
                        P.step("vector",
                               lambda e, r: e.reduce_sum(tmp1[:, 0:1], strb[:, :],
                                                         axis=AX.X,
                                                         apply_absolute_value=True),
                               waits=[(sL, t_l)], fence=True)
                        v_red = P.step(
                            "vector",
                            (lambda e, r, j=j: e.tensor_add(strp[:, j:j + 1],
                                                            strp[:, j:j + 1],
                                                            tmp1[:, 0:1])),
                            inc=(vv, 1), fence=True)
                t_red = None
                for j in range(3):
                    t_red = P.step(
                        "tensor",
                        (lambda e, r, j=j: e.matmul(pR[0:1, j:j + 1], strp[:, j:j + 1],
                                                    onesc[:, :], start=(j == 0),
                                                    stop=(j == 2),
                                                    skip_group_check=True)),
                        waits=[(vv, v_red), (gg, g_ones)],
                        inc=(tm, 1) if j == 2 else None)
                a_st = P.step("scalar", lambda e, r: e.copy(stats_f[:, 0:3], pR[0:1, 0:3]),
                              waits=[(tm, t_red)], inc=(aa, 1))

                def enc(e, r, F):
                    e.memset(stats_bf[:, :], 0.0)
                    F(e.tensor_copy(stats_bf[:, 0:3], stats_f[:, 0:3]))
                    F(e.tensor_copy(stats_f[:, 4:7], stats_bf[:, 0:3]))
                    F(e.tensor_sub(stats_f[:, 8:11], stats_f[:, 0:3], stats_f[:, 4:7]))
                    F(e.tensor_copy(stats_bf[:, 3:6], stats_f[:, 8:11]))
                    F(e.tensor_copy(stats_f[:, 4:7], stats_bf[:, 3:6]))
                    F(e.tensor_sub(stats_f[:, 8:11], stats_f[:, 8:11],
                                   stats_f[:, 4:7]))
                    return e.tensor_copy(stats_bf[:, 6:9], stats_f[:, 8:11])
                v_enc = P.step("vector", enc, waits=[(aa, a_st)], inc=(vv, 1))
                v_zt = P.step("vector", lambda e, r: e.memset(ztail[:, :], 0.0),
                              inc=(vv, 1), fence=True)
                z_t = 0
                for half in range(2):
                    z_t = P.dma("sync", sL,
                                bass.AP(ag_in, half * ROW + KV_AREA,
                                        [[ROW, 1], [1, 1024]]),
                                ztail[0:1, :], waits=[(vv, v_zt)])
                P.dma("gpsimd", sG,
                      bass.AP(ag_in, KV_AREA, [[ROW, 1], [1, 16]]),
                      stats_bf[0:1, 0:16], waits=[(vv, v_enc), (sL, z_t)])
                g_stats = sG.v

                def newton(e, F, src_ap, eps):
                    F(e.reciprocal(tmp1[:, 1:2], tmp1[:, 0:1]))
                    e.tensor_mul(tmp1[:, 2:3], tmp1[:, 1:2], tmp1[:, 1:2])
                    F(e.tensor_scalar(tmp1[:, 3:4], src_ap, 1.0, float(eps),
                                      OP.mult, OP.add))
                    F(e.tensor_mul(tmp1[:, 2:3], tmp1[:, 2:3], tmp1[:, 3:4]))
                    F(e.tensor_scalar(tmp1[:, 2:3], tmp1[:, 2:3], -0.5, 1.5,
                                      OP.mult, OP.add))
                    return e.tensor_mul(tmp1[:, 1:2], tmp1[:, 1:2], tmp1[:, 2:3])

                a_xn_copy = 0
                v_xq_copy = 0
                t_xnb_used = 0
                t_xqb_used = 0
                a_xq_prev = 0
                for tt in range(NT):
                    xs_ap = x_sb[:, tt * D:(tt + 1) * D]
                    P.step("vector", (lambda e, r, xs=xs_ap: e.tensor_mul(
                        xn_f[:, :], xs, xs)),
                        waits=[(sL, v_x), (tm, t_xnb_used), (aa, a_xq_prev)])
                    P.step("vector", lambda e, r: e.reduce_sum(
                        tmp2[:, 5:6], xn_f[:, :], axis=AX.X), fence=True)
                    P.step("vector", (lambda e, r, xs=xs_ap: e.reduce_sum(
                        tmp1[:, 4:5], xs, axis=AX.X)), fence=True)
                    def vbn(e, r, F):
                        e.tensor_scalar(tmp1[:, 4:5], tmp1[:, 4:5], 1.0 / D, None,
                                        OP.mult)
                        F(e.tensor_scalar(tmp1[:, 5:6], tmp2[:, 5:6], 1.0 / D, None,
                                          OP.mult))
                        F(e.tensor_mul(tmp1[:, 6:7], tmp1[:, 4:5], tmp1[:, 4:5]))
                        return e.tensor_sub(tmp2[:, 0:1], tmp1[:, 5:6], tmp1[:, 6:7])
                    v_bn = P.step("vector", vbn, inc=(vv, 1), fence=True)
                    a1 = P.step("scalar", lambda e, r: e.activation(
                        tmp1[:, 0:1], tmp2[:, 0:1], AF.Sqrt, bias=np.float32(LN_EPS)),
                        waits=[(vv, v_bn)], inc=(aa, 1))
                    P.step("vector", (lambda e, r, F: newton(e, F, tmp2[:, 0:1],
                                                             LN_EPS)),
                           waits=[(aa, a1)])
                    P.step("vector", lambda e, r: e.tensor_scalar(
                        tmp1[:, 4:5], tmp1[:, 4:5], -1.0, None, OP.mult), fence=True)
                    P.step("vector", (lambda e, r, xs=xs_ap: e.tensor_scalar(
                        xn_f[:, :], xs, tmp1[:, 4:5], tmp1[:, 1:2], OP.add, OP.mult)))
                    v_xnb = P.step("vector",
                                   lambda e, r: e.tensor_copy(xn_b[:, :], xn_f[:, :]),
                                   inc=(vv, 1))
                    P.step("vector", (lambda e, r, xs=xs_ap: e.tensor_mul(
                        xs, xn_f[:, :], xn_f[:, :])))
                    P.step("vector", (lambda e, r, xs=xs_ap: e.reduce_sum(
                        tmp2[:, 5:6], xs, axis=AX.X)), fence=True)
                    v_ms = P.step("vector", lambda e, r: e.tensor_scalar(
                        tmp2[:, 0:1], tmp2[:, 5:6], 1.0 / D, None, OP.mult),
                        inc=(vv, 1), fence=True)
                    a2 = P.step("scalar", lambda e, r: e.activation(
                        tmp1[:, 0:1], tmp2[:, 0:1], AF.Sqrt, bias=np.float32(RMS_EPS)),
                        waits=[(vv, v_ms)], inc=(aa, 1))
                    P.step("vector", (lambda e, r, F: newton(e, F, tmp2[:, 0:1],
                                                             RMS_EPS)),
                           waits=[(aa, a2)])
                    P.step("vector", lambda e, r: e.reduce_max(
                        tmp1[:, 6:7], xn_f[:, :], axis=AX.X,
                        apply_absolute_value=True), fence=True)

                    def scales(e, r, F, tt=tt):
                        F(e.tensor_mul(tmp1[:, 7:8], tmp1[:, 1:2], tmp1[:, 6:7]))
                        F(e.tensor_scalar(tmp1[:, 7:8], tmp1[:, 7:8], float(Q_EPS),
                                          None, OP.max))
                        e.tensor_scalar(inv_sx[:, tt:tt + 1], tmp1[:, 7:8], 1.0 / 127.0,
                                        None, OP.mult)
                        F(e.reciprocal(tmp2[:, 1:2], tmp1[:, 7:8]))
                        F(e.tensor_scalar(tmp2[:, 1:2], tmp2[:, 1:2], 127.0, None,
                                          OP.mult))
                        return e.tensor_mul(tmp2[:, 2:3], tmp2[:, 1:2], tmp1[:, 1:2])
                    P.step("vector", scales, fence=True)
                    P.step("vector", lambda e, r: e.tensor_scalar(
                        xn_f[:, :], xn_f[:, :], tmp2[:, 2:3], MAGIC, OP.mult,
                        OP.add), fence=True)
                    v_u = P.step("vector", (lambda e, r, tt=tt: e.tensor_scalar(
                        xn_f[:, :], xn_f[:, :], MAGIC, 0.0, OP.subtract, OP.add,
                        accum_out=rsx[:, tt:tt + 1])), inc=(vv, 1))
                    P.step("scalar", lambda e, r: e.copy(i32b[:, :], xn_f[:, :]),
                           waits=[(vv, v_u), (tm, t_xqb_used)], fence=True)
                    a_xq = P.step("scalar", lambda e, r: e.copy(xq_b[:, :], i32b[:, :]),
                                  inc=(aa, 1))
                    a_xq_prev = a_xq
                    for s_ in range(16):
                        t1 = P.step("tensor", (lambda e, r, s_=s_: e.transpose(
                            pT[:, :], xn_b[:, s_ * 128:(s_ + 1) * 128], id_sb[:, :])),
                            waits=[(vv, v_xnb), (sG, v_id), (aa, a_xn_copy)],
                            inc=(tm, 1))
                        if s_ == 15:
                            t_xnb_used = t1
                        a_xn_copy = P.step("scalar", (lambda e, r, s_=s_, tt=tt: e.copy(
                            xn_fm[:, s_ * 1024 + tt * 128: s_ * 1024 + (tt + 1) * 128],
                            pT[:, :])), waits=[(tm, t1)], inc=(aa, 1))
                        t2 = P.step("tensor", (lambda e, r, s_=s_: e.transpose(
                            pT2[:, :], xq_b[:, s_ * 128:(s_ + 1) * 128], id_sb[:, :])),
                            waits=[(aa, a_xq), (vv, v_xq_copy)], inc=(tm, 1))
                        if s_ == 15:
                            t_xqb_used = t2
                        v_xq_copy = P.step("vector", (lambda e, r, s_=s_, tt=tt:
                            e.tensor_copy(
                                xq_fm[:, s_ * 1024 + tt * 128: s_ * 1024 + (tt + 1) * 128],
                                pT2[:, :])), waits=[(tm, t2)], inc=(vv, 1))
                P.run(nc, sems)

            ctxB = ExitStack()
            with ctxB:
                wbuf = [sb(f"wbuf{i}", [128, 3 * 16 * 128], BF16, ctxB) for i in range(2)]
                kst = [sb(f"kst{i}", [128, H * 512], BF16, ctxB) for i in range(2)]
                vst = [sb(f"vst{i}", [128, H * 4 * 132], BF16, ctxB) for i in range(2)]
                qev = sb("qev", [128, 512], BF16, ctxB)
                vtm = sb("vtm", [128, 512], BF16, ctxB)
                pQ = psum("pQ", [128, 512], F32, ctxB)
                pK = psum("pK", [128, 512], F32, ctxB)
                pV = psum("pV", [128, 512], F32, ctxB)
                pVT = psum("pVT", [128, 128], BF16, ctxB)

                P = Phase(fence_all=FENCE_ALL["B"])
                for half in range(2):
                    P.step("vector", (lambda e, r, half=half: e.memset(
                        bass.AP(vst[half], 128, [[H * 4 * 132, 128], [132, 64], [1, 4]]),
                        1.0)))
                w_done = []
                tmB0 = tm.v
                head_mm_done = [tmB0 + 14 * (h + 1) for h in range(H)]
                for h in range(H):
                    par = h % 2
                    waits = [(tm, head_mm_done[h - 2])] if h >= 2 else []
                    tgt = None
                    for p_ in range(3):
                        tgt = P.dma(
                            "gpsimd", sG,
                            bass.AP(wbuf[par], p_ * 2048,
                                    [[3 * 16 * 128, 128], [128, 16], [1, 128]]),
                            wqkv.ap()[p_, h].rearrange("(s p) c -> p s c", p=128),
                            waits=waits if p_ == 0 else [])
                    w_done.append(tgt)

                a_qe = 0
                v_ke = 0
                v_ve = 0
                t_vt_used = 0
                st_q_last = 0
                for h in range(H):
                    par = h % 2
                    for half in range(2):
                        t_q = None
                        for s_ in range(16):
                            waits = [(sG, w_done[h])]
                            if s_ == 0:
                                waits.append((aa, a_qe))
                            t_q = P.step("tensor", (lambda e, r, s_=s_, half=half, par=par:
                                e.matmul(pQ[:, :],
                                         wbuf[par][:, 0 * 2048 + s_ * 128:(s_ + 1) * 128],
                                         xn_fm[:, s_ * 1024 + half * 512:
                                               s_ * 1024 + half * 512 + 512],
                                         start=(s_ == 0), stop=(s_ == 15))),
                                waits=waits, inc=(tm, 1) if s_ == 15 else None)
                        a_qe = P.step("scalar", lambda e, r: e.copy(qev[:, :], pQ[:, :]),
                                      waits=[(tm, t_q)], inc=(aa, 1))
                        st_q_last = P.dma(
                            "scalar", sS,
                            bass.AP(q_dram, half * 128 * H * 512 + h * 512,
                                    [[H * 512, 128], [1, 512]]),
                            qev[:, :], waits=[(aa, a_qe)])
                        t_k = None
                        for s_ in range(16):
                            waits = [(vv, v_ke)] if s_ == 0 else []
                            t_k = P.step("tensor", (lambda e, r, s_=s_, half=half, par=par:
                                e.matmul(pK[:, :],
                                         wbuf[par][:, 1 * 2048 + s_ * 128:
                                                   1 * 2048 + (s_ + 1) * 128],
                                         xn_fm[:, s_ * 1024 + half * 512:
                                               s_ * 1024 + half * 512 + 512],
                                         start=(s_ == 0), stop=(s_ == 15))),
                                waits=waits, inc=(tm, 1) if s_ == 15 else None)
                        v_ke = P.step("vector", (lambda e, r, h=h, half=half:
                            e.tensor_copy(kst[half][:, h * 512:(h + 1) * 512], pK[:, :])),
                            waits=[(tm, t_k)], inc=(vv, 1))
                        t_v = None
                        for s_ in range(16):
                            waits = [(vv, v_ve)] if s_ == 0 else []
                            t_v = P.step("tensor", (lambda e, r, s_=s_, half=half, par=par:
                                e.matmul(pV[:, :],
                                         wbuf[par][:, 2 * 2048 + s_ * 128:
                                                   2 * 2048 + (s_ + 1) * 128],
                                         xn_fm[:, s_ * 1024 + half * 512:
                                               s_ * 1024 + half * 512 + 512],
                                         start=(s_ == 0), stop=(s_ == 15))),
                                waits=waits, inc=(tm, 1) if s_ == 15 else None)
                        v_vt = P.step("vector", lambda e, r: e.tensor_copy(vtm[:, :],
                                                                           pV[:, :]),
                                      waits=[(tm, t_v), (tm, t_vt_used)], inc=(vv, 1))
                        for kts in range(4):
                            t_tr = P.step("tensor", (lambda e, r, kts=kts: e.transpose(
                                pVT[:, :], vtm[:, kts * 128:(kts + 1) * 128], id_sb[:, :])),
                                waits=[(vv, v_vt), (vv, v_ve)], inc=(tm, 1))
                            if kts == 3:
                                t_vt_used = t_tr
                            v_ve = P.step("vector", (lambda e, r, h=h, half=half, kts=kts:
                                e.tensor_copy(
                                    vst[half][:, (h * 4 + kts) * 132:
                                              (h * 4 + kts) * 132 + 128],
                                    pVT[:, :])), waits=[(tm, t_tr)], inc=(vv, 1))
                    assert tm.v == head_mm_done[h], (h, tm.v, head_mm_done[h])

                st_kv = 0
                for half in range(2):
                    for (srcb, base, width) in [(kst[half], 0, 8192),
                                                (vst[half], 8192, 8448)]:
                        st_kv = P.dma(
                            "scalar", sS,
                            bass.AP(ag_in, half * ROW + base,
                                    [[KVW, 128], [1, width]]),
                            srcb[:, :], waits=[(vv, v_ve), (vv, v_ke)])
                P.step("gpsimd", lambda e, r: e.collective_compute(
                    "AllGather", OP.bypass, replica_groups=[list(range(NCORE))],
                    ins=[ag_in.ap().opt()], outs=[ag_out.ap().opt()]),
                    waits=[(sS, st_kv), (sG, g_stats)], inc=(cc, 1))
                P.run(nc, sems)

        ctxC = ExitStack()
        with ctxC:
            w1st = [sb(f"w1st{i}", [128, 16 * 256], F32, ctxC) for i in range(2)]
            w2st = [sb(f"w2st{i}", [128, 16 * 256], F32, ctxC) for i in range(2)]
            wq1 = [sb(f"wq1_{i}", [128, 16 * 256], BF16, ctxC) for i in range(2)]
            wq2 = [sb(f"wq2_{i}", [128, 16 * 256], BF16, ctxC) for i in range(2)]
            w3st = [sb(f"w3st{i}", [128, D], F32, ctxC) for i in range(2)]
            wq3 = [sb(f"wq3_{i}", [128, D], BF16, ctxC) for i in range(2)]
            wgtb = sb("wgtb", [128, 16 * 256], BF16, ctxC)
            silb = sb("silb", [128, 256], F32, ctxC)
            h2b = sb("h2b", [128, 256], F32, ctxC)
            hbuf = [sb(f"hbuf{i}", [128, 256], F32, ctxC) for i in range(2)]
            gath = sb("gath", [1, 256], F32, ctxC)
            gath_b = sb("gath_b", [1, 128], BF16, ctxC)
            pH1 = [psum(f"pH1_{i}", [128, 256], F32, ctxC) for i in range(2)]
            pH2 = [psum(f"pH2_{i}", [128, 256], F32, ctxC) for i in range(2)]
            pB = psum("pBb", [128, 8], F32, ctxC)

            P = Phase(fence_all=FENCE_ALL["C"])
            g_st = P.dma("gpsimd", sG,
                         bass.AP(gath_b, 0, [[128, 1], [16, 8], [1, 16]]),
                         bass.AP(ag_out, KV_AREA, [[1, 1], [2 * ROW, 8], [1, 16]]),
                         waits=[(cc, 1)])
            P.step("vector", lambda e, r: e.tensor_copy(gath[0:1, 0:128],
                                                        gath_b[0:1, :]),
                   waits=[(sG, g_st)], fence=True)

            def dec(e, r, F):
                last = None
                for j in range(3):
                    F(e.tensor_reduce(gath[0:1, 160:168],
                                      bass.AP(gath, j, [[256, 1], [16, 8], [3, 3]]),
                                      AX.X, OP.add))
                    last = e.reduce_sum(stats_f[0:1, 8 + j:9 + j],
                                        gath[0:1, 160:168], axis=AX.X)
                    if j < 2:
                        F(last)
                return last
            P.step("vector", dec)
            NW = [float(DFF * D), float(DFF * D), float(D * DFF)]

            def scl(e, r, F):
                last = None
                for j in range(3):
                    F(e.tensor_scalar(stats_f[0:1, 8 + j:9 + j],
                                      stats_f[0:1, 8 + j:9 + j],
                                      1.0 / NW[j], None, OP.mult))
                    F(e.tensor_scalar(stats_f[0:1, 8 + j:9 + j],
                                      stats_f[0:1, 8 + j:9 + j],
                                      float(Q_EPS), None, OP.max))
                    e.tensor_scalar(stats_f[0:1, 12 + j:13 + j],
                                    stats_f[0:1, 8 + j:9 + j], 0.5, None, OP.mult)
                    last = e.reciprocal(stats_f[0:1, j:j + 1],
                                        stats_f[0:1, 8 + j:9 + j])
                return last
            v_scl = P.step("vector", scl, inc=(vv, 1))
            t_bc = None
            for j in range(6):
                so = j if j < 3 else (12 + j - 3)
                t_bc = P.step("tensor", (lambda e, r, j=j, so=so: e.matmul(
                    pB[:, j:j + 1], ones_row[0:1, :], stats_f[0:1, so:so + 1],
                    start=(j == 0), stop=(j == 5), skip_group_check=True)),
                    waits=[(vv, v_scl)], inc=(tm, 1) if j == 5 else None)
            a_bc = P.step("scalar", lambda e, r: (
                e.copy(sw_b[:, 0:3], pB[:, 0:3]),
                e.copy(mh_b[:, 0:3], pB[:, 3:6]))[-1], waits=[(tm, t_bc)], inc=(aa, 1))

            def ffd(e, r, F):
                e.tensor_scalar(f1t[:, :], inv_sx[:, :], mh_b[:, 0:1], None, OP.mult)
                F(e.tensor_scalar(f2t[:, :], inv_sx[:, :], mh_b[:, 1:2], None,
                                  OP.mult))
                e.tensor_mul(b1t[:, :], rsx[:, :], f1t[:, :])
                F(e.tensor_mul(b2t[:, :], rsx[:, :], f2t[:, :]))
                e.tensor_scalar(b1t[:, :], b1t[:, :], -1.0, None, OP.mult)
                return e.tensor_scalar(b2t[:, :], b2t[:, :], -1.0, None, OP.mult)
            v_ffd = P.step("vector", ffd, waits=[(aa, a_bc)], inc=(vv, 1))

            w3q_done = [0] * 64
            w3_tl = [0] * 64
            w3q_store = 0

            def emit_w3_load(P, j3):
                par = j3 % 2
                waits = [(vv, w3q_done[j3 - 2])] if j3 >= 2 else []
                w3_tl[j3] = P.dma("sync", sL, w3st[par][:, :], w3c.ap()[j3],
                                  waits=waits)

            def emit_w3_quant(P, j3):
                nonlocal w3q_store
                par = j3 % 2
                a_s3 = P.step("scalar", (lambda e, r, par=par: e.activation(
                    wq3[par][:, :], w3st[par][:, :], AF.Sign, bias=np.float32(0.5),
                    scale=sw_b[:, 2:3])), waits=[(sL, w3_tl[j3]), (aa, a_bc)],
                    inc=(aa, 1))
                P.step("vector", (lambda e, r, par=par: e.tensor_scalar(
                    wgtb[:, 0:D], w3st[par][:, :], sw_b[:, 2:3], 0.5, OP.mult,
                    OP.is_gt)), waits=[(aa, a_s3)], fence=True)
                w3q_done[j3] = P.step("vector", (lambda e, r, par=par:
                    e.scalar_tensor_tensor(wq3[par][:, :], wgtb[:, 0:D], 2.0,
                                           wq3[par][:, :], OP.mult, OP.add)),
                    inc=(vv, 1))
                w3q_store = P.dma("scalar", sS, w3q_dram.ap()[j3], wq3[par][:, :],
                                  waits=[(vv, w3q_done[j3])])
                if j3 + 2 < 64:
                    emit_w3_load(P, j3 + 2)

            w1done = [0] * 32
            w2done = [0] * 32
            wq_consume = [0] * 32
            h_mm_done = [0] * 32
            evac_done = [0] * (32 * NT)
            h_dma_t = [0] * (32 * NT)
            h_store = 0

            def emit_wload(P, j):
                par = j % 2
                waits = [(vv, wq_consume[j - 2])] if j >= 2 else []
                w1done[j] = P.dma(
                    "sync", sL,
                    bass.AP(w1st[par], 0, [[16 * 256, 128], [256, 16], [1, 256]]),
                    w1c.ap()[j].rearrange("(s p) c -> p s c", p=128), waits=waits)
                w2done[j] = P.dma(
                    "scalar", sS,
                    bass.AP(w2st[par], 0, [[16 * 256, 128], [256, 16], [1, 256]]),
                    w2c.ap()[j].rearrange("(s p) c -> p s c", p=128), waits=waits)

            emit_wload(P, 0)
            emit_wload(P, 1)
            emit_w3_load(P, 0)
            emit_w3_load(P, 1)
            for j in range(32):
                par = j % 2
                waits_q = [(sL, w1done[j]), (aa, a_bc)]
                if j >= 2:
                    waits_q.append((tm, h_mm_done[j - 2]))
                a_s1 = P.step("scalar", (lambda e, r, par=par: e.activation(
                    wq1[par][:, :], w1st[par][:, :], AF.Sign, bias=np.float32(0.5),
                    scale=sw_b[:, 0:1])), waits=waits_q, inc=(aa, 1))
                P.step("vector", (lambda e, r, par=par: e.tensor_scalar(
                    wgtb[:, :], w1st[par][:, :], sw_b[:, 0:1], 0.5, OP.mult,
                    OP.is_gt)), waits=[(aa, a_s1)], fence=True)
                P.step("vector", (lambda e, r, par=par: e.scalar_tensor_tensor(
                    wq1[par][:, :], wgtb[:, :], 2.0, wq1[par][:, :], OP.mult,
                    OP.add)))
                waits_q2 = [(sS, w2done[j]), (aa, a_bc)]
                if j >= 2:
                    waits_q2.append((tm, h_mm_done[j - 2]))
                a_s2 = P.step("scalar", (lambda e, r, par=par: e.activation(
                    wq2[par][:, :], w2st[par][:, :], AF.Sign, bias=np.float32(0.5),
                    scale=sw_b[:, 1:2])), waits=waits_q2, inc=(aa, 1))
                emit_w3_quant(P, 2 * j)
                P.step("vector", (lambda e, r, par=par: e.tensor_scalar(
                    wgtb[:, :], w2st[par][:, :], sw_b[:, 1:2], 0.5, OP.mult,
                    OP.is_gt)), waits=[(aa, a_s2)], fence=True)
                v_qw = P.step("vector", (lambda e, r, par=par: e.scalar_tensor_tensor(
                    wq2[par][:, :], wgtb[:, :], 2.0, wq2[par][:, :], OP.mult,
                    OP.add)), inc=(vv, 1))
                wq_consume[j] = v_qw
                emit_w3_quant(P, 2 * j + 1)
                if j + 2 < 32:
                    emit_wload(P, j + 2)

                t_h = None
                for tt in range(NT):
                    ppar = tt % 2
                    idx = j * NT + tt
                    for s_ in range(16):
                        waits = [(vv, v_qw)]
                        if s_ == 0 and idx >= 2:
                            waits.append((vv, evac_done[idx - 2]))
                        P.step("tensor", (lambda e, r, s_=s_, tt=tt, par=par, ppar=ppar:
                            e.matmul(pH1[ppar][:, :],
                                     xq_fm[:, s_ * 1024 + tt * 128:
                                           s_ * 1024 + (tt + 1) * 128],
                                     wq1[par][:, s_ * 256:(s_ + 1) * 256],
                                     start=(s_ == 0), stop=(s_ == 15))),
                            waits=waits)
                    for s_ in range(16):
                        t_h = P.step("tensor", (lambda e, r, s_=s_, tt=tt, par=par,
                                                ppar=ppar:
                            e.matmul(pH2[ppar][:, :],
                                     xq_fm[:, s_ * 1024 + tt * 128:
                                           s_ * 1024 + (tt + 1) * 128],
                                     wq2[par][:, s_ * 256:(s_ + 1) * 256],
                                     start=(s_ == 0), stop=(s_ == 15))),
                            inc=(tm, 1) if s_ == 15 else None)
                    sil_waits = [(tm, t_h), (vv, v_ffd)]
                    if idx >= 1:
                        sil_waits.append((vv, evac_done[idx - 1]))
                    a_sil = P.step("scalar", (lambda e, r, tt=tt, ppar=ppar:
                        e.activation(silb[:, :], pH1[ppar][:, :], AF.Silu,
                                     bias=b1t[:, tt:tt + 1], scale=f1t[:, tt:tt + 1])),
                        waits=sil_waits, inc=(aa, 1))
                    P.step("vector", (lambda e, r, tt=tt, ppar=ppar: e.tensor_scalar(
                        h2b[:, :], pH2[ppar][:, :], f2t[:, tt:tt + 1], b2t[:, tt:tt + 1],
                        OP.mult, OP.add)), waits=[(tm, t_h), (vv, v_ffd)],
                        fence=True)
                    hb = hbuf[idx % 2]
                    ev_waits = [(aa, a_sil)]
                    if idx >= 2:
                        ev_waits.append((sL, h_dma_t[idx - 2]))
                    evac_done[idx] = P.step("vector", (lambda e, r, hb=hb: e.tensor_mul(
                        hb[:, :], silb[:, :], h2b[:, :])), waits=ev_waits,
                        inc=(vv, 1))
                    h_mm_done[j] = t_h
                    h_dma_t[idx] = P.dma(
                        "sync", sL,
                        bass.AP(h_dram, tt * 128 * DFF + j * 256, [[DFF, 128], [1, 256]]),
                        hb[:, :], waits=[(vv, evac_done[idx])])
                    h_store = h_dma_t[idx]
            P.run(nc, sems)

        ctxAC.close()

        s_sb = sb("s_sb", [128, NT * D], BF16)
        ctxDE = ExitStack()
        ctxDE.__enter__()
        q_stage = sb("q_stage", [128, H * 512], BF16, ctxDE)
        ctxD = ExitStack()
        with ctxD:
            O_all = sb("O_all", [128, 2 * 16 * 528], F32, ctxD)
            kv_sb = [sb(f"kv_sb{i}", [128, KVW], BF16, ctxD) for i in range(2)]
            P_sb = [sb(f"P_sb{i}", [128, 512], BF16, ctxD) for i in range(3)]
            ctxDP = ExitStack()
            ctxDP.__enter__()
            pS = [psum(f"pS{i}", [128, 512], F32, ctxDP) for i in range(3)]
            pO = psum("pO", [128, 2048], F32, ctxDP)

            P = Phase(fence_all=FENCE_ALL["D"])
            P.want_regs("gpsimd", ["kr"])
            P.want_regs("sync", ["qr"])

            v_init = P.step("vector", lambda e, r: e.memset(O_all[:, :], 0.0),
                            inc=(vv, 1))

            tmD0 = tm.v
            pair_mm = [tmD0 + 320 * (i + 1) for i in range(NPAIR)]
            kv_done = [0] * NPAIR
            last_exp = [0, 0, 0]
            last_av = [0, 0, 0]
            last_oacc = v_init

            pair_last_st = [tmD0 + 320 * i + 15 * 20 + 16 for i in range(NPAIR)]
            q_done = [0] * NPAIR
            for i in range(NPAIR):
                par = i % 2
                waits = [(sL, v_tab), (cc, 1)]
                if i >= 2:
                    waits.append((tm, pair_mm[i - 2]))
                P.step("gpsimd", (lambda e, r, i=i: e.reg_load(
                    r["kr"], bass.AP(tab_sb, TAB_K + i, [[128, 1], [1, 1]]))),
                    waits=waits)
                kv_done[i] = P.step("gpsimd", (lambda e, r, par=par: e.dma_start(
                    out=kv_sb[par][:, :],
                    in_=bass.AP(ag_out, r["kr"], [[KVW, 128], [1, KVW]]))),
                    inc=(sG, 16))
            for i in range(NPAIR):
                q_waits = [(sL, v_tab), (sS, st_q_last), (cc, 1)]
                if i >= 1:
                    q_waits.append((tm, pair_last_st[i - 1]))
                if i < 2:
                    q_done[i] = P.dma("sync", sL, q_stage[:, :], q_dram.ap()[i],
                                      waits=q_waits)
                else:
                    P.step("sync", (lambda e, r, i=i: e.reg_load(
                        r["qr"], bass.AP(tab_sb, TAB_Q + i, [[128, 1], [1, 1]]))),
                        waits=q_waits)
                    q_done[i] = P.step("sync", (lambda e, r: e.dma_start(
                        out=q_stage[:, :],
                        in_=bass.AP(q_dram, r["qr"], [[H * 512, 128], [1, H * 512]]))),
                        inc=(sL, 16))

            for i in range(NPAIR):
                par = i % 2
                for h in range(H):
                    for kts in range(4):
                        b = (h * 4 + kts) % 3
                        waits = [(sG, kv_done[i]), (sL, q_done[i]),
                                 (aa, last_exp[b]), (vv, v_init)]
                        t_st = P.step("tensor", (lambda e, r, par=par, h=h, kts=kts, b=b:
                            e.matmul(pS[b][:, :],
                                     kv_sb[par][:, h * 512 + kts * 128:
                                                h * 512 + (kts + 1) * 128],
                                     q_stage[:, h * 512:(h + 1) * 512],
                                     start=True, stop=True)),
                            waits=waits, inc=(tm, 1))
                        if i < 2:
                            v_m = P.step("vector", (lambda e, r, b=b, kts=kts:
                                e.tensor_tensor(pS[b][:, :], pS[b][:, :],
                                                mask_sb[:, kts * 512:(kts + 1) * 512],
                                                OP.add)),
                                waits=[(tm, t_st), (sG, v_mask)], inc=(vv, 1))
                            exp_waits = [(vv, v_m)]
                        else:
                            exp_waits = [(tm, t_st)]
                        exp_waits.append((tm, last_av[b]))
                        a_e = P.step("scalar", (lambda e, r, b=b: e.activation(
                            P_sb[b][:, :], pS[b][:, :], AF.Exp, bias=EXPC,
                            scale=SC_QK)), waits=exp_waits, inc=(aa, 1),
                            fence=DEXP)
                        last_exp[b] = a_e
                        av_waits = [(aa, a_e)]
                        if kts == 0:
                            av_waits.append((vv, last_oacc))
                        t_av = None
                        for qs in range(4):
                            t_av = P.step("tensor", (lambda e, r, b=b, qs=qs, par=par,
                                                     h=h, kts=kts:
                                e.matmul(pO[:, qs * 512:qs * 512 + 129],
                                         P_sb[b][:, qs * 128:(qs + 1) * 128],
                                         kv_sb[par][:, 8192 + (h * 4 + kts) * 132:
                                                    8192 + (h * 4 + kts) * 132 + 129],
                                         start=(kts == 0), stop=(kts == 3),
                                         skip_group_check=True)),
                                waits=av_waits if qs == 0 else [], inc=(tm, 1))
                        last_av[b] = t_av
                    pO_ap = bass.AP(pO, 0, [[2048, 128], [512, 4], [1, 129]])
                    for half in range(2):
                        oap = bass.AP(O_all, half * 8448 + h * 528,
                                      [[2 * 16 * 528, 128], [132, 4], [1, 129]])
                        st_ = P.step("vector", (lambda e, r, oap=oap, i=i, half=half:
                            e.scalar_tensor_tensor(
                                oap, pO_ap, tabm_sb[:, 2 * i + half:2 * i + half + 1],
                                oap, OP.mult, OP.add)),
                            waits=[(tm, last_av[0]), (tm, last_av[1]),
                                   (tm, last_av[2]), (sL, v_tabm)],
                            inc=(vv, 1) if half == 1 else None, fence=DSTT)
                    last_oacc = st_
                assert tm.v == pair_mm[i], (tm.v, pair_mm[i])
            P.run(nc, sems)
            ctxDP.close()

            o_fm = kv_sb[0]
            pE1ctx = ExitStack()
            with pE1ctx:
                o_nrm = [sb(f"o_nrm{i}", [128, 128], BF16, pE1ctx) for i in range(2)]
                pOT = [psum(f"pOT{i}", [128, 128], BF16, pE1ctx) for i in range(2)]
                P = Phase(fence_all=FENCE_ALL["E1"])
                P.step("vector", lambda e, r: e.reciprocal(
                    bass.AP(O_all, 128, [[2 * 16 * 528, 128], [132, 128], [1, 1]]),
                    bass.AP(O_all, 128, [[2 * 16 * 528, 128], [132, 128], [1, 1]])),
                    fence=True)
                last_tr = [0, 0]
                last_cp = [0, 0]
                for h in range(H):
                    for qt in range(8):
                        b = (h * 8 + qt) % 2
                        off = (qt // 4) * 8448 + h * 528 + (qt % 4) * 132
                        v_n = P.step("vector", (lambda e, r, off=off, b=b:
                            e.tensor_scalar(o_nrm[b][:, :],
                                            O_all[:, off:off + 128],
                                            O_all[:, off + 128:off + 129],
                                            None, OP.mult)),
                            waits=[(tm, last_tr[b])], inc=(vv, 1), fence=E1VN)
                        t_t = P.step("tensor", (lambda e, r, b=b: e.transpose(
                            pOT[b][:, :], o_nrm[b][:, :], id_sb[:, :])),
                            waits=[(vv, v_n), (aa, last_cp[b])], inc=(tm, 1))
                        last_tr[b] = t_t
                        off2 = h * 1024 + qt * 128
                        last_cp[b] = P.step("scalar", (lambda e, r, off2=off2, b=b:
                            e.copy(o_fm[:, off2:off2 + 128], pOT[b][:, :])),
                            waits=[(tm, t_t)], inc=(aa, 1), fence=E1CP)
                P.run(nc, sems)

            ctxE = ExitStack()
            with ctxE:
                woc = [sb(f"woc{i}", [128, 512], BF16, ctxE) for i in range(2)]
                pSP = [psum(f"pSP{i}", [128, 512], F32, ctxE) for i in range(8)]
                P = Phase(fence_all=FENCE_ALL["E"])
                v_se = [0] * 8
                wo_last = [0, 0]
                for dc in range(4):
                    t_s = [None] * 8
                    for s_ in range(16):
                        wpar = s_ % 2
                        wo_waits = [(tm, wo_last[wpar])] if (dc > 0 or s_ >= 2) else []
                        wo_done = P.dma(
                            "gpsimd", sG, woc[wpar][:, :],
                            bass.AP(woutT, s_ * 128 * D + dc * 512,
                                    [[D, 128], [1, 512]]),
                            waits=wo_waits)
                        for qt in range(8):
                            waits = [(sG, wo_done)]
                            if s_ == 0:
                                waits.append((vv, v_se[qt]))
                            t_s[qt] = P.step("tensor", (lambda e, r, s_=s_, qt=qt,
                                                        dc=dc, wpar=wpar:
                                e.matmul(pSP[qt][:, :],
                                         o_fm[:, s_ * 1024 + qt * 128:
                                              s_ * 1024 + (qt + 1) * 128],
                                         woc[wpar][:, :],
                                         start=(s_ == 0), stop=(s_ == 15))),
                                waits=waits,
                                inc=(tm, 1) if qt == 7 else None)
                        wo_last[wpar] = t_s[7]
                    for qt in range(8):
                        v_se[qt] = P.step("vector", (lambda e, r, qt=qt, dc=dc:
                            e.tensor_copy(s_sb[:, qt * D + dc * 512:
                                               qt * D + (dc + 1) * 512],
                                          pSP[qt][:, :])),
                            waits=[(tm, t_s[7])], inc=(vv, 1))
                P.run(nc, sems)

        ctxDE.close()

        ctxF = ExitStack()
        with ctxF:
            hst = [sb(f"hst{i}", [128, DFF], F32, ctxF) for i in range(2)]
            hq_fm = sb("hq_fm", [128, 64 * 512], BF16, ctxF)
            hq_tm = sb("hq_tm", [128, 2048], BF16, ctxF)
            i32f = sb("i32f", [128, 2048], I32, ctxF)
            w3t = [sb(f"w3t{i}", [128, 4 * 512], BF16, ctxF) for i in range(2)]
            g_b = sb("g_b", [128, 512], BF16, ctxF)
            xre = [sb(f"xre{i}", [128, 512], F32, ctxF) for i in range(2)]
            ob = [sb(f"ob{i}", [128, 512], F32, ctxF) for i in range(2)]
            hsq = sb("hsq", [128, 2048], F32, ctxF)
            acc4 = sb("acc4", [128, 4], F32, ctxF)
            pG = [psum(f"pG{i}", [128, 512], F32, ctxF) for i in range(4)]
            pFT = [psum(f"pFT{i}", [128, 128], BF16, ctxF) for i in range(2)]

            P = Phase(fence_all=FENCE_ALL["F"])

            def newtonF(e, src_ap, eps):
                e.reciprocal(tmp1[:, 1:2], tmp1[:, 0:1])
                e.tensor_mul(tmp1[:, 2:3], tmp1[:, 1:2], tmp1[:, 1:2])
                e.tensor_scalar(tmp1[:, 3:4], src_ap, 1.0, float(eps), OP.mult, OP.add)
                e.tensor_mul(tmp1[:, 2:3], tmp1[:, 2:3], tmp1[:, 3:4])
                e.tensor_scalar(tmp1[:, 2:3], tmp1[:, 2:3], -0.5, 1.5, OP.mult, OP.add)
                return e.tensor_mul(tmp1[:, 1:2], tmp1[:, 1:2], tmp1[:, 2:3])

            h_loaded = [0] * NT
            h_loaded[0] = P.dma("sync", sL, hst[0][:, :], h_dram.ap()[0],
                                waits=[(sL, h_store)])
            h_loaded[1] = P.dma("sync", sL, hst[1][:, :], h_dram.ap()[1])

            t_trF_used = [0, 0]
            a_cpF = [0, 0]
            storm_done = 0
            st_out = 0
            x_loaded = {}
            g_done = [0] * 4
            w3t_last = [0, 0]
            last_vo = 0
            last_vo_bo = [0, 0]
            st_out_bo = [0, 0]
            a_hq = 0
            for tg in range(2):
                for tl in range(4):
                    tt = tg * 4 + tl
                    par = tt % 2
                    for q4 in range(4):
                        sl = slice(q4 * 2048, (q4 + 1) * 2048)
                        P.step("vector", (lambda e, r, par=par, sl=sl:
                            e.tensor_mul(hsq[:, :], hst[par][:, sl], hst[par][:, sl])),
                            waits=[(sL, h_loaded[tt])])
                        P.step("vector", (lambda e, r, q4=q4: e.reduce_sum(
                            acc4[:, q4:q4 + 1], hsq[:, :], axis=AX.X)), fence=True)
                    def vbnF(e, r, F):
                        F(e.reduce_sum(tmp2[:, 5:6], acc4[:, :], axis=AX.X))
                        return e.tensor_scalar(tmp2[:, 0:1], tmp2[:, 5:6], 1.0 / DFF,
                                               None, OP.mult)
                    v_bn = P.step("vector", vbnF, inc=(vv, 1), fence=True)
                    a1 = P.step("scalar", lambda e, r: e.activation(
                        tmp1[:, 0:1], tmp2[:, 0:1], AF.Sqrt, bias=np.float32(RMS_EPS)),
                        waits=[(vv, v_bn)], inc=(aa, 1))
                    P.step("vector", (lambda e, r: newtonF(e, tmp2[:, 0:1], RMS_EPS)),
                           waits=[(aa, a1)])
                    P.step("vector", (lambda e, r, par=par: e.reduce_max(
                        tmp1[:, 6:7], hst[par][:, :], axis=AX.X,
                        apply_absolute_value=True)), fence=True)

                    def scalesF(e, r, F, tt=tt):
                        F(e.tensor_mul(tmp1[:, 7:8], tmp1[:, 1:2], tmp1[:, 6:7]))
                        F(e.tensor_scalar(tmp1[:, 7:8], tmp1[:, 7:8], float(Q_EPS),
                                          None, OP.max))
                        F(e.tensor_scalar(tmp2[:, 3:4], tmp1[:, 7:8], 1.0 / 127.0,
                                          None, OP.mult))
                        e.tensor_scalar(f3t[:, tt:tt + 1], tmp2[:, 3:4], mh_b[:, 2:3],
                                        None, OP.mult)
                        F(e.reciprocal(tmp2[:, 1:2], tmp1[:, 7:8]))
                        F(e.tensor_scalar(tmp2[:, 1:2], tmp2[:, 1:2], 127.0, None,
                                          OP.mult))
                        return e.tensor_mul(tmp2[:, 2:3], tmp2[:, 1:2], tmp1[:, 1:2])
                    P.step("vector", scalesF, fence=True)
                    for q4 in range(4):
                        sl = slice(q4 * 2048, (q4 + 1) * 2048)
                        P.step("vector", (lambda e, r, par=par, sl=sl: e.tensor_scalar(
                            hst[par][:, sl], hst[par][:, sl], tmp2[:, 2:3], MAGIC,
                            OP.mult, OP.add)), fence=True)
                        v_u = P.step("vector", (lambda e, r, par=par, sl=sl, q4=q4:
                            e.tensor_scalar(hst[par][:, sl], hst[par][:, sl], MAGIC,
                                            0.0, OP.subtract, OP.add,
                                            accum_out=acc4[:, q4:q4 + 1])),
                            inc=(vv, 1))
                        P.step("scalar", (lambda e, r, par=par, sl=sl: e.copy(
                            i32f[:, :], hst[par][:, sl])),
                            waits=[(vv, v_u), (tm, t_trF_used[0]),
                                   (tm, t_trF_used[1])], fence=True)
                        a_hq = P.step("scalar", lambda e, r: e.copy(hq_tm[:, :],
                                                                    i32f[:, :]),
                                      inc=(aa, 1))
                        for s2 in range(16):
                            ds = q4 * 16 + s2
                            btr = ds % 2
                            t_t = P.step("tensor", (lambda e, r, s2=s2, btr=btr:
                                e.transpose(pFT[btr][:, :],
                                            hq_tm[:, s2 * 128:(s2 + 1) * 128],
                                            id_sb[:, :])),
                                waits=[(aa, a_hq), (vv, a_cpF[btr]),
                                       (tm, storm_done)], inc=(tm, 1))
                            t_trF_used[btr] = t_t
                            a_cpF[btr] = P.step("vector", (lambda e, r, ds=ds, tl=tl,
                                                           btr=btr:
                                e.tensor_copy(hq_fm[:, ds * 512 + tl * 128:
                                                    ds * 512 + (tl + 1) * 128],
                                              pFT[btr][:, :])),
                                waits=[(tm, t_t)], inc=(vv, 1), fence=True)
                    def vrsF(e, r, F, tt=tt):
                        F(e.reduce_sum(rsh[:, tt:tt + 1], acc4[:, :], axis=AX.X))
                        F(e.tensor_mul(tmp2[:, 4:5], rsh[:, tt:tt + 1],
                                       f3t[:, tt:tt + 1]))
                        return e.tensor_scalar(b3t[:, tt:tt + 1], tmp2[:, 4:5], -1.0,
                                               None, OP.mult)
                    v_rs = P.step("vector", vrsF, inc=(vv, 1), fence=True)
                    nxt = tt + 2
                    if nxt < NT:
                        h_loaded[nxt] = P.dma("sync", sL, hst[par][:, :],
                                              h_dram.ap()[nxt],
                                              waits=[(vv, v_rs), (aa, a_hq)])
                hq_ready = vv.v

                for dq in range(4):
                    w3_done = [0] * 16
                    for s4 in range(2):
                        w3_done[s4] = P.dma(
                            "scalar", sS,
                            bass.AP(w3t[s4 % 2], 0, [[4 * 512, 128], [512, 4],
                                                     [1, 512]]),
                            bass.AP(w3q_dram, s4 * 4 * 128 * D + dq * 512,
                                    [[D, 128], [128 * D, 4], [1, 512]]),
                            waits=[(sS, w3q_store), (tm, w3t_last[s4 % 2])])
                    ds_mm = [0] * 64
                    for s4 in range(16):
                        wpar = s4 % 2
                        for dsl in range(4):
                            ds = s4 * 4 + dsl
                            for tl in range(4):
                                waits = [(sS, w3_done[s4]), (vv, hq_ready)]
                                if ds == 0:
                                    waits.append((aa, g_done[tl]))
                                ds_mm[ds] = P.step("tensor", (lambda e, r, ds=ds,
                                                              tl=tl, wpar=wpar,
                                                              dsl=dsl:
                                    e.matmul(pG[tl][:, :],
                                             hq_fm[:, ds * 512 + tl * 128:
                                                   ds * 512 + (tl + 1) * 128],
                                             w3t[wpar][:, dsl * 512:
                                                       (dsl + 1) * 512],
                                             start=(ds == 0), stop=(ds == 63))),
                                    waits=waits,
                                    inc=(tm, 1) if tl == 3 else None)
                        w3t_last[wpar] = ds_mm[s4 * 4 + 3]
                        if s4 + 2 < 16:
                            w3_done[s4 + 2] = P.dma(
                                "scalar", sS,
                                bass.AP(w3t[wpar], 0, [[4 * 512, 128], [512, 4],
                                                       [1, 512]]),
                                bass.AP(w3q_dram, (s4 + 2) * 4 * 128 * D + dq * 512,
                                        [[D, 128], [128 * D, 4], [1, 512]]),
                                waits=[(tm, ds_mm[s4 * 4 + 3])])
                    storm_done = ds_mm[63]
                    for tl in range(4):
                        tt = tg * 4 + tl
                        bo = tl % 2
                        xk = (tg, dq, tl)
                        x_loaded[xk] = P.dma(
                            "sync", sL, xre[bo][:, :],
                            bass.AP(xin, (tt * 128) * D + dq * 512,
                                    [[D, 128], [1, 512]]),
                            waits=[(vv, last_vo_bo[bo])])
                        a_g = P.step("scalar", (lambda e, r, tl=tl, tt=tt:
                            e.activation(g_b[:, :], pG[tl][:, :], AF.Silu,
                                         bias=b3t[:, tt:tt + 1],
                                         scale=f3t[:, tt:tt + 1])),
                            waits=[(tm, storm_done), (vv, last_vo)], inc=(aa, 1))
                        g_done[tl] = a_g
                        def vcomb(e, r, F, tt=tt, dq=dq, bo=bo):
                            F(e.tensor_mul(g_b[:, :],
                                           s_sb[:, tt * D + dq * 512:
                                                tt * D + (dq + 1) * 512],
                                           g_b[:, :]))
                            return e.tensor_add(ob[bo][:, :], xre[bo][:, :],
                                                g_b[:, :])
                        v_o = P.step("vector", vcomb,
                            waits=[(aa, a_g), (sL, x_loaded[xk]),
                                   (sL, st_out_bo[bo])], inc=(vv, 1), fence=True)
                        last_vo = v_o
                        last_vo_bo[bo] = v_o
                        st_out_bo[bo] = P.dma(
                            "sync", sL,
                            bass.AP(out, (tt * 128) * D + dq * 512,
                                    [[D, 128], [1, 512]]),
                            ob[bo][:, :], waits=[(vv, v_o)])
                        st_out = st_out_bo[bo]
            P.step("sync", None, waits=[(sL, st_out)])
            P.run(nc, sems)

    return nc



def _host_prep(L0_attn_x, L0_attn_w_in, L0_attn_w_out, L0_ffn_w1, L0_ffn_w2,
               L0_ffn_w3):
    c = np.ascontiguousarray
    x = np.asarray(L0_attn_x, np.float32).reshape(S, D)
    w_in = np.asarray(L0_attn_w_in, np.float32)
    w_out = np.asarray(L0_attn_w_out, np.float32)
    w1 = np.asarray(L0_ffn_w1, np.float32)
    w2 = np.asarray(L0_ffn_w2, np.float32)
    w3 = np.asarray(L0_ffn_w3, np.float32)

    wqkv = c(w_in.reshape(H, 3, DH, D).transpose(1, 0, 3, 2))
    woutT = c(w_out.T.reshape(16, 128, D))
    w1c = c(w1.T.reshape(D, 32, 256).transpose(1, 0, 2))
    w2c = c(w2.T.reshape(D, 32, 256).transpose(1, 0, 2))
    w3c = c(w3.T.reshape(64, 128, D))

    kk = np.arange(512).reshape(4, 128)
    qq = np.arange(512)
    masks = np.where(kk[:, :, None] <= qq[None, None, :], 0.0, MASKVAL
                     ).astype(np.float32)
    ident = np.eye(128, dtype=np.float32)

    shared = dict(wqkv=wqkv, woutT=woutT, w1c=w1c, w2c=w2c, w3c=w3c,
                  masks=masks, ident=ident)

    def rowof(g):
        return 2 * g if g <= 7 else 2 * (15 - g) + 1

    in_maps = []
    for core in range(NCORE):
        chA, chB = core, 15 - core
        xc = c(np.concatenate([x[CH * chA:CH * (chA + 1)],
                               x[CH * chB:CH * (chB + 1)]], axis=0))
        w1s_ = c(w1[1024 * core:1024 * (core + 1)].reshape(2, 128, 8192))
        w2s_ = c(w2[1024 * core:1024 * (core + 1)].reshape(2, 128, 8192))
        w3s_ = c(w3[256 * core:256 * (core + 1)].reshape(2, 128, 8192))
        pairs = [(0, chA), (1, chB)]
        pairs += [(0, j) for j in range(chA)]
        pairs += [(1, j) for j in range(chB)]
        assert len(pairs) == NPAIR
        tabv = np.zeros(128, np.int32)
        tabmv = np.zeros((128, 64), np.float32)
        for i, (qsel, kch) in enumerate(pairs):
            tabv[TAB_K + i] = rowof(kch) * ROW
            tabv[TAB_Q + i] = qsel * (128 * H * 512)
            tabmv[:, 2 * i + qsel] = 1.0
        in_maps.append(dict(shared, xin=xc, w1s=w1s_, w2s=w2s_, w3s=w3s_,
                            tabi=tabv.reshape(1, 128), tabm=tabmv))
    return in_maps


def kernel(L0_attn_x, L0_attn_w_in, L0_attn_w_out, L0_ffn_w1, L0_ffn_w2,
           L0_ffn_w3):
    in_maps = _host_prep(L0_attn_x, L0_attn_w_in, L0_attn_w_out, L0_ffn_w1,
                         L0_ffn_w2, L0_ffn_w3)
    nc = build_program()
    res = run_bass_kernel_spmd(nc, in_maps, core_ids=list(range(NCORE)))
    full = np.zeros((1, S, D), np.float32)
    for core in range(NCORE):
        o = np.asarray(res.results[core]["outt"], np.float32)
        chA, chB = core, 15 - core
        full[0, CH * chA:CH * (chA + 1)] = o[0:CH]
        full[0, CH * chB:CH * (chB + 1)] = o[CH:2 * CH]
    return full


if __name__ == "__main__":
    nc = build_program()
    print("build OK; instructions:", nc.n_instructions())


# version 20
# speedup vs baseline: 1.0189x; 1.0115x over previous
from contextlib import ExitStack

import numpy as np
import concourse.bass as bass
import concourse.mybir as mybir
from concourse.bass_utils import run_bass_kernel_spmd

F32 = mybir.dt.float32
BF16 = mybir.dt.bfloat16
I32 = mybir.dt.int32
AF = mybir.ActivationFunctionType
OP = mybir.AluOpType
AX = mybir.AxisListType

NCORE = 8
S = 8192
D = 2048
H = 16
DH = 128
DFF = 8192
T = 1024
NT = 8
CH = 512

KVW = 8192 + 8448
KV_AREA = 128 * KVW
STATS = 1024
ROW = KV_AREA + STATS

NPAIR = 17
TAB_K, TAB_V, TAB_Q, TAB_O = 0, 17, 34, 51

MAGIC = 12582912.0
SC_QK = np.float32(1.0 / np.sqrt(128.0))
EXPC = np.float32(-15.0)
MASKVAL = -1.0e6
LN_EPS = 1.0e-5
RMS_EPS = 1.0e-6
Q_EPS = 1.0e-5

ENGS = ["sync", "gpsimd", "tensor", "vector", "scalar"]


class Sem:
    __slots__ = ("name", "v")

    def __init__(self, name):
        self.name = name
        self.v = 0

    def inc(self, n=1):
        self.v += n
        return self.v


FCNT = {e: 0 for e in ENGS}
FSEM = {"vector": "Fv", "scalar": "Fa", "gpsimd": "Fg", "tensor": "Ft",
        "sync": "Fs"}
FENCE_ALL = {"A": False, "B": False, "C": False, "D": False, "E1": False,
             "E": True, "F": True}
E1VN = True
E1CP = False
DEXP = False
DSTT = True


class Phase:

    def __init__(self, fence_all=False):
        self.fence_all = fence_all
        self.plan = {e: [] for e in ENGS}
        self.regs = {e: [] for e in ENGS}

    def want_regs(self, eng, names):
        self.regs[eng] = list(names)

    def step(self, eng, act, waits=(), inc=None, fence=False):
        if self.fence_all and eng in ("vector", "scalar"):
            fence = True
        inc_val = inc[0].inc(inc[1]) if inc is not None else None
        self.plan[eng].append((list(waits), act, inc, fence, inc_val))
        return inc_val

    def dma(self, eng, sem, out_ap, in_ap, waits=(), **kw):
        return self.step(
            eng,
            (lambda e, r, o=out_ap, i=in_ap, kw=kw: e.dma_start(out=o, in_=i, **kw)),
            waits=waits, inc=(sem, 16), fence=False)

    def run(self, nc, sems):
        def mk(engname):
            steps = self.plan[engname]
            regnames = self.regs[engname]
            fsem = sems[FSEM[engname]]

            def closure(eng):
                def F(inst):
                    FCNT[engname] += 1
                    inst.then_inc(fsem, 1)
                    eng.wait_ge(fsem, FCNT[engname])
                    return inst

                with ExitStack() as rs:
                    regs = {n: rs.enter_context(eng.register(n)) for n in regnames}
                    last = {}
                    for si, (waits, act, inc, fence, inc_val) in enumerate(steps):
                        for s, v in waits:
                            if v > 0 and last.get(s.name, -1) < v:
                                eng.wait_ge(sems[s.name], v)
                                last[s.name] = v
                        try:
                            if act is None:
                                inst = None
                            elif (act.__code__.co_argcount >= 3
                                  and act.__code__.co_varnames[2] == "F"):
                                inst = act(eng, regs, F)
                            else:
                                inst = act(eng, regs)
                        except Exception:
                            import traceback as _tb
                            print(f"STEP FAIL eng={engname} idx={si}")
                            _tb.print_exc()
                            raise
                        if inc is not None:
                            inst.then_inc(sems[inc[0].name], inc[1])
                            if fence:
                                eng.wait_ge(sems[inc[0].name], inc_val)
                        elif fence and inst is not None:
                            F(inst)
            return closure

        with nc.Block() as block:
            for engname in ENGS:
                if self.plan[engname]:
                    getattr(block, engname)(mk(engname))


def build_program():
    for k in FCNT:
        FCNT[k] = 0
    nc = bass.Bass("TRN2", target_bir_lowering=False, detect_race_conditions=False)

    xin = nc.dram_tensor("xin", [T, D], F32, kind="ExternalInput")
    wqkv = nc.dram_tensor("wqkv", [3, H, D, DH], F32, kind="ExternalInput")
    woutT = nc.dram_tensor("woutT", [16, 128, D], F32, kind="ExternalInput")
    w1c = nc.dram_tensor("w1c", [32, D, 256], F32, kind="ExternalInput")
    w2c = nc.dram_tensor("w2c", [32, D, 256], F32, kind="ExternalInput")
    w3c = nc.dram_tensor("w3c", [64, 128, D], F32, kind="ExternalInput")
    w1s = nc.dram_tensor("w1s", [2, 128, 8192], F32, kind="ExternalInput")
    w2s = nc.dram_tensor("w2s", [2, 128, 8192], F32, kind="ExternalInput")
    w3s = nc.dram_tensor("w3s", [2, 128, 8192], F32, kind="ExternalInput")
    masks = nc.dram_tensor("masks", [4, 128, 512], F32, kind="ExternalInput")
    ident = nc.dram_tensor("ident", [128, 128], F32, kind="ExternalInput")
    tab = nc.dram_tensor("tabi", [1, 128], I32, kind="ExternalInput")
    tabm = nc.dram_tensor("tabm", [128, 64], F32, kind="ExternalInput")
    out = nc.dram_tensor("outt", [T, D], F32, kind="ExternalOutput")

    ag_in = nc.dram_tensor("ag_in", [2, ROW], BF16, kind="Internal")
    ag_out = nc.dram_tensor("ag_out", [16, ROW], BF16, kind="Internal")
    q_dram = nc.dram_tensor("q_dram", [2, 128, H * 512], BF16, kind="Internal")
    h_dram = nc.dram_tensor("h_dram", [NT, 128, DFF], F32, kind="Internal")
    w3q_dram = nc.dram_tensor("w3q_dram", [64, 128, D], BF16, kind="Internal")

    sems = {}
    S_ = {}
    gctx = ExitStack()
    with gctx:
        for nm in ["sL", "sS", "sG", "cc", "tm", "vv", "aa", "gg",
                   "Fv", "Fa", "Fg", "Ft", "Fs"]:
            sems[nm] = gctx.enter_context(nc.semaphore(nm))
            S_[nm] = Sem(nm)
        sL, sS, sG, cc, tm, vv, aa, gg = (
            S_[k] for k in ["sL", "sS", "sG", "cc", "tm", "vv", "aa", "gg"])

        def sb(name, shape, dtype=F32, ctx=gctx):
            return ctx.enter_context(nc.sbuf_tensor(name, shape, dtype))

        def psum(name, shape, dtype=F32, ctx=gctx):
            return ctx.enter_context(nc.psum_tensor(name, shape, dtype))

        tab_sb = sb("tab_sb", [1, 128], I32)
        tabm_sb = sb("tabm_sb", [128, 64], F32)
        mask_sb = sb("mask_sb", [128, 4 * 512], BF16)
        id_sb = sb("id_sb", [128, 128], BF16)
        ones_row = sb("ones_row", [1, 128], F32)
        onesc = sb("onesc", [128, 1], F32)
        rsx = sb("rsx", [128, NT], F32)
        inv_sx = sb("inv_sx", [128, NT], F32)
        f1t = sb("f1t", [128, NT], F32)
        f2t = sb("f2t", [128, NT], F32)
        b1t = sb("b1t", [128, NT], F32)
        b2t = sb("b2t", [128, NT], F32)
        f3t = sb("f3t", [128, NT], F32)
        b3t = sb("b3t", [128, NT], F32)
        rsh = sb("rsh", [128, NT], F32)
        sw_b = sb("sw_b", [128, 3], F32)
        mh_b = sb("mh_b", [128, 3], F32)
        stats_f = sb("stats_f", [1, 16], F32)
        stats_bf = sb("stats_bf", [1, 16], BF16)
        tmp1 = sb("tmp1", [128, 8], F32)
        tmp2 = sb("tmp2", [128, 8], F32)

        ctxAC = ExitStack()
        ctxAC.__enter__()
        xq_fm = sb("xq_fm", [128, 16 * 1024], BF16, ctxAC)
        ctxAB = ExitStack()
        with ctxAB:
            xn_fm = sb("xn_fm", [128, 16 * 1024], BF16, ctxAB)

            ctxA = ExitStack()
            with ctxA:
                x_sb = sb("x_sb", [128, NT * D], F32, ctxA)
                xn_f = sb("xn_f", [128, D], F32, ctxA)
                xn_b = sb("xn_b", [128, D], BF16, ctxA)
                xq_b = sb("xq_b", [128, D], BF16, ctxA)
                i32b = sb("i32b", [128, D], I32, ctxA)
                strb = sb("strb", [128, 8192], F32, ctxA)
                bnst = sb("bnst", [128, 6], F32, ctxA)
                bnag = sb("bnag", [128, 2], F32, ctxA)
                strp = sb("strp", [128, 3], F32, ctxA)
                ztail = sb("ztail", [1, 1024], BF16, ctxA)
                pT = psum("pT", [128, 128], BF16, ctxA)
                pT2 = psum("pT2", [128, 128], BF16, ctxA)
                pR = psum("pR", [1, 8], F32, ctxA)

                P = Phase(fence_all=FENCE_ALL["A"])
                v_x = P.dma("sync", sL,
                            bass.AP(x_sb, 0, [[NT * D, 128], [D, NT], [1, D]]),
                            xin.ap().rearrange("(t p) d -> p t d", p=128))
                v_tab = P.dma("sync", sL, tab_sb[:, :], tab.ap())
                v_tabm = P.dma("sync", sL, tabm_sb[:, :], tabm.ap())
                v_mask = P.dma("gpsimd", sG,
                               bass.AP(mask_sb, 0, [[4 * 512, 128], [512, 4], [1, 512]]),
                               masks.ap().transpose([1, 0, 2]))
                v_id = P.dma("gpsimd", sG, id_sb[:, :], ident.ap())
                P.step("gpsimd", lambda e, r: e.memset(ones_row[:, :], 1.0))
                g_ones = P.step("gpsimd", lambda e, r: e.memset(onesc[:, :], 1.0),
                                inc=(gg, 1))

                P.step("vector", lambda e, r: e.memset(strp[:, :], 0.0))
                v_red = 0
                first = True
                for (wsrc, j) in [(w1s, 0), (w2s, 1), (w3s, 2)]:
                    for b in range(2):
                        waits = [] if first else [(vv, v_red)]
                        first = False
                        t_l = P.dma("sync", sL, strb[:, :], wsrc.ap()[b], waits=waits)
                        P.step("vector",
                               lambda e, r: e.reduce_sum(tmp1[:, 0:1], strb[:, :],
                                                         axis=AX.X,
                                                         apply_absolute_value=True),
                               waits=[(sL, t_l)], fence=True)
                        v_red = P.step(
                            "vector",
                            (lambda e, r, j=j: e.tensor_add(strp[:, j:j + 1],
                                                            strp[:, j:j + 1],
                                                            tmp1[:, 0:1])),
                            inc=(vv, 1), fence=True)
                t_red = None
                for j in range(3):
                    t_red = P.step(
                        "tensor",
                        (lambda e, r, j=j: e.matmul(pR[0:1, j:j + 1], strp[:, j:j + 1],
                                                    onesc[:, :], start=(j == 0),
                                                    stop=(j == 2),
                                                    skip_group_check=True)),
                        waits=[(vv, v_red), (gg, g_ones)],
                        inc=(tm, 1) if j == 2 else None)
                a_st = P.step("scalar", lambda e, r: e.copy(stats_f[:, 0:3], pR[0:1, 0:3]),
                              waits=[(tm, t_red)], inc=(aa, 1))

                def enc(e, r, F):
                    e.memset(stats_bf[:, :], 0.0)
                    F(e.tensor_copy(stats_bf[:, 0:3], stats_f[:, 0:3]))
                    F(e.tensor_copy(stats_f[:, 4:7], stats_bf[:, 0:3]))
                    F(e.tensor_sub(stats_f[:, 8:11], stats_f[:, 0:3], stats_f[:, 4:7]))
                    F(e.tensor_copy(stats_bf[:, 3:6], stats_f[:, 8:11]))
                    F(e.tensor_copy(stats_f[:, 4:7], stats_bf[:, 3:6]))
                    F(e.tensor_sub(stats_f[:, 8:11], stats_f[:, 8:11],
                                   stats_f[:, 4:7]))
                    return e.tensor_copy(stats_bf[:, 6:9], stats_f[:, 8:11])
                v_enc = P.step("vector", enc, waits=[(aa, a_st)], inc=(vv, 1))
                v_zt = P.step("vector", lambda e, r: e.memset(ztail[:, :], 0.0),
                              inc=(vv, 1), fence=True)
                z_t = 0
                for half in range(2):
                    z_t = P.dma("sync", sL,
                                bass.AP(ag_in, half * ROW + KV_AREA,
                                        [[ROW, 1], [1, 1024]]),
                                ztail[0:1, :], waits=[(vv, v_zt)])
                P.dma("gpsimd", sG,
                      bass.AP(ag_in, KV_AREA, [[ROW, 1], [1, 16]]),
                      stats_bf[0:1, 0:16], waits=[(vv, v_enc), (sL, z_t)])
                g_stats = sG.v

                def newton(e, F, src_ap, eps):
                    F(e.reciprocal(tmp1[:, 1:2], tmp1[:, 0:1]))
                    e.tensor_mul(tmp1[:, 2:3], tmp1[:, 1:2], tmp1[:, 1:2])
                    F(e.tensor_scalar(tmp1[:, 3:4], src_ap, 1.0, float(eps),
                                      OP.mult, OP.add))
                    F(e.tensor_mul(tmp1[:, 2:3], tmp1[:, 2:3], tmp1[:, 3:4]))
                    F(e.tensor_scalar(tmp1[:, 2:3], tmp1[:, 2:3], -0.5, 1.5,
                                      OP.mult, OP.add))
                    return e.tensor_mul(tmp1[:, 1:2], tmp1[:, 1:2], tmp1[:, 2:3])

                a_xn_copy = 0
                v_xq_copy = 0
                t_xnb_used = 0
                t_xqb_used = 0
                a_xq_prev = 0
                for tt in range(NT):
                    xs_ap = x_sb[:, tt * D:(tt + 1) * D]
                    P.step("vector", (lambda e, r, xs=xs_ap: e.tensor_mul(
                        xn_f[:, :], xs, xs)),
                        waits=[(sL, v_x), (tm, t_xnb_used), (aa, a_xq_prev)])
                    P.step("vector", lambda e, r: e.reduce_sum(
                        tmp2[:, 5:6], xn_f[:, :], axis=AX.X), fence=True)
                    P.step("vector", (lambda e, r, xs=xs_ap: e.reduce_sum(
                        tmp1[:, 4:5], xs, axis=AX.X)), fence=True)
                    def vbn(e, r, F):
                        e.tensor_scalar(tmp1[:, 4:5], tmp1[:, 4:5], 1.0 / D, None,
                                        OP.mult)
                        F(e.tensor_scalar(tmp1[:, 5:6], tmp2[:, 5:6], 1.0 / D, None,
                                          OP.mult))
                        F(e.tensor_mul(tmp1[:, 6:7], tmp1[:, 4:5], tmp1[:, 4:5]))
                        return e.tensor_sub(tmp2[:, 0:1], tmp1[:, 5:6], tmp1[:, 6:7])
                    v_bn = P.step("vector", vbn, inc=(vv, 1), fence=True)
                    a1 = P.step("scalar", lambda e, r: e.activation(
                        tmp1[:, 0:1], tmp2[:, 0:1], AF.Sqrt, bias=np.float32(LN_EPS)),
                        waits=[(vv, v_bn)], inc=(aa, 1))
                    P.step("vector", (lambda e, r, F: newton(e, F, tmp2[:, 0:1],
                                                             LN_EPS)),
                           waits=[(aa, a1)])
                    P.step("vector", lambda e, r: e.tensor_scalar(
                        tmp1[:, 4:5], tmp1[:, 4:5], -1.0, None, OP.mult), fence=True)
                    P.step("vector", (lambda e, r, xs=xs_ap: e.tensor_scalar(
                        xn_f[:, :], xs, tmp1[:, 4:5], tmp1[:, 1:2], OP.add, OP.mult)))
                    v_xnb = P.step("vector",
                                   lambda e, r: e.tensor_copy(xn_b[:, :], xn_f[:, :]),
                                   inc=(vv, 1))
                    P.step("vector", (lambda e, r, xs=xs_ap: e.tensor_mul(
                        xs, xn_f[:, :], xn_f[:, :])))
                    P.step("vector", (lambda e, r, xs=xs_ap: e.reduce_sum(
                        tmp2[:, 5:6], xs, axis=AX.X)), fence=True)
                    v_ms = P.step("vector", lambda e, r: e.tensor_scalar(
                        tmp2[:, 0:1], tmp2[:, 5:6], 1.0 / D, None, OP.mult),
                        inc=(vv, 1), fence=True)
                    a2 = P.step("scalar", lambda e, r: e.activation(
                        tmp1[:, 0:1], tmp2[:, 0:1], AF.Sqrt, bias=np.float32(RMS_EPS)),
                        waits=[(vv, v_ms)], inc=(aa, 1))
                    P.step("vector", (lambda e, r, F: newton(e, F, tmp2[:, 0:1],
                                                             RMS_EPS)),
                           waits=[(aa, a2)])
                    P.step("vector", lambda e, r: e.reduce_max(
                        tmp1[:, 6:7], xn_f[:, :], axis=AX.X,
                        apply_absolute_value=True), fence=True)

                    def scales(e, r, F, tt=tt):
                        F(e.tensor_mul(tmp1[:, 7:8], tmp1[:, 1:2], tmp1[:, 6:7]))
                        F(e.tensor_scalar(tmp1[:, 7:8], tmp1[:, 7:8], float(Q_EPS),
                                          None, OP.max))
                        e.tensor_scalar(inv_sx[:, tt:tt + 1], tmp1[:, 7:8], 1.0 / 127.0,
                                        None, OP.mult)
                        F(e.reciprocal(tmp2[:, 1:2], tmp1[:, 7:8]))
                        F(e.tensor_scalar(tmp2[:, 1:2], tmp2[:, 1:2], 127.0, None,
                                          OP.mult))
                        return e.tensor_mul(tmp2[:, 2:3], tmp2[:, 1:2], tmp1[:, 1:2])
                    P.step("vector", scales, fence=True)
                    P.step("vector", lambda e, r: e.tensor_scalar(
                        xn_f[:, :], xn_f[:, :], tmp2[:, 2:3], MAGIC, OP.mult,
                        OP.add), fence=True)
                    v_u = P.step("vector", (lambda e, r, tt=tt: e.tensor_scalar(
                        xn_f[:, :], xn_f[:, :], MAGIC, 0.0, OP.subtract, OP.add,
                        accum_out=rsx[:, tt:tt + 1])), inc=(vv, 1))
                    P.step("scalar", lambda e, r: e.copy(i32b[:, :], xn_f[:, :]),
                           waits=[(vv, v_u), (tm, t_xqb_used)], fence=True)
                    a_xq = P.step("scalar", lambda e, r: e.copy(xq_b[:, :], i32b[:, :]),
                                  inc=(aa, 1))
                    a_xq_prev = a_xq
                    for s_ in range(16):
                        t1 = P.step("tensor", (lambda e, r, s_=s_: e.transpose(
                            pT[:, :], xn_b[:, s_ * 128:(s_ + 1) * 128], id_sb[:, :])),
                            waits=[(vv, v_xnb), (sG, v_id), (aa, a_xn_copy)],
                            inc=(tm, 1))
                        if s_ == 15:
                            t_xnb_used = t1
                        a_xn_copy = P.step("scalar", (lambda e, r, s_=s_, tt=tt: e.copy(
                            xn_fm[:, s_ * 1024 + tt * 128: s_ * 1024 + (tt + 1) * 128],
                            pT[:, :])), waits=[(tm, t1)], inc=(aa, 1))
                        t2 = P.step("tensor", (lambda e, r, s_=s_: e.transpose(
                            pT2[:, :], xq_b[:, s_ * 128:(s_ + 1) * 128], id_sb[:, :])),
                            waits=[(aa, a_xq), (vv, v_xq_copy)], inc=(tm, 1))
                        if s_ == 15:
                            t_xqb_used = t2
                        v_xq_copy = P.step("vector", (lambda e, r, s_=s_, tt=tt:
                            e.tensor_copy(
                                xq_fm[:, s_ * 1024 + tt * 128: s_ * 1024 + (tt + 1) * 128],
                                pT2[:, :])), waits=[(tm, t2)], inc=(vv, 1))
                P.run(nc, sems)

            ctxB = ExitStack()
            with ctxB:
                wbuf = [sb(f"wbuf{i}", [128, 3 * 16 * 128], BF16, ctxB) for i in range(2)]
                kst = [sb(f"kst{i}", [128, H * 512], BF16, ctxB) for i in range(2)]
                vst = [sb(f"vst{i}", [128, H * 4 * 132], BF16, ctxB) for i in range(2)]
                qev = sb("qev", [128, 512], BF16, ctxB)
                vtm = sb("vtm", [128, 512], BF16, ctxB)
                pQ = psum("pQ", [128, 512], F32, ctxB)
                pK = psum("pK", [128, 512], F32, ctxB)
                pV = psum("pV", [128, 512], F32, ctxB)
                pVT = psum("pVT", [128, 128], BF16, ctxB)

                P = Phase(fence_all=FENCE_ALL["B"])
                for half in range(2):
                    P.step("vector", (lambda e, r, half=half: e.memset(
                        bass.AP(vst[half], 128, [[H * 4 * 132, 128], [132, 64], [1, 4]]),
                        1.0)))
                w_done = []
                tmB0 = tm.v
                head_mm_done = [tmB0 + 14 * (h + 1) for h in range(H)]
                for h in range(H):
                    par = h % 2
                    waits = [(tm, head_mm_done[h - 2])] if h >= 2 else []
                    tgt = None
                    for p_ in range(3):
                        tgt = P.dma(
                            "gpsimd", sG,
                            bass.AP(wbuf[par], p_ * 2048,
                                    [[3 * 16 * 128, 128], [128, 16], [1, 128]]),
                            wqkv.ap()[p_, h].rearrange("(s p) c -> p s c", p=128),
                            waits=waits if p_ == 0 else [])
                    w_done.append(tgt)

                a_qe = 0
                v_ke = 0
                v_ve = 0
                t_vt_used = 0
                st_q_last = 0
                for h in range(H):
                    par = h % 2
                    for half in range(2):
                        t_q = None
                        for s_ in range(16):
                            waits = [(sG, w_done[h])]
                            if s_ == 0:
                                waits.append((aa, a_qe))
                            t_q = P.step("tensor", (lambda e, r, s_=s_, half=half, par=par:
                                e.matmul(pQ[:, :],
                                         wbuf[par][:, 0 * 2048 + s_ * 128:(s_ + 1) * 128],
                                         xn_fm[:, s_ * 1024 + half * 512:
                                               s_ * 1024 + half * 512 + 512],
                                         start=(s_ == 0), stop=(s_ == 15))),
                                waits=waits, inc=(tm, 1) if s_ == 15 else None)
                        a_qe = P.step("scalar", lambda e, r: e.copy(qev[:, :], pQ[:, :]),
                                      waits=[(tm, t_q)], inc=(aa, 1))
                        st_q_last = P.dma(
                            "scalar", sS,
                            bass.AP(q_dram, half * 128 * H * 512 + h * 512,
                                    [[H * 512, 128], [1, 512]]),
                            qev[:, :], waits=[(aa, a_qe)])
                        t_k = None
                        for s_ in range(16):
                            waits = [(vv, v_ke)] if s_ == 0 else []
                            t_k = P.step("tensor", (lambda e, r, s_=s_, half=half, par=par:
                                e.matmul(pK[:, :],
                                         wbuf[par][:, 1 * 2048 + s_ * 128:
                                                   1 * 2048 + (s_ + 1) * 128],
                                         xn_fm[:, s_ * 1024 + half * 512:
                                               s_ * 1024 + half * 512 + 512],
                                         start=(s_ == 0), stop=(s_ == 15))),
                                waits=waits, inc=(tm, 1) if s_ == 15 else None)
                        v_ke = P.step("vector", (lambda e, r, h=h, half=half:
                            e.tensor_copy(kst[half][:, h * 512:(h + 1) * 512], pK[:, :])),
                            waits=[(tm, t_k)], inc=(vv, 1))
                        t_v = None
                        for s_ in range(16):
                            waits = [(vv, v_ve)] if s_ == 0 else []
                            t_v = P.step("tensor", (lambda e, r, s_=s_, half=half, par=par:
                                e.matmul(pV[:, :],
                                         wbuf[par][:, 2 * 2048 + s_ * 128:
                                                   2 * 2048 + (s_ + 1) * 128],
                                         xn_fm[:, s_ * 1024 + half * 512:
                                               s_ * 1024 + half * 512 + 512],
                                         start=(s_ == 0), stop=(s_ == 15))),
                                waits=waits, inc=(tm, 1) if s_ == 15 else None)
                        v_vt = P.step("vector", lambda e, r: e.tensor_copy(vtm[:, :],
                                                                           pV[:, :]),
                                      waits=[(tm, t_v), (tm, t_vt_used)], inc=(vv, 1))
                        for kts in range(4):
                            t_tr = P.step("tensor", (lambda e, r, kts=kts: e.transpose(
                                pVT[:, :], vtm[:, kts * 128:(kts + 1) * 128], id_sb[:, :])),
                                waits=[(vv, v_vt), (vv, v_ve)], inc=(tm, 1))
                            if kts == 3:
                                t_vt_used = t_tr
                            v_ve = P.step("vector", (lambda e, r, h=h, half=half, kts=kts:
                                e.tensor_copy(
                                    vst[half][:, (h * 4 + kts) * 132:
                                              (h * 4 + kts) * 132 + 128],
                                    pVT[:, :])), waits=[(tm, t_tr)], inc=(vv, 1))
                    assert tm.v == head_mm_done[h], (h, tm.v, head_mm_done[h])

                st_kv = 0
                for half in range(2):
                    for (srcb, base, width) in [(kst[half], 0, 8192),
                                                (vst[half], 8192, 8448)]:
                        st_kv = P.dma(
                            "scalar", sS,
                            bass.AP(ag_in, half * ROW + base,
                                    [[KVW, 128], [1, width]]),
                            srcb[:, :], waits=[(vv, v_ve), (vv, v_ke)])
                P.step("gpsimd", lambda e, r: e.collective_compute(
                    "AllGather", OP.bypass, replica_groups=[list(range(NCORE))],
                    ins=[ag_in.ap().opt()], outs=[ag_out.ap().opt()]),
                    waits=[(sS, st_kv), (sG, g_stats)], inc=(cc, 1))
                P.run(nc, sems)

        ctxC = ExitStack()
        with ctxC:
            w1st = [sb(f"w1st{i}", [128, 16 * 256], F32, ctxC) for i in range(2)]
            w2st = [sb(f"w2st{i}", [128, 16 * 256], F32, ctxC) for i in range(2)]
            wq1 = [sb(f"wq1_{i}", [128, 16 * 256], BF16, ctxC) for i in range(2)]
            wq2 = [sb(f"wq2_{i}", [128, 16 * 256], BF16, ctxC) for i in range(2)]
            w3st = [sb(f"w3st{i}", [128, D], F32, ctxC) for i in range(2)]
            wq3 = [sb(f"wq3_{i}", [128, D], BF16, ctxC) for i in range(2)]
            wgtb = sb("wgtb", [128, 16 * 256], BF16, ctxC)
            silb = sb("silb", [128, 256], F32, ctxC)
            h2b = sb("h2b", [128, 256], F32, ctxC)
            hbuf = [sb(f"hbuf{i}", [128, 256], F32, ctxC) for i in range(2)]
            gath = sb("gath", [1, 256], F32, ctxC)
            gath_b = sb("gath_b", [1, 128], BF16, ctxC)
            pH1 = [psum(f"pH1_{i}", [128, 256], F32, ctxC) for i in range(2)]
            pH2 = [psum(f"pH2_{i}", [128, 256], F32, ctxC) for i in range(2)]
            pB = psum("pBb", [128, 8], F32, ctxC)

            P = Phase(fence_all=FENCE_ALL["C"])
            g_st = P.dma("gpsimd", sG,
                         bass.AP(gath_b, 0, [[128, 1], [16, 8], [1, 16]]),
                         bass.AP(ag_out, KV_AREA, [[1, 1], [2 * ROW, 8], [1, 16]]),
                         waits=[(cc, 1)])
            P.step("vector", lambda e, r: e.tensor_copy(gath[0:1, 0:128],
                                                        gath_b[0:1, :]),
                   waits=[(sG, g_st)], fence=True)

            def dec(e, r, F):
                last = None
                for j in range(3):
                    F(e.tensor_reduce(gath[0:1, 160:168],
                                      bass.AP(gath, j, [[256, 1], [16, 8], [3, 3]]),
                                      AX.X, OP.add))
                    last = e.reduce_sum(stats_f[0:1, 8 + j:9 + j],
                                        gath[0:1, 160:168], axis=AX.X)
                    if j < 2:
                        F(last)
                return last
            P.step("vector", dec, fence=True)
            NW = [float(DFF * D), float(DFF * D), float(D * DFF)]

            def scl(e, r, F):
                last = None
                for j in range(3):
                    F(e.tensor_scalar(stats_f[0:1, 8 + j:9 + j],
                                      stats_f[0:1, 8 + j:9 + j],
                                      1.0 / NW[j], None, OP.mult))
                    F(e.tensor_scalar(stats_f[0:1, 8 + j:9 + j],
                                      stats_f[0:1, 8 + j:9 + j],
                                      float(Q_EPS), None, OP.max))
                    e.tensor_scalar(stats_f[0:1, 12 + j:13 + j],
                                    stats_f[0:1, 8 + j:9 + j], 0.5, None, OP.mult)
                    last = e.reciprocal(stats_f[0:1, j:j + 1],
                                        stats_f[0:1, 8 + j:9 + j])
                return last
            v_scl = P.step("vector", scl, inc=(vv, 1), fence=True)
            t_bc = None
            for j in range(6):
                so = j if j < 3 else (12 + j - 3)
                t_bc = P.step("tensor", (lambda e, r, j=j, so=so: e.matmul(
                    pB[:, j:j + 1], ones_row[0:1, :], stats_f[0:1, so:so + 1],
                    start=(j == 0), stop=(j == 5), skip_group_check=True)),
                    waits=[(vv, v_scl)], inc=(tm, 1) if j == 5 else None)
            a_bc = P.step("scalar", lambda e, r: (
                e.copy(sw_b[:, 0:3], pB[:, 0:3]),
                e.copy(mh_b[:, 0:3], pB[:, 3:6]))[-1], waits=[(tm, t_bc)],
                inc=(aa, 1), fence=True)

            def ffd(e, r, F):
                e.tensor_scalar(f1t[:, :], inv_sx[:, :], mh_b[:, 0:1], None, OP.mult)
                F(e.tensor_scalar(f2t[:, :], inv_sx[:, :], mh_b[:, 1:2], None,
                                  OP.mult))
                e.tensor_mul(b1t[:, :], rsx[:, :], f1t[:, :])
                F(e.tensor_mul(b2t[:, :], rsx[:, :], f2t[:, :]))
                e.tensor_scalar(b1t[:, :], b1t[:, :], -1.0, None, OP.mult)
                return e.tensor_scalar(b2t[:, :], b2t[:, :], -1.0, None, OP.mult)
            v_ffd = P.step("vector", ffd, waits=[(aa, a_bc)], inc=(vv, 1),
                           fence=True)

            w3q_done = [0] * 64
            w3_tl = [0] * 64
            w3q_store = 0

            def emit_w3_load(P, j3):
                par = j3 % 2
                waits = [(vv, w3q_done[j3 - 2])] if j3 >= 2 else []
                w3_tl[j3] = P.dma("sync", sL, w3st[par][:, :], w3c.ap()[j3],
                                  waits=waits)

            def emit_w3_quant(P, j3):
                nonlocal w3q_store
                par = j3 % 2
                a_s3 = P.step("scalar", (lambda e, r, par=par: e.activation(
                    wq3[par][:, :], w3st[par][:, :], AF.Sign, bias=np.float32(0.5),
                    scale=sw_b[:, 2:3])), waits=[(sL, w3_tl[j3]), (aa, a_bc)],
                    inc=(aa, 1))
                P.step("vector", (lambda e, r, par=par: e.tensor_scalar(
                    wgtb[:, 0:D], w3st[par][:, :], sw_b[:, 2:3], 0.5, OP.mult,
                    OP.is_gt)), waits=[(aa, a_s3)], fence=True)
                w3q_done[j3] = P.step("vector", (lambda e, r, par=par:
                    e.scalar_tensor_tensor(wq3[par][:, :], wgtb[:, 0:D], 2.0,
                                           wq3[par][:, :], OP.mult, OP.add)),
                    inc=(vv, 1))
                w3q_store = P.dma("scalar", sS, w3q_dram.ap()[j3], wq3[par][:, :],
                                  waits=[(vv, w3q_done[j3])])
                if j3 + 2 < 64:
                    emit_w3_load(P, j3 + 2)

            w1done = [0] * 32
            w2done = [0] * 32
            wq_consume = [0] * 32
            h_mm_done = [0] * 32
            evac_done = [0] * (32 * NT)
            h_dma_t = [0] * (32 * NT)
            h_store = 0

            def emit_wload(P, j):
                par = j % 2
                waits = [(vv, wq_consume[j - 2])] if j >= 2 else []
                w1done[j] = P.dma(
                    "sync", sL,
                    bass.AP(w1st[par], 0, [[16 * 256, 128], [256, 16], [1, 256]]),
                    w1c.ap()[j].rearrange("(s p) c -> p s c", p=128), waits=waits)
                w2done[j] = P.dma(
                    "scalar", sS,
                    bass.AP(w2st[par], 0, [[16 * 256, 128], [256, 16], [1, 256]]),
                    w2c.ap()[j].rearrange("(s p) c -> p s c", p=128), waits=waits)

            emit_wload(P, 0)
            emit_wload(P, 1)
            emit_w3_load(P, 0)
            emit_w3_load(P, 1)
            for j in range(32):
                par = j % 2
                waits_q = [(sL, w1done[j]), (aa, a_bc)]
                if j >= 2:
                    waits_q.append((tm, h_mm_done[j - 2]))
                a_s1 = P.step("scalar", (lambda e, r, par=par: e.activation(
                    wq1[par][:, :], w1st[par][:, :], AF.Sign, bias=np.float32(0.5),
                    scale=sw_b[:, 0:1])), waits=waits_q, inc=(aa, 1))
                P.step("vector", (lambda e, r, par=par: e.tensor_scalar(
                    wgtb[:, :], w1st[par][:, :], sw_b[:, 0:1], 0.5, OP.mult,
                    OP.is_gt)), waits=[(aa, a_s1)], fence=True)
                P.step("vector", (lambda e, r, par=par: e.scalar_tensor_tensor(
                    wq1[par][:, :], wgtb[:, :], 2.0, wq1[par][:, :], OP.mult,
                    OP.add)))
                waits_q2 = [(sS, w2done[j]), (aa, a_bc)]
                if j >= 2:
                    waits_q2.append((tm, h_mm_done[j - 2]))
                a_s2 = P.step("scalar", (lambda e, r, par=par: e.activation(
                    wq2[par][:, :], w2st[par][:, :], AF.Sign, bias=np.float32(0.5),
                    scale=sw_b[:, 1:2])), waits=waits_q2, inc=(aa, 1))
                emit_w3_quant(P, 2 * j)
                P.step("vector", (lambda e, r, par=par: e.tensor_scalar(
                    wgtb[:, :], w2st[par][:, :], sw_b[:, 1:2], 0.5, OP.mult,
                    OP.is_gt)), waits=[(aa, a_s2)], fence=True)
                v_qw = P.step("vector", (lambda e, r, par=par: e.scalar_tensor_tensor(
                    wq2[par][:, :], wgtb[:, :], 2.0, wq2[par][:, :], OP.mult,
                    OP.add)), inc=(vv, 1))
                wq_consume[j] = v_qw
                emit_w3_quant(P, 2 * j + 1)
                if j + 2 < 32:
                    emit_wload(P, j + 2)

                t_h = None
                for tt in range(NT):
                    ppar = tt % 2
                    idx = j * NT + tt
                    for s_ in range(16):
                        waits = [(vv, v_qw)]
                        if s_ == 0 and idx >= 2:
                            waits.append((vv, evac_done[idx - 2]))
                        P.step("tensor", (lambda e, r, s_=s_, tt=tt, par=par, ppar=ppar:
                            e.matmul(pH1[ppar][:, :],
                                     xq_fm[:, s_ * 1024 + tt * 128:
                                           s_ * 1024 + (tt + 1) * 128],
                                     wq1[par][:, s_ * 256:(s_ + 1) * 256],
                                     start=(s_ == 0), stop=(s_ == 15))),
                            waits=waits)
                    for s_ in range(16):
                        t_h = P.step("tensor", (lambda e, r, s_=s_, tt=tt, par=par,
                                                ppar=ppar:
                            e.matmul(pH2[ppar][:, :],
                                     xq_fm[:, s_ * 1024 + tt * 128:
                                           s_ * 1024 + (tt + 1) * 128],
                                     wq2[par][:, s_ * 256:(s_ + 1) * 256],
                                     start=(s_ == 0), stop=(s_ == 15))),
                            inc=(tm, 1) if s_ == 15 else None)
                    sil_waits = [(tm, t_h), (vv, v_ffd)]
                    if idx >= 1:
                        sil_waits.append((vv, evac_done[idx - 1]))
                    a_sil = P.step("scalar", (lambda e, r, tt=tt, ppar=ppar:
                        e.activation(silb[:, :], pH1[ppar][:, :], AF.Silu,
                                     bias=b1t[:, tt:tt + 1], scale=f1t[:, tt:tt + 1])),
                        waits=sil_waits, inc=(aa, 1))
                    P.step("vector", (lambda e, r, tt=tt, ppar=ppar: e.tensor_scalar(
                        h2b[:, :], pH2[ppar][:, :], f2t[:, tt:tt + 1], b2t[:, tt:tt + 1],
                        OP.mult, OP.add)), waits=[(tm, t_h), (vv, v_ffd)],
                        fence=True)
                    hb = hbuf[idx % 2]
                    ev_waits = [(aa, a_sil)]
                    if idx >= 2:
                        ev_waits.append((sL, h_dma_t[idx - 2]))
                    evac_done[idx] = P.step("vector", (lambda e, r, hb=hb: e.tensor_mul(
                        hb[:, :], silb[:, :], h2b[:, :])), waits=ev_waits,
                        inc=(vv, 1))
                    h_mm_done[j] = t_h
                    h_dma_t[idx] = P.dma(
                        "sync", sL,
                        bass.AP(h_dram, tt * 128 * DFF + j * 256, [[DFF, 128], [1, 256]]),
                        hb[:, :], waits=[(vv, evac_done[idx])])
                    h_store = h_dma_t[idx]
            P.run(nc, sems)

        ctxAC.close()

        s_sb = sb("s_sb", [128, NT * D], BF16)
        ctxDE = ExitStack()
        ctxDE.__enter__()
        q_stage = sb("q_stage", [128, H * 512], BF16, ctxDE)
        ctxD = ExitStack()
        with ctxD:
            O_all = sb("O_all", [128, 2 * 16 * 528], F32, ctxD)
            kv_sb = [sb(f"kv_sb{i}", [128, KVW], BF16, ctxD) for i in range(2)]
            P_sb = [sb(f"P_sb{i}", [128, 512], BF16, ctxD) for i in range(3)]
            ctxDP = ExitStack()
            ctxDP.__enter__()
            pS = [psum(f"pS{i}", [128, 512], F32, ctxDP) for i in range(3)]
            pO = psum("pO", [128, 2048], F32, ctxDP)

            P = Phase(fence_all=FENCE_ALL["D"])
            P.want_regs("gpsimd", ["kr"])
            P.want_regs("sync", ["qr"])

            v_init = P.step("vector", lambda e, r: e.memset(O_all[:, :], 0.0),
                            inc=(vv, 1))

            tmD0 = tm.v
            pair_mm = [tmD0 + 320 * (i + 1) + (1 if i < NPAIR - 1 else 0)
                       for i in range(NPAIR)]
            kv_done = [0] * NPAIR
            last_exp = [0, 0, 0]
            last_av = [0, 0, 0]
            last_oacc = v_init

            pair_last_st = [tmD0 + 320 * i + 312 for i in range(NPAIR)]
            q_done = [0] * NPAIR
            for i in range(NPAIR):
                par = i % 2
                waits = [(sL, v_tab), (cc, 1)]
                if i >= 2:
                    waits.append((tm, pair_mm[i - 2]))
                P.step("gpsimd", (lambda e, r, i=i: e.reg_load(
                    r["kr"], bass.AP(tab_sb, TAB_K + i, [[128, 1], [1, 1]]))),
                    waits=waits)
                kv_done[i] = P.step("gpsimd", (lambda e, r, par=par: e.dma_start(
                    out=kv_sb[par][:, :],
                    in_=bass.AP(ag_out, r["kr"], [[KVW, 128], [1, KVW]]))),
                    inc=(sG, 16))
            for i in range(NPAIR):
                q_waits = [(sL, v_tab), (sS, st_q_last), (cc, 1)]
                if i >= 1:
                    q_waits.append((tm, pair_last_st[i - 1]))
                if i < 2:
                    q_done[i] = P.dma("sync", sL, q_stage[:, :], q_dram.ap()[i],
                                      waits=q_waits)
                else:
                    P.step("sync", (lambda e, r, i=i: e.reg_load(
                        r["qr"], bass.AP(tab_sb, TAB_Q + i, [[128, 1], [1, 1]]))),
                        waits=q_waits)
                    q_done[i] = P.step("sync", (lambda e, r: e.dma_start(
                        out=q_stage[:, :],
                        in_=bass.AP(q_dram, r["qr"], [[H * 512, 128], [1, H * 512]]))),
                        inc=(sL, 16))

            tiles = [(i, h, kts) for i in range(NPAIR)
                     for h in range(H) for kts in range(4)]
            NTILE = len(tiles)
            exp_of = [0] * NTILE
            for g in range(NTILE + 1):
                if g < NTILE:
                    i, h, kts = tiles[g]
                    par = i % 2
                    b = g % 3
                    waits = [(sG, kv_done[i]), (sL, q_done[i]),
                             (aa, last_exp[b]), (vv, v_init)]
                    t_st = P.step("tensor", (lambda e, r, par=par, h=h, kts=kts, b=b:
                        e.matmul(pS[b][:, :],
                                 kv_sb[par][:, h * 512 + kts * 128:
                                            h * 512 + (kts + 1) * 128],
                                 q_stage[:, h * 512:(h + 1) * 512],
                                 start=True, stop=True)),
                        waits=waits, inc=(tm, 1))
                    if h == 15 and kts == 3:
                        assert tm.v == pair_last_st[i], (i, tm.v, pair_last_st[i])
                    if i < 2:
                        v_m = P.step("vector", (lambda e, r, b=b, kts=kts:
                            e.tensor_tensor(pS[b][:, :], pS[b][:, :],
                                            mask_sb[:, kts * 512:(kts + 1) * 512],
                                            OP.add)),
                            waits=[(tm, t_st), (sG, v_mask)], inc=(vv, 1))
                        exp_waits = [(vv, v_m)]
                    else:
                        exp_waits = [(tm, t_st)]
                    exp_waits.append((tm, last_av[b]))
                    a_e = P.step("scalar", (lambda e, r, b=b: e.activation(
                        P_sb[b][:, :], pS[b][:, :], AF.Exp, bias=EXPC,
                        scale=SC_QK)), waits=exp_waits, inc=(aa, 1),
                        fence=DEXP)
                    last_exp[b] = a_e
                    exp_of[g] = a_e
                if g >= 1:
                    i2, h2, kts2 = tiles[g - 1]
                    par2 = i2 % 2
                    b2 = (g - 1) % 3
                    av_waits = [(aa, exp_of[g - 1])]
                    if kts2 == 0:
                        av_waits.append((vv, last_oacc))
                    t_av = None
                    for qs in range(4):
                        t_av = P.step("tensor", (lambda e, r, b=b2, qs=qs, par=par2,
                                                 h=h2, kts=kts2:
                            e.matmul(pO[:, qs * 512:qs * 512 + 129],
                                     P_sb[b][:, qs * 128:(qs + 1) * 128],
                                     kv_sb[par][:, 8192 + (h * 4 + kts) * 132:
                                                8192 + (h * 4 + kts) * 132 + 129],
                                     start=(kts == 0), stop=(kts == 3),
                                     skip_group_check=True)),
                            waits=av_waits if qs == 0 else [], inc=(tm, 1))
                    last_av[b2] = t_av
                    if kts2 == 3:
                        pO_ap = bass.AP(pO, 0, [[2048, 128], [512, 4], [1, 129]])
                        for half in range(2):
                            oap = bass.AP(O_all, half * 8448 + h2 * 528,
                                          [[2 * 16 * 528, 128], [132, 4], [1, 129]])
                            st_ = P.step("vector", (lambda e, r, oap=oap, i=i2,
                                                    half=half:
                                e.scalar_tensor_tensor(
                                    oap, pO_ap,
                                    tabm_sb[:, 2 * i + half:2 * i + half + 1],
                                    oap, OP.mult, OP.add)),
                                waits=[(tm, last_av[0]), (tm, last_av[1]),
                                       (tm, last_av[2]), (sL, v_tabm)],
                                inc=(vv, 1) if half == 1 else None, fence=DSTT)
                        last_oacc = st_
                        if h2 == 15:
                            assert tm.v == pair_mm[i2], (i2, tm.v, pair_mm[i2])
            P.run(nc, sems)
            ctxDP.close()

            o_fm = kv_sb[0]
            pE1ctx = ExitStack()
            with pE1ctx:
                o_nrm = [sb(f"o_nrm{i}", [128, 128], BF16, pE1ctx) for i in range(2)]
                pOT = [psum(f"pOT{i}", [128, 128], BF16, pE1ctx) for i in range(2)]
                P = Phase(fence_all=FENCE_ALL["E1"])
                P.step("vector", lambda e, r: e.reciprocal(
                    bass.AP(O_all, 128, [[2 * 16 * 528, 128], [132, 128], [1, 1]]),
                    bass.AP(O_all, 128, [[2 * 16 * 528, 128], [132, 128], [1, 1]])),
                    fence=True)
                last_tr = [0, 0]
                last_cp = [0, 0]
                for h in range(H):
                    for qt in range(8):
                        b = (h * 8 + qt) % 2
                        off = (qt // 4) * 8448 + h * 528 + (qt % 4) * 132
                        v_n = P.step("vector", (lambda e, r, off=off, b=b:
                            e.tensor_scalar(o_nrm[b][:, :],
                                            O_all[:, off:off + 128],
                                            O_all[:, off + 128:off + 129],
                                            None, OP.mult)),
                            waits=[(tm, last_tr[b])], inc=(vv, 1), fence=E1VN)
                        t_t = P.step("tensor", (lambda e, r, b=b: e.transpose(
                            pOT[b][:, :], o_nrm[b][:, :], id_sb[:, :])),
                            waits=[(vv, v_n), (aa, last_cp[b])], inc=(tm, 1))
                        last_tr[b] = t_t
                        off2 = h * 1024 + qt * 128
                        last_cp[b] = P.step("scalar", (lambda e, r, off2=off2, b=b:
                            e.copy(o_fm[:, off2:off2 + 128], pOT[b][:, :])),
                            waits=[(tm, t_t)], inc=(aa, 1), fence=E1CP)
                P.run(nc, sems)

            ctxE = ExitStack()
            with ctxE:
                woc = [sb(f"woc{i}", [128, 512], BF16, ctxE) for i in range(2)]
                pSP = [psum(f"pSP{i}", [128, 512], F32, ctxE) for i in range(8)]
                P = Phase(fence_all=FENCE_ALL["E"])
                v_se = [0] * 8
                wo_last = [0, 0]
                for dc in range(4):
                    t_s = [None] * 8
                    for s_ in range(16):
                        wpar = s_ % 2
                        wo_waits = [(tm, wo_last[wpar])] if (dc > 0 or s_ >= 2) else []
                        wo_done = P.dma(
                            "gpsimd", sG, woc[wpar][:, :],
                            bass.AP(woutT, s_ * 128 * D + dc * 512,
                                    [[D, 128], [1, 512]]),
                            waits=wo_waits)
                        for qt in range(8):
                            waits = [(sG, wo_done)]
                            if s_ == 0:
                                waits.append((vv, v_se[qt]))
                            t_s[qt] = P.step("tensor", (lambda e, r, s_=s_, qt=qt,
                                                        dc=dc, wpar=wpar:
                                e.matmul(pSP[qt][:, :],
                                         o_fm[:, s_ * 1024 + qt * 128:
                                              s_ * 1024 + (qt + 1) * 128],
                                         woc[wpar][:, :],
                                         start=(s_ == 0), stop=(s_ == 15))),
                                waits=waits,
                                inc=(tm, 1) if qt == 7 else None)
                        wo_last[wpar] = t_s[7]
                    for qt in range(8):
                        v_se[qt] = P.step("vector", (lambda e, r, qt=qt, dc=dc:
                            e.tensor_copy(s_sb[:, qt * D + dc * 512:
                                               qt * D + (dc + 1) * 512],
                                          pSP[qt][:, :])),
                            waits=[(tm, t_s[7])], inc=(vv, 1))
                P.run(nc, sems)

        ctxDE.close()

        ctxF = ExitStack()
        with ctxF:
            hst = [sb(f"hst{i}", [128, DFF], F32, ctxF) for i in range(2)]
            hq_fm = sb("hq_fm", [128, 64 * 512], BF16, ctxF)
            hq_tm = sb("hq_tm", [128, 2048], BF16, ctxF)
            i32f = sb("i32f", [128, 2048], I32, ctxF)
            w3t = [sb(f"w3t{i}", [128, 4 * 512], BF16, ctxF) for i in range(2)]
            g_b = sb("g_b", [128, 512], BF16, ctxF)
            xre = [sb(f"xre{i}", [128, 512], F32, ctxF) for i in range(2)]
            ob = [sb(f"ob{i}", [128, 512], F32, ctxF) for i in range(2)]
            hsq = sb("hsq", [128, 2048], F32, ctxF)
            acc4 = sb("acc4", [128, 4], F32, ctxF)
            pG = [psum(f"pG{i}", [128, 512], F32, ctxF) for i in range(4)]
            pFT = [psum(f"pFT{i}", [128, 128], BF16, ctxF) for i in range(2)]

            P = Phase(fence_all=FENCE_ALL["F"])

            def newtonF(e, src_ap, eps):
                e.reciprocal(tmp1[:, 1:2], tmp1[:, 0:1])
                e.tensor_mul(tmp1[:, 2:3], tmp1[:, 1:2], tmp1[:, 1:2])
                e.tensor_scalar(tmp1[:, 3:4], src_ap, 1.0, float(eps), OP.mult, OP.add)
                e.tensor_mul(tmp1[:, 2:3], tmp1[:, 2:3], tmp1[:, 3:4])
                e.tensor_scalar(tmp1[:, 2:3], tmp1[:, 2:3], -0.5, 1.5, OP.mult, OP.add)
                return e.tensor_mul(tmp1[:, 1:2], tmp1[:, 1:2], tmp1[:, 2:3])

            h_loaded = [0] * NT
            h_loaded[0] = P.dma("sync", sL, hst[0][:, :], h_dram.ap()[0],
                                waits=[(sL, h_store)])
            h_loaded[1] = P.dma("sync", sL, hst[1][:, :], h_dram.ap()[1])

            t_trF_used = [0, 0]
            a_cpF = [0, 0]
            storm_done = 0
            st_out = 0
            x_loaded = {}
            g_done = [0] * 4
            w3t_last = [0, 0]
            last_vo = 0
            last_vo_bo = [0, 0]
            st_out_bo = [0, 0]
            a_hq = 0
            for tg in range(2):
                for tl in range(4):
                    tt = tg * 4 + tl
                    par = tt % 2
                    for q4 in range(4):
                        sl = slice(q4 * 2048, (q4 + 1) * 2048)
                        P.step("vector", (lambda e, r, par=par, sl=sl:
                            e.tensor_mul(hsq[:, :], hst[par][:, sl], hst[par][:, sl])),
                            waits=[(sL, h_loaded[tt])])
                        P.step("vector", (lambda e, r, q4=q4: e.reduce_sum(
                            acc4[:, q4:q4 + 1], hsq[:, :], axis=AX.X)), fence=True)
                    def vbnF(e, r, F):
                        F(e.reduce_sum(tmp2[:, 5:6], acc4[:, :], axis=AX.X))
                        return e.tensor_scalar(tmp2[:, 0:1], tmp2[:, 5:6], 1.0 / DFF,
                                               None, OP.mult)
                    v_bn = P.step("vector", vbnF, inc=(vv, 1), fence=True)
                    a1 = P.step("scalar", lambda e, r: e.activation(
                        tmp1[:, 0:1], tmp2[:, 0:1], AF.Sqrt, bias=np.float32(RMS_EPS)),
                        waits=[(vv, v_bn)], inc=(aa, 1))
                    P.step("vector", (lambda e, r: newtonF(e, tmp2[:, 0:1], RMS_EPS)),
                           waits=[(aa, a1)])
                    P.step("vector", (lambda e, r, par=par: e.reduce_max(
                        tmp1[:, 6:7], hst[par][:, :], axis=AX.X,
                        apply_absolute_value=True)), fence=True)

                    def scalesF(e, r, F, tt=tt):
                        F(e.tensor_mul(tmp1[:, 7:8], tmp1[:, 1:2], tmp1[:, 6:7]))
                        F(e.tensor_scalar(tmp1[:, 7:8], tmp1[:, 7:8], float(Q_EPS),
                                          None, OP.max))
                        F(e.tensor_scalar(tmp2[:, 3:4], tmp1[:, 7:8], 1.0 / 127.0,
                                          None, OP.mult))
                        e.tensor_scalar(f3t[:, tt:tt + 1], tmp2[:, 3:4], mh_b[:, 2:3],
                                        None, OP.mult)
                        F(e.reciprocal(tmp2[:, 1:2], tmp1[:, 7:8]))
                        F(e.tensor_scalar(tmp2[:, 1:2], tmp2[:, 1:2], 127.0, None,
                                          OP.mult))
                        return e.tensor_mul(tmp2[:, 2:3], tmp2[:, 1:2], tmp1[:, 1:2])
                    P.step("vector", scalesF, fence=True)
                    for q4 in range(4):
                        sl = slice(q4 * 2048, (q4 + 1) * 2048)
                        P.step("vector", (lambda e, r, par=par, sl=sl: e.tensor_scalar(
                            hst[par][:, sl], hst[par][:, sl], tmp2[:, 2:3], MAGIC,
                            OP.mult, OP.add)), fence=True)
                        v_u = P.step("vector", (lambda e, r, par=par, sl=sl, q4=q4:
                            e.tensor_scalar(hst[par][:, sl], hst[par][:, sl], MAGIC,
                                            0.0, OP.subtract, OP.add,
                                            accum_out=acc4[:, q4:q4 + 1])),
                            inc=(vv, 1))
                        P.step("scalar", (lambda e, r, par=par, sl=sl: e.copy(
                            i32f[:, :], hst[par][:, sl])),
                            waits=[(vv, v_u), (tm, t_trF_used[0]),
                                   (tm, t_trF_used[1])], fence=True)
                        a_hq = P.step("scalar", lambda e, r: e.copy(hq_tm[:, :],
                                                                    i32f[:, :]),
                                      inc=(aa, 1))
                        for s2 in range(16):
                            ds = q4 * 16 + s2
                            btr = ds % 2
                            t_t = P.step("tensor", (lambda e, r, s2=s2, btr=btr:
                                e.transpose(pFT[btr][:, :],
                                            hq_tm[:, s2 * 128:(s2 + 1) * 128],
                                            id_sb[:, :])),
                                waits=[(aa, a_hq), (vv, a_cpF[btr]),
                                       (tm, storm_done)], inc=(tm, 1))
                            t_trF_used[btr] = t_t
                            a_cpF[btr] = P.step("vector", (lambda e, r, ds=ds, tl=tl,
                                                           btr=btr:
                                e.tensor_copy(hq_fm[:, ds * 512 + tl * 128:
                                                    ds * 512 + (tl + 1) * 128],
                                              pFT[btr][:, :])),
                                waits=[(tm, t_t)], inc=(vv, 1), fence=True)
                    def vrsF(e, r, F, tt=tt):
                        F(e.reduce_sum(rsh[:, tt:tt + 1], acc4[:, :], axis=AX.X))
                        F(e.tensor_mul(tmp2[:, 4:5], rsh[:, tt:tt + 1],
                                       f3t[:, tt:tt + 1]))
                        return e.tensor_scalar(b3t[:, tt:tt + 1], tmp2[:, 4:5], -1.0,
                                               None, OP.mult)
                    v_rs = P.step("vector", vrsF, inc=(vv, 1), fence=True)
                    nxt = tt + 2
                    if nxt < NT:
                        h_loaded[nxt] = P.dma("sync", sL, hst[par][:, :],
                                              h_dram.ap()[nxt],
                                              waits=[(vv, v_rs), (aa, a_hq)])
                hq_ready = vv.v

                for dq in range(4):
                    w3_done = [0] * 16
                    for s4 in range(2):
                        w3_done[s4] = P.dma(
                            "scalar", sS,
                            bass.AP(w3t[s4 % 2], 0, [[4 * 512, 128], [512, 4],
                                                     [1, 512]]),
                            bass.AP(w3q_dram, s4 * 4 * 128 * D + dq * 512,
                                    [[D, 128], [128 * D, 4], [1, 512]]),
                            waits=[(sS, w3q_store), (tm, w3t_last[s4 % 2])])
                    ds_mm = [0] * 64
                    for s4 in range(16):
                        wpar = s4 % 2
                        for dsl in range(4):
                            ds = s4 * 4 + dsl
                            for tl in range(4):
                                waits = [(sS, w3_done[s4]), (vv, hq_ready)]
                                if ds == 0:
                                    waits.append((aa, g_done[tl]))
                                ds_mm[ds] = P.step("tensor", (lambda e, r, ds=ds,
                                                              tl=tl, wpar=wpar,
                                                              dsl=dsl:
                                    e.matmul(pG[tl][:, :],
                                             hq_fm[:, ds * 512 + tl * 128:
                                                   ds * 512 + (tl + 1) * 128],
                                             w3t[wpar][:, dsl * 512:
                                                       (dsl + 1) * 512],
                                             start=(ds == 0), stop=(ds == 63))),
                                    waits=waits,
                                    inc=(tm, 1) if tl == 3 else None)
                        w3t_last[wpar] = ds_mm[s4 * 4 + 3]
                        if s4 + 2 < 16:
                            w3_done[s4 + 2] = P.dma(
                                "scalar", sS,
                                bass.AP(w3t[wpar], 0, [[4 * 512, 128], [512, 4],
                                                       [1, 512]]),
                                bass.AP(w3q_dram, (s4 + 2) * 4 * 128 * D + dq * 512,
                                        [[D, 128], [128 * D, 4], [1, 512]]),
                                waits=[(tm, ds_mm[s4 * 4 + 3])])
                    storm_done = ds_mm[63]
                    for tl in range(4):
                        tt = tg * 4 + tl
                        bo = tl % 2
                        xk = (tg, dq, tl)
                        x_loaded[xk] = P.dma(
                            "sync", sL, xre[bo][:, :],
                            bass.AP(xin, (tt * 128) * D + dq * 512,
                                    [[D, 128], [1, 512]]),
                            waits=[(vv, last_vo_bo[bo])])
                        a_g = P.step("scalar", (lambda e, r, tl=tl, tt=tt:
                            e.activation(g_b[:, :], pG[tl][:, :], AF.Silu,
                                         bias=b3t[:, tt:tt + 1],
                                         scale=f3t[:, tt:tt + 1])),
                            waits=[(tm, storm_done), (vv, last_vo)], inc=(aa, 1),
                            fence=True)
                        g_done[tl] = a_g
                        def vcomb(e, r, F, tt=tt, dq=dq, bo=bo):
                            F(e.tensor_mul(g_b[:, :],
                                           s_sb[:, tt * D + dq * 512:
                                                tt * D + (dq + 1) * 512],
                                           g_b[:, :]))
                            return e.tensor_add(ob[bo][:, :], xre[bo][:, :],
                                                g_b[:, :])
                        v_o = P.step("vector", vcomb,
                            waits=[(aa, a_g), (sL, x_loaded[xk]),
                                   (sL, st_out_bo[bo])], inc=(vv, 1), fence=True)
                        last_vo = v_o
                        last_vo_bo[bo] = v_o
                        st_out_bo[bo] = P.dma(
                            "sync", sL,
                            bass.AP(out, (tt * 128) * D + dq * 512,
                                    [[D, 128], [1, 512]]),
                            ob[bo][:, :], waits=[(vv, v_o)])
                        st_out = st_out_bo[bo]
            P.step("sync", None, waits=[(sL, st_out)])
            P.run(nc, sems)

    return nc



def _host_prep(L0_attn_x, L0_attn_w_in, L0_attn_w_out, L0_ffn_w1, L0_ffn_w2,
               L0_ffn_w3):
    c = np.ascontiguousarray
    x = np.asarray(L0_attn_x, np.float32).reshape(S, D)
    w_in = np.asarray(L0_attn_w_in, np.float32)
    w_out = np.asarray(L0_attn_w_out, np.float32)
    w1 = np.asarray(L0_ffn_w1, np.float32)
    w2 = np.asarray(L0_ffn_w2, np.float32)
    w3 = np.asarray(L0_ffn_w3, np.float32)

    wqkv = c(w_in.reshape(H, 3, DH, D).transpose(1, 0, 3, 2))
    woutT = c(w_out.T.reshape(16, 128, D))
    w1c = c(w1.T.reshape(D, 32, 256).transpose(1, 0, 2))
    w2c = c(w2.T.reshape(D, 32, 256).transpose(1, 0, 2))
    w3c = c(w3.T.reshape(64, 128, D))

    kk = np.arange(512).reshape(4, 128)
    qq = np.arange(512)
    masks = np.where(kk[:, :, None] <= qq[None, None, :], 0.0, MASKVAL
                     ).astype(np.float32)
    ident = np.eye(128, dtype=np.float32)

    shared = dict(wqkv=wqkv, woutT=woutT, w1c=w1c, w2c=w2c, w3c=w3c,
                  masks=masks, ident=ident)

    def rowof(g):
        return 2 * g if g <= 7 else 2 * (15 - g) + 1

    in_maps = []
    for core in range(NCORE):
        chA, chB = core, 15 - core
        xc = c(np.concatenate([x[CH * chA:CH * (chA + 1)],
                               x[CH * chB:CH * (chB + 1)]], axis=0))
        w1s_ = c(w1[1024 * core:1024 * (core + 1)].reshape(2, 128, 8192))
        w2s_ = c(w2[1024 * core:1024 * (core + 1)].reshape(2, 128, 8192))
        w3s_ = c(w3[256 * core:256 * (core + 1)].reshape(2, 128, 8192))
        pairs = [(0, chA), (1, chB)]
        pairs += [(0, j) for j in range(chA)]
        pairs += [(1, j) for j in range(chB)]
        assert len(pairs) == NPAIR
        tabv = np.zeros(128, np.int32)
        tabmv = np.zeros((128, 64), np.float32)
        for i, (qsel, kch) in enumerate(pairs):
            tabv[TAB_K + i] = rowof(kch) * ROW
            tabv[TAB_Q + i] = qsel * (128 * H * 512)
            tabmv[:, 2 * i + qsel] = 1.0
        in_maps.append(dict(shared, xin=xc, w1s=w1s_, w2s=w2s_, w3s=w3s_,
                            tabi=tabv.reshape(1, 128), tabm=tabmv))
    return in_maps


def kernel(L0_attn_x, L0_attn_w_in, L0_attn_w_out, L0_ffn_w1, L0_ffn_w2,
           L0_ffn_w3):
    in_maps = _host_prep(L0_attn_x, L0_attn_w_in, L0_attn_w_out, L0_ffn_w1,
                         L0_ffn_w2, L0_ffn_w3)
    nc = build_program()
    res = run_bass_kernel_spmd(nc, in_maps, core_ids=list(range(NCORE)))
    full = np.zeros((1, S, D), np.float32)
    for core in range(NCORE):
        o = np.asarray(res.results[core]["outt"], np.float32)
        chA, chB = core, 15 - core
        full[0, CH * chA:CH * (chA + 1)] = o[0:CH]
        full[0, CH * chB:CH * (chB + 1)] = o[CH:2 * CH]
    return full


if __name__ == "__main__":
    nc = build_program()
    print("build OK; instructions:", nc.n_instructions())


# version 22
# speedup vs baseline: 1.1643x; 1.1427x over previous
from contextlib import ExitStack

import numpy as np
import concourse.bass as bass
import concourse.mybir as mybir
from concourse.bass_utils import run_bass_kernel_spmd

F32 = mybir.dt.float32
BF16 = mybir.dt.bfloat16
I32 = mybir.dt.int32
AF = mybir.ActivationFunctionType
OP = mybir.AluOpType
AX = mybir.AxisListType

NCORE = 8
S = 8192
D = 2048
H = 16
DH = 128
DFF = 8192
T = 1024
NT = 8
CH = 512

KVW = 8192 + 8448
KV_AREA = 128 * KVW
STATS = 1024
ROW = KV_AREA + STATS

NPAIR = 17
TAB_K, TAB_V, TAB_Q, TAB_O = 0, 17, 34, 51

MAGIC = 12582912.0
SC_QK = np.float32(1.0 / np.sqrt(128.0))
EXPC = np.float32(-15.0)
MASKVAL = -1.0e6
LN_EPS = 1.0e-5
RMS_EPS = 1.0e-6
Q_EPS = 1.0e-5

ENGS = ["sync", "gpsimd", "tensor", "vector", "scalar"]


class Sem:
    __slots__ = ("name", "v")

    def __init__(self, name):
        self.name = name
        self.v = 0

    def inc(self, n=1):
        self.v += n
        return self.v


FCNT = {e: 0 for e in ENGS}
FSEM = {"vector": "Fv", "scalar": "Fa", "gpsimd": "Fg", "tensor": "Ft",
        "sync": "Fs"}
FENCE_ALL = {"A": False, "B": False, "C": False, "D": False, "E1": False,
             "E": True, "F": True}
E1VN = True
E1CP = False
DEXP = False
DSTT = True


class Phase:

    def __init__(self, fence_all=False):
        self.fence_all = fence_all
        self.plan = {e: [] for e in ENGS}
        self.regs = {e: [] for e in ENGS}

    def want_regs(self, eng, names):
        self.regs[eng] = list(names)

    def step(self, eng, act, waits=(), inc=None, fence=False):
        if self.fence_all and eng in ("vector", "scalar"):
            fence = True
        inc_val = inc[0].inc(inc[1]) if inc is not None else None
        self.plan[eng].append((list(waits), act, inc, fence, inc_val))
        return inc_val

    def dma(self, eng, sem, out_ap, in_ap, waits=(), **kw):
        return self.step(
            eng,
            (lambda e, r, o=out_ap, i=in_ap, kw=kw: e.dma_start(out=o, in_=i, **kw)),
            waits=waits, inc=(sem, 16), fence=False)

    def run(self, nc, sems):
        def mk(engname):
            steps = self.plan[engname]
            regnames = self.regs[engname]
            fsem = sems[FSEM[engname]]

            def closure(eng):
                def F(inst):
                    FCNT[engname] += 1
                    inst.then_inc(fsem, 1)
                    eng.wait_ge(fsem, FCNT[engname])
                    return inst

                with ExitStack() as rs:
                    regs = {n: rs.enter_context(eng.register(n)) for n in regnames}
                    last = {}
                    for si, (waits, act, inc, fence, inc_val) in enumerate(steps):
                        for s, v in waits:
                            if v > 0 and last.get(s.name, -1) < v:
                                eng.wait_ge(sems[s.name], v)
                                last[s.name] = v
                        try:
                            if act is None:
                                inst = None
                            elif (act.__code__.co_argcount >= 3
                                  and act.__code__.co_varnames[2] == "F"):
                                inst = act(eng, regs, F)
                            else:
                                inst = act(eng, regs)
                        except Exception:
                            import traceback as _tb
                            print(f"STEP FAIL eng={engname} idx={si}")
                            _tb.print_exc()
                            raise
                        if inc is not None:
                            inst.then_inc(sems[inc[0].name], inc[1])
                            if fence:
                                eng.wait_ge(sems[inc[0].name], inc_val)
                        elif fence and inst is not None:
                            F(inst)
            return closure

        with nc.Block() as block:
            for engname in ENGS:
                if self.plan[engname]:
                    getattr(block, engname)(mk(engname))


def build_program():
    for k in FCNT:
        FCNT[k] = 0
    nc = bass.Bass("TRN2", target_bir_lowering=False, detect_race_conditions=False)

    xin = nc.dram_tensor("xin", [T, D], F32, kind="ExternalInput")
    wqkv = nc.dram_tensor("wqkv", [3, H, D, DH], F32, kind="ExternalInput")
    woutT = nc.dram_tensor("woutT", [16, 128, D], F32, kind="ExternalInput")
    w1c = nc.dram_tensor("w1c", [32, D, 256], F32, kind="ExternalInput")
    w2c = nc.dram_tensor("w2c", [32, D, 256], F32, kind="ExternalInput")
    w3c = nc.dram_tensor("w3c", [64, 128, D], F32, kind="ExternalInput")
    w1s = nc.dram_tensor("w1s", [2, 128, 8192], F32, kind="ExternalInput")
    w2s = nc.dram_tensor("w2s", [2, 128, 8192], F32, kind="ExternalInput")
    w3s = nc.dram_tensor("w3s", [2, 128, 8192], F32, kind="ExternalInput")
    masks = nc.dram_tensor("masks", [4, 128, 512], F32, kind="ExternalInput")
    ident = nc.dram_tensor("ident", [128, 128], F32, kind="ExternalInput")
    tab = nc.dram_tensor("tabi", [1, 128], I32, kind="ExternalInput")
    tabm = nc.dram_tensor("tabm", [128, 64], F32, kind="ExternalInput")
    out = nc.dram_tensor("outt", [T, D], F32, kind="ExternalOutput")

    ag_in = nc.dram_tensor("ag_in", [2, ROW], BF16, kind="Internal")
    ag_out = nc.dram_tensor("ag_out", [16, ROW], BF16, kind="Internal")
    q_dram = nc.dram_tensor("q_dram", [2, 128, H * 512], BF16, kind="Internal")
    h_dram = nc.dram_tensor("h_dram", [NT, 128, DFF], F32, kind="Internal")
    w3q_dram = nc.dram_tensor("w3q_dram", [64, 128, D], BF16, kind="Internal")

    sems = {}
    S_ = {}
    gctx = ExitStack()
    with gctx:
        for nm in ["sL", "sS", "sG", "cc", "tm", "vv", "aa", "gg",
                   "Fv", "Fa", "Fg", "Ft", "Fs"]:
            sems[nm] = gctx.enter_context(nc.semaphore(nm))
            S_[nm] = Sem(nm)
        sL, sS, sG, cc, tm, vv, aa, gg = (
            S_[k] for k in ["sL", "sS", "sG", "cc", "tm", "vv", "aa", "gg"])

        def sb(name, shape, dtype=F32, ctx=gctx):
            return ctx.enter_context(nc.sbuf_tensor(name, shape, dtype))

        def psum(name, shape, dtype=F32, ctx=gctx):
            return ctx.enter_context(nc.psum_tensor(name, shape, dtype))

        tab_sb = sb("tab_sb", [1, 128], I32)
        tabm_sb = sb("tabm_sb", [128, 64], F32)
        mask_sb = sb("mask_sb", [128, 4 * 512], BF16)
        id_sb = sb("id_sb", [128, 128], BF16)
        ones_row = sb("ones_row", [1, 128], F32)
        onesc = sb("onesc", [128, 1], F32)
        rsx = sb("rsx", [128, NT], F32)
        inv_sx = sb("inv_sx", [128, NT], F32)
        f1t = sb("f1t", [128, NT], F32)
        f2t = sb("f2t", [128, NT], F32)
        b1t = sb("b1t", [128, NT], F32)
        b2t = sb("b2t", [128, NT], F32)
        f3t = sb("f3t", [128, NT], F32)
        b3t = sb("b3t", [128, NT], F32)
        rsh = sb("rsh", [128, NT], F32)
        sw_b = sb("sw_b", [128, 3], F32)
        mh_b = sb("mh_b", [128, 3], F32)
        stats_f = sb("stats_f", [1, 16], F32)
        stats_bf = sb("stats_bf", [1, 16], BF16)
        tmp1 = sb("tmp1", [128, 8], F32)
        tmp2 = sb("tmp2", [128, 8], F32)

        ctxAC = ExitStack()
        ctxAC.__enter__()
        xq_fm = sb("xq_fm", [128, 16 * 1024], BF16, ctxAC)
        ctxAB = ExitStack()
        with ctxAB:
            xn_fm = sb("xn_fm", [128, 16 * 1024], BF16, ctxAB)

            ctxA = ExitStack()
            with ctxA:
                x_sb = sb("x_sb", [128, NT * D], F32, ctxA)
                xn_f = sb("xn_f", [128, D], F32, ctxA)
                xn_b = sb("xn_b", [128, D], BF16, ctxA)
                xq_b = sb("xq_b", [128, D], BF16, ctxA)
                i32b = sb("i32b", [128, D], I32, ctxA)
                strb = sb("strb", [128, 8192], F32, ctxA)
                bnst = sb("bnst", [128, 6], F32, ctxA)
                bnag = sb("bnag", [128, 2], F32, ctxA)
                strp = sb("strp", [128, 3], F32, ctxA)
                ztail = sb("ztail", [1, 1024], BF16, ctxA)
                pT = psum("pT", [128, 128], BF16, ctxA)
                pT2 = psum("pT2", [128, 128], BF16, ctxA)
                pR = psum("pR", [1, 8], F32, ctxA)

                P = Phase(fence_all=FENCE_ALL["A"])
                v_x = P.dma("sync", sL,
                            bass.AP(x_sb, 0, [[NT * D, 128], [D, NT], [1, D]]),
                            xin.ap().rearrange("(t p) d -> p t d", p=128))
                v_tab = P.dma("sync", sL, tab_sb[:, :], tab.ap())
                v_tabm = P.dma("sync", sL, tabm_sb[:, :], tabm.ap())
                v_mask = P.dma("gpsimd", sG,
                               bass.AP(mask_sb, 0, [[4 * 512, 128], [512, 4], [1, 512]]),
                               masks.ap().transpose([1, 0, 2]))
                v_id = P.dma("gpsimd", sG, id_sb[:, :], ident.ap())
                P.step("gpsimd", lambda e, r: e.memset(ones_row[:, :], 1.0))
                g_ones = P.step("gpsimd", lambda e, r: e.memset(onesc[:, :], 1.0),
                                inc=(gg, 1))

                P.step("vector", lambda e, r: e.memset(strp[:, :], 0.0))
                v_red = 0
                first = True
                for (wsrc, j) in [(w1s, 0), (w2s, 1), (w3s, 2)]:
                    for b in range(2):
                        waits = [] if first else [(vv, v_red)]
                        first = False
                        t_l = P.dma("sync", sL, strb[:, :], wsrc.ap()[b], waits=waits)
                        P.step("vector",
                               lambda e, r: e.reduce_sum(tmp1[:, 0:1], strb[:, :],
                                                         axis=AX.X,
                                                         apply_absolute_value=True),
                               waits=[(sL, t_l)], fence=True)
                        v_red = P.step(
                            "vector",
                            (lambda e, r, j=j: e.tensor_add(strp[:, j:j + 1],
                                                            strp[:, j:j + 1],
                                                            tmp1[:, 0:1])),
                            inc=(vv, 1), fence=True)
                t_red = None
                for j in range(3):
                    t_red = P.step(
                        "tensor",
                        (lambda e, r, j=j: e.matmul(pR[0:1, j:j + 1], strp[:, j:j + 1],
                                                    onesc[:, :], start=(j == 0),
                                                    stop=(j == 2),
                                                    skip_group_check=True)),
                        waits=[(vv, v_red), (gg, g_ones)],
                        inc=(tm, 1) if j == 2 else None)
                a_st = P.step("scalar", lambda e, r: e.copy(stats_f[:, 0:3], pR[0:1, 0:3]),
                              waits=[(tm, t_red)], inc=(aa, 1))

                def enc(e, r, F):
                    e.memset(stats_bf[:, :], 0.0)
                    F(e.tensor_copy(stats_bf[:, 0:3], stats_f[:, 0:3]))
                    F(e.tensor_copy(stats_f[:, 4:7], stats_bf[:, 0:3]))
                    F(e.tensor_sub(stats_f[:, 8:11], stats_f[:, 0:3], stats_f[:, 4:7]))
                    F(e.tensor_copy(stats_bf[:, 3:6], stats_f[:, 8:11]))
                    F(e.tensor_copy(stats_f[:, 4:7], stats_bf[:, 3:6]))
                    F(e.tensor_sub(stats_f[:, 8:11], stats_f[:, 8:11],
                                   stats_f[:, 4:7]))
                    return e.tensor_copy(stats_bf[:, 6:9], stats_f[:, 8:11])
                v_enc = P.step("vector", enc, waits=[(aa, a_st)], inc=(vv, 1))
                v_zt = P.step("vector", lambda e, r: e.memset(ztail[:, :], 0.0),
                              inc=(vv, 1), fence=True)
                z_t = 0
                for half in range(2):
                    z_t = P.dma("sync", sL,
                                bass.AP(ag_in, half * ROW + KV_AREA,
                                        [[ROW, 1], [1, 1024]]),
                                ztail[0:1, :], waits=[(vv, v_zt)])
                P.dma("gpsimd", sG,
                      bass.AP(ag_in, KV_AREA, [[ROW, 1], [1, 16]]),
                      stats_bf[0:1, 0:16], waits=[(vv, v_enc), (sL, z_t)])
                g_stats = sG.v

                def newton(e, F, src_ap, eps):
                    F(e.reciprocal(tmp1[:, 1:2], tmp1[:, 0:1]))
                    e.tensor_mul(tmp1[:, 2:3], tmp1[:, 1:2], tmp1[:, 1:2])
                    F(e.tensor_scalar(tmp1[:, 3:4], src_ap, 1.0, float(eps),
                                      OP.mult, OP.add))
                    F(e.tensor_mul(tmp1[:, 2:3], tmp1[:, 2:3], tmp1[:, 3:4]))
                    F(e.tensor_scalar(tmp1[:, 2:3], tmp1[:, 2:3], -0.5, 1.5,
                                      OP.mult, OP.add))
                    return e.tensor_mul(tmp1[:, 1:2], tmp1[:, 1:2], tmp1[:, 2:3])

                a_xn_copy = 0
                v_xq_copy = 0
                t_xnb_used = 0
                t_xqb_used = 0
                a_xq_prev = 0
                for tt in range(NT):
                    xs_ap = x_sb[:, tt * D:(tt + 1) * D]
                    P.step("vector", (lambda e, r, xs=xs_ap: e.tensor_mul(
                        xn_f[:, :], xs, xs)),
                        waits=[(sL, v_x), (tm, t_xnb_used), (aa, a_xq_prev)])
                    P.step("vector", lambda e, r: e.reduce_sum(
                        tmp2[:, 5:6], xn_f[:, :], axis=AX.X), fence=True)
                    P.step("vector", (lambda e, r, xs=xs_ap: e.reduce_sum(
                        tmp1[:, 4:5], xs, axis=AX.X)), fence=True)
                    def vbn(e, r, F):
                        e.tensor_scalar(tmp1[:, 4:5], tmp1[:, 4:5], 1.0 / D, None,
                                        OP.mult)
                        F(e.tensor_scalar(tmp1[:, 5:6], tmp2[:, 5:6], 1.0 / D, None,
                                          OP.mult))
                        F(e.tensor_mul(tmp1[:, 6:7], tmp1[:, 4:5], tmp1[:, 4:5]))
                        return e.tensor_sub(tmp2[:, 0:1], tmp1[:, 5:6], tmp1[:, 6:7])
                    v_bn = P.step("vector", vbn, inc=(vv, 1), fence=True)
                    a1 = P.step("scalar", lambda e, r: e.activation(
                        tmp1[:, 0:1], tmp2[:, 0:1], AF.Sqrt, bias=np.float32(LN_EPS)),
                        waits=[(vv, v_bn)], inc=(aa, 1))
                    P.step("vector", (lambda e, r, F: newton(e, F, tmp2[:, 0:1],
                                                             LN_EPS)),
                           waits=[(aa, a1)])
                    P.step("vector", lambda e, r: e.tensor_scalar(
                        tmp1[:, 4:5], tmp1[:, 4:5], -1.0, None, OP.mult), fence=True)
                    P.step("vector", (lambda e, r, xs=xs_ap: e.tensor_scalar(
                        xn_f[:, :], xs, tmp1[:, 4:5], tmp1[:, 1:2], OP.add, OP.mult)))
                    v_xnb = P.step("vector",
                                   lambda e, r: e.tensor_copy(xn_b[:, :], xn_f[:, :]),
                                   inc=(vv, 1))
                    P.step("vector", (lambda e, r, xs=xs_ap: e.tensor_mul(
                        xs, xn_f[:, :], xn_f[:, :])))
                    P.step("vector", (lambda e, r, xs=xs_ap: e.reduce_sum(
                        tmp2[:, 5:6], xs, axis=AX.X)), fence=True)
                    v_ms = P.step("vector", lambda e, r: e.tensor_scalar(
                        tmp2[:, 0:1], tmp2[:, 5:6], 1.0 / D, None, OP.mult),
                        inc=(vv, 1), fence=True)
                    a2 = P.step("scalar", lambda e, r: e.activation(
                        tmp1[:, 0:1], tmp2[:, 0:1], AF.Sqrt, bias=np.float32(RMS_EPS)),
                        waits=[(vv, v_ms)], inc=(aa, 1))
                    P.step("vector", (lambda e, r, F: newton(e, F, tmp2[:, 0:1],
                                                             RMS_EPS)),
                           waits=[(aa, a2)])
                    P.step("vector", lambda e, r: e.reduce_max(
                        tmp1[:, 6:7], xn_f[:, :], axis=AX.X,
                        apply_absolute_value=True), fence=True)

                    def scales(e, r, F, tt=tt):
                        F(e.tensor_mul(tmp1[:, 7:8], tmp1[:, 1:2], tmp1[:, 6:7]))
                        F(e.tensor_scalar(tmp1[:, 7:8], tmp1[:, 7:8], float(Q_EPS),
                                          None, OP.max))
                        e.tensor_scalar(inv_sx[:, tt:tt + 1], tmp1[:, 7:8], 1.0 / 127.0,
                                        None, OP.mult)
                        F(e.reciprocal(tmp2[:, 1:2], tmp1[:, 7:8]))
                        F(e.tensor_scalar(tmp2[:, 1:2], tmp2[:, 1:2], 127.0, None,
                                          OP.mult))
                        return e.tensor_mul(tmp2[:, 2:3], tmp2[:, 1:2], tmp1[:, 1:2])
                    P.step("vector", scales, fence=True)
                    P.step("vector", lambda e, r: e.tensor_scalar(
                        xn_f[:, :], xn_f[:, :], tmp2[:, 2:3], MAGIC, OP.mult,
                        OP.add), fence=True)
                    v_u = P.step("vector", (lambda e, r, tt=tt: e.tensor_scalar(
                        xn_f[:, :], xn_f[:, :], MAGIC, 0.0, OP.subtract, OP.add,
                        accum_out=rsx[:, tt:tt + 1])), inc=(vv, 1))
                    P.step("scalar", lambda e, r: e.copy(i32b[:, :], xn_f[:, :]),
                           waits=[(vv, v_u), (tm, t_xqb_used)], fence=True)
                    a_xq = P.step("scalar", lambda e, r: e.copy(xq_b[:, :], i32b[:, :]),
                                  inc=(aa, 1))
                    a_xq_prev = a_xq
                    for s_ in range(16):
                        t1 = P.step("tensor", (lambda e, r, s_=s_: e.transpose(
                            pT[:, :], xn_b[:, s_ * 128:(s_ + 1) * 128], id_sb[:, :])),
                            waits=[(vv, v_xnb), (sG, v_id), (aa, a_xn_copy)],
                            inc=(tm, 1))
                        if s_ == 15:
                            t_xnb_used = t1
                        a_xn_copy = P.step("scalar", (lambda e, r, s_=s_, tt=tt: e.copy(
                            xn_fm[:, s_ * 1024 + tt * 128: s_ * 1024 + (tt + 1) * 128],
                            pT[:, :])), waits=[(tm, t1)], inc=(aa, 1))
                        t2 = P.step("tensor", (lambda e, r, s_=s_: e.transpose(
                            pT2[:, :], xq_b[:, s_ * 128:(s_ + 1) * 128], id_sb[:, :])),
                            waits=[(aa, a_xq), (vv, v_xq_copy)], inc=(tm, 1))
                        if s_ == 15:
                            t_xqb_used = t2
                        v_xq_copy = P.step("vector", (lambda e, r, s_=s_, tt=tt:
                            e.tensor_copy(
                                xq_fm[:, s_ * 1024 + tt * 128: s_ * 1024 + (tt + 1) * 128],
                                pT2[:, :])), waits=[(tm, t2)], inc=(vv, 1))
                P.run(nc, sems)

            ctxB = ExitStack()
            with ctxB:
                wbuf = [sb(f"wbuf{i}", [128, 3 * 16 * 128], BF16, ctxB) for i in range(2)]
                kst = [sb(f"kst{i}", [128, H * 512], BF16, ctxB) for i in range(2)]
                vst = [sb(f"vst{i}", [128, H * 4 * 132], BF16, ctxB) for i in range(2)]
                qev = sb("qev", [128, 512], BF16, ctxB)
                vtm = sb("vtm", [128, 512], BF16, ctxB)
                pQ = psum("pQ", [128, 512], F32, ctxB)
                pK = psum("pK", [128, 512], F32, ctxB)
                pV = psum("pV", [128, 512], F32, ctxB)
                pVT = psum("pVT", [128, 128], BF16, ctxB)

                P = Phase(fence_all=FENCE_ALL["B"])
                for half in range(2):
                    P.step("vector", (lambda e, r, half=half: e.memset(
                        bass.AP(vst[half], 128, [[H * 4 * 132, 128], [132, 64], [1, 4]]),
                        1.0)))
                w_done = []
                tmB0 = tm.v
                head_mm_done = [tmB0 + 14 * (h + 1) for h in range(H)]
                for h in range(H):
                    par = h % 2
                    waits = [(tm, head_mm_done[h - 2])] if h >= 2 else []
                    tgt = None
                    for p_ in range(3):
                        tgt = P.dma(
                            "gpsimd", sG,
                            bass.AP(wbuf[par], p_ * 2048,
                                    [[3 * 16 * 128, 128], [128, 16], [1, 128]]),
                            wqkv.ap()[p_, h].rearrange("(s p) c -> p s c", p=128),
                            waits=waits if p_ == 0 else [])
                    w_done.append(tgt)

                a_qe = 0
                v_ke = 0
                v_ve = 0
                t_vt_used = 0
                st_q_last = 0
                for h in range(H):
                    par = h % 2
                    for half in range(2):
                        t_q = None
                        for s_ in range(16):
                            waits = [(sG, w_done[h])]
                            if s_ == 0:
                                waits.append((aa, a_qe))
                            t_q = P.step("tensor", (lambda e, r, s_=s_, half=half, par=par:
                                e.matmul(pQ[:, :],
                                         wbuf[par][:, 0 * 2048 + s_ * 128:(s_ + 1) * 128],
                                         xn_fm[:, s_ * 1024 + half * 512:
                                               s_ * 1024 + half * 512 + 512],
                                         start=(s_ == 0), stop=(s_ == 15))),
                                waits=waits, inc=(tm, 1) if s_ == 15 else None)
                        a_qe = P.step("scalar", lambda e, r: e.copy(qev[:, :], pQ[:, :]),
                                      waits=[(tm, t_q)], inc=(aa, 1))
                        st_q_last = P.dma(
                            "scalar", sS,
                            bass.AP(q_dram, half * 128 * H * 512 + h * 512,
                                    [[H * 512, 128], [1, 512]]),
                            qev[:, :], waits=[(aa, a_qe)])
                        t_k = None
                        for s_ in range(16):
                            waits = [(vv, v_ke)] if s_ == 0 else []
                            t_k = P.step("tensor", (lambda e, r, s_=s_, half=half, par=par:
                                e.matmul(pK[:, :],
                                         wbuf[par][:, 1 * 2048 + s_ * 128:
                                                   1 * 2048 + (s_ + 1) * 128],
                                         xn_fm[:, s_ * 1024 + half * 512:
                                               s_ * 1024 + half * 512 + 512],
                                         start=(s_ == 0), stop=(s_ == 15))),
                                waits=waits, inc=(tm, 1) if s_ == 15 else None)
                        v_ke = P.step("vector", (lambda e, r, h=h, half=half:
                            e.tensor_copy(kst[half][:, h * 512:(h + 1) * 512], pK[:, :])),
                            waits=[(tm, t_k)], inc=(vv, 1))
                        t_v = None
                        for s_ in range(16):
                            waits = [(vv, v_ve)] if s_ == 0 else []
                            t_v = P.step("tensor", (lambda e, r, s_=s_, half=half, par=par:
                                e.matmul(pV[:, :],
                                         wbuf[par][:, 2 * 2048 + s_ * 128:
                                                   2 * 2048 + (s_ + 1) * 128],
                                         xn_fm[:, s_ * 1024 + half * 512:
                                               s_ * 1024 + half * 512 + 512],
                                         start=(s_ == 0), stop=(s_ == 15))),
                                waits=waits, inc=(tm, 1) if s_ == 15 else None)
                        v_vt = P.step("vector", lambda e, r: e.tensor_copy(vtm[:, :],
                                                                           pV[:, :]),
                                      waits=[(tm, t_v), (tm, t_vt_used)], inc=(vv, 1))
                        for kts in range(4):
                            t_tr = P.step("tensor", (lambda e, r, kts=kts: e.transpose(
                                pVT[:, :], vtm[:, kts * 128:(kts + 1) * 128], id_sb[:, :])),
                                waits=[(vv, v_vt), (vv, v_ve)], inc=(tm, 1))
                            if kts == 3:
                                t_vt_used = t_tr
                            v_ve = P.step("vector", (lambda e, r, h=h, half=half, kts=kts:
                                e.tensor_copy(
                                    vst[half][:, (h * 4 + kts) * 132:
                                              (h * 4 + kts) * 132 + 128],
                                    pVT[:, :])), waits=[(tm, t_tr)], inc=(vv, 1))
                    assert tm.v == head_mm_done[h], (h, tm.v, head_mm_done[h])

                st_kv = 0
                for half in range(2):
                    for (srcb, base, width) in [(kst[half], 0, 8192),
                                                (vst[half], 8192, 8448)]:
                        st_kv = P.dma(
                            "scalar", sS,
                            bass.AP(ag_in, half * ROW + base,
                                    [[KVW, 128], [1, width]]),
                            srcb[:, :], waits=[(vv, v_ve), (vv, v_ke)])
                P.step("gpsimd", lambda e, r: e.collective_compute(
                    "AllGather", OP.bypass, replica_groups=[list(range(NCORE))],
                    ins=[ag_in.ap().opt()], outs=[ag_out.ap().opt()]),
                    waits=[(sS, st_kv), (sG, g_stats)], inc=(cc, 1))
                P.run(nc, sems)

        ctxC = ExitStack()
        with ctxC:
            w1st = [sb(f"w1st{i}", [128, 16 * 256], F32, ctxC) for i in range(2)]
            w2st = [sb(f"w2st{i}", [128, 16 * 256], F32, ctxC) for i in range(2)]
            wq1 = [sb(f"wq1_{i}", [128, 16 * 256], BF16, ctxC) for i in range(2)]
            wq2 = [sb(f"wq2_{i}", [128, 16 * 256], BF16, ctxC) for i in range(2)]
            w3st = [sb(f"w3st{i}", [128, D], F32, ctxC) for i in range(2)]
            wq3 = [sb(f"wq3_{i}", [128, D], BF16, ctxC) for i in range(2)]
            wgtb = sb("wgtb", [128, 16 * 256], BF16, ctxC)
            silb = [sb(f"silb{i}", [128, 256], F32, ctxC) for i in range(2)]
            h2b = sb("h2b", [128, 256], F32, ctxC)
            hbuf = [sb(f"hbuf{i}", [128, 256], F32, ctxC) for i in range(2)]
            gath = sb("gath", [1, 256], F32, ctxC)
            gath_b = sb("gath_b", [1, 128], BF16, ctxC)
            pH1 = [psum(f"pH1_{i}", [128, 256], F32, ctxC) for i in range(3)]
            pH2 = [psum(f"pH2_{i}", [128, 256], F32, ctxC) for i in range(3)]
            pB = psum("pBb", [128, 8], F32, ctxC)

            P = Phase(fence_all=FENCE_ALL["C"])
            g_st = P.dma("gpsimd", sG,
                         bass.AP(gath_b, 0, [[128, 1], [16, 8], [1, 16]]),
                         bass.AP(ag_out, KV_AREA, [[1, 1], [2 * ROW, 8], [1, 16]]),
                         waits=[(cc, 1)])
            P.step("vector", lambda e, r: e.tensor_copy(gath[0:1, 0:128],
                                                        gath_b[0:1, :]),
                   waits=[(sG, g_st)], fence=True)

            def dec(e, r, F):
                last = None
                for j in range(3):
                    F(e.tensor_reduce(gath[0:1, 160:168],
                                      bass.AP(gath, j, [[256, 1], [16, 8], [3, 3]]),
                                      AX.X, OP.add))
                    last = e.reduce_sum(stats_f[0:1, 8 + j:9 + j],
                                        gath[0:1, 160:168], axis=AX.X)
                    if j < 2:
                        F(last)
                return last
            P.step("vector", dec, fence=True)
            NW = [float(DFF * D), float(DFF * D), float(D * DFF)]

            def scl(e, r, F):
                last = None
                for j in range(3):
                    F(e.tensor_scalar(stats_f[0:1, 8 + j:9 + j],
                                      stats_f[0:1, 8 + j:9 + j],
                                      1.0 / NW[j], None, OP.mult))
                    F(e.tensor_scalar(stats_f[0:1, 8 + j:9 + j],
                                      stats_f[0:1, 8 + j:9 + j],
                                      float(Q_EPS), None, OP.max))
                    e.tensor_scalar(stats_f[0:1, 12 + j:13 + j],
                                    stats_f[0:1, 8 + j:9 + j], 0.5, None, OP.mult)
                    last = e.reciprocal(stats_f[0:1, j:j + 1],
                                        stats_f[0:1, 8 + j:9 + j])
                return last
            v_scl = P.step("vector", scl, inc=(vv, 1), fence=True)
            t_bc = None
            for j in range(6):
                so = j if j < 3 else (12 + j - 3)
                t_bc = P.step("tensor", (lambda e, r, j=j, so=so: e.matmul(
                    pB[:, j:j + 1], ones_row[0:1, :], stats_f[0:1, so:so + 1],
                    start=(j == 0), stop=(j == 5), skip_group_check=True)),
                    waits=[(vv, v_scl)], inc=(tm, 1) if j == 5 else None)
            a_bc = P.step("scalar", lambda e, r: (
                e.copy(sw_b[:, 0:3], pB[:, 0:3]),
                e.copy(mh_b[:, 0:3], pB[:, 3:6]))[-1], waits=[(tm, t_bc)],
                inc=(aa, 1), fence=True)

            def ffd(e, r, F):
                e.tensor_scalar(f1t[:, :], inv_sx[:, :], mh_b[:, 0:1], None, OP.mult)
                F(e.tensor_scalar(f2t[:, :], inv_sx[:, :], mh_b[:, 1:2], None,
                                  OP.mult))
                e.tensor_mul(b1t[:, :], rsx[:, :], f1t[:, :])
                F(e.tensor_mul(b2t[:, :], rsx[:, :], f2t[:, :]))
                e.tensor_scalar(b1t[:, :], b1t[:, :], -1.0, None, OP.mult)
                return e.tensor_scalar(b2t[:, :], b2t[:, :], -1.0, None, OP.mult)
            v_ffd = P.step("vector", ffd, waits=[(aa, a_bc)], inc=(vv, 1),
                           fence=True)

            w3q_done = [0] * 64
            w3_tl = [0] * 64
            w3q_store = 0

            def emit_w3_load(P, j3):
                par = j3 % 2
                waits = [(vv, w3q_done[j3 - 2])] if j3 >= 2 else []
                w3_tl[j3] = P.dma("sync", sL, w3st[par][:, :], w3c.ap()[j3],
                                  waits=waits)

            def emit_w3_quant(P, j3):
                nonlocal w3q_store
                par = j3 % 2
                a_s3 = P.step("scalar", (lambda e, r, par=par: e.activation(
                    wq3[par][:, :], w3st[par][:, :], AF.Sign, bias=np.float32(0.5),
                    scale=sw_b[:, 2:3])), waits=[(sL, w3_tl[j3]), (aa, a_bc)],
                    inc=(aa, 1))
                P.step("vector", (lambda e, r, par=par: e.tensor_scalar(
                    wgtb[:, 0:D], w3st[par][:, :], sw_b[:, 2:3], 0.5, OP.mult,
                    OP.is_gt)), waits=[(aa, a_s3)], fence=True)
                w3q_done[j3] = P.step("vector", (lambda e, r, par=par:
                    e.scalar_tensor_tensor(wq3[par][:, :], wgtb[:, 0:D], 2.0,
                                           wq3[par][:, :], OP.mult, OP.add)),
                    inc=(vv, 1))
                w3q_store = P.dma("scalar", sS, w3q_dram.ap()[j3], wq3[par][:, :],
                                  waits=[(vv, w3q_done[j3])])
                if j3 + 2 < 64:
                    emit_w3_load(P, j3 + 2)

            w1done = [0] * 32
            w2done = [0] * 32
            wq_consume = [0] * 32
            h_mm_done = [0] * 32
            evac_done = [0] * (32 * NT)
            h_dma_t = [0] * (32 * NT)
            h_store = 0

            def emit_wload(P, j):
                par = j % 2
                waits = [(vv, wq_consume[j - 2])] if j >= 2 else []
                w1done[j] = P.dma(
                    "sync", sL,
                    bass.AP(w1st[par], 0, [[16 * 256, 128], [256, 16], [1, 256]]),
                    w1c.ap()[j].rearrange("(s p) c -> p s c", p=128), waits=waits)
                w2done[j] = P.dma(
                    "scalar", sS,
                    bass.AP(w2st[par], 0, [[16 * 256, 128], [256, 16], [1, 256]]),
                    w2c.ap()[j].rearrange("(s p) c -> p s c", p=128), waits=waits)

            emit_wload(P, 0)
            emit_wload(P, 1)
            emit_w3_load(P, 0)
            emit_w3_load(P, 1)
            for j in range(32):
                par = j % 2
                waits_q = [(sL, w1done[j]), (aa, a_bc)]
                if j >= 2:
                    waits_q.append((tm, h_mm_done[j - 2]))
                a_s1 = P.step("scalar", (lambda e, r, par=par: e.activation(
                    wq1[par][:, :], w1st[par][:, :], AF.Sign, bias=np.float32(0.5),
                    scale=sw_b[:, 0:1])), waits=waits_q, inc=(aa, 1))
                P.step("vector", (lambda e, r, par=par: e.tensor_scalar(
                    wgtb[:, :], w1st[par][:, :], sw_b[:, 0:1], 0.5, OP.mult,
                    OP.is_gt)), waits=[(aa, a_s1)], fence=True)
                P.step("vector", (lambda e, r, par=par: e.scalar_tensor_tensor(
                    wq1[par][:, :], wgtb[:, :], 2.0, wq1[par][:, :], OP.mult,
                    OP.add)))
                waits_q2 = [(sS, w2done[j]), (aa, a_bc)]
                if j >= 2:
                    waits_q2.append((tm, h_mm_done[j - 2]))
                a_s2 = P.step("scalar", (lambda e, r, par=par: e.activation(
                    wq2[par][:, :], w2st[par][:, :], AF.Sign, bias=np.float32(0.5),
                    scale=sw_b[:, 1:2])), waits=waits_q2, inc=(aa, 1))
                emit_w3_quant(P, 2 * j)
                P.step("vector", (lambda e, r, par=par: e.tensor_scalar(
                    wgtb[:, :], w2st[par][:, :], sw_b[:, 1:2], 0.5, OP.mult,
                    OP.is_gt)), waits=[(aa, a_s2)], fence=True)
                v_qw = P.step("vector", (lambda e, r, par=par: e.scalar_tensor_tensor(
                    wq2[par][:, :], wgtb[:, :], 2.0, wq2[par][:, :], OP.mult,
                    OP.add)), inc=(vv, 1))
                wq_consume[j] = v_qw
                emit_w3_quant(P, 2 * j + 1)
                if j + 2 < 32:
                    emit_wload(P, j + 2)

                t_h = None
                for tt in range(NT):
                    idx = j * NT + tt
                    ppar = idx % 3
                    for s_ in range(16):
                        waits = [(vv, v_qw)]
                        if s_ == 0 and idx >= 3:
                            waits.append((vv, evac_done[idx - 3]))
                        P.step("tensor", (lambda e, r, s_=s_, tt=tt, par=par, ppar=ppar:
                            e.matmul(pH1[ppar][:, :],
                                     xq_fm[:, s_ * 1024 + tt * 128:
                                           s_ * 1024 + (tt + 1) * 128],
                                     wq1[par][:, s_ * 256:(s_ + 1) * 256],
                                     start=(s_ == 0), stop=(s_ == 15))),
                            waits=waits)
                    for s_ in range(16):
                        t_h = P.step("tensor", (lambda e, r, s_=s_, tt=tt, par=par,
                                                ppar=ppar:
                            e.matmul(pH2[ppar][:, :],
                                     xq_fm[:, s_ * 1024 + tt * 128:
                                           s_ * 1024 + (tt + 1) * 128],
                                     wq2[par][:, s_ * 256:(s_ + 1) * 256],
                                     start=(s_ == 0), stop=(s_ == 15))),
                            inc=(tm, 1) if s_ == 15 else None)
                    sil_waits = [(tm, t_h), (vv, v_ffd)]
                    if idx >= 2:
                        sil_waits.append((vv, evac_done[idx - 2]))
                    spar = idx % 2
                    a_sil = P.step("scalar", (lambda e, r, tt=tt, ppar=ppar, spar=spar:
                        e.activation(silb[spar][:, :], pH1[ppar][:, :], AF.Silu,
                                     bias=b1t[:, tt:tt + 1], scale=f1t[:, tt:tt + 1])),
                        waits=sil_waits, inc=(aa, 1))
                    P.step("vector", (lambda e, r, tt=tt, ppar=ppar: e.tensor_scalar(
                        h2b[:, :], pH2[ppar][:, :], f2t[:, tt:tt + 1], b2t[:, tt:tt + 1],
                        OP.mult, OP.add)), waits=[(tm, t_h), (vv, v_ffd)],
                        fence=True)
                    hb = hbuf[idx % 2]
                    ev_waits = [(aa, a_sil)]
                    if idx >= 2:
                        ev_waits.append((sL, h_dma_t[idx - 2]))
                    evac_done[idx] = P.step("vector", (lambda e, r, hb=hb, spar=spar:
                        e.tensor_mul(
                        hb[:, :], silb[spar][:, :], h2b[:, :])), waits=ev_waits,
                        inc=(vv, 1))
                    h_mm_done[j] = t_h
                    h_dma_t[idx] = P.dma(
                        "sync", sL,
                        bass.AP(h_dram, tt * 128 * DFF + j * 256, [[DFF, 128], [1, 256]]),
                        hb[:, :], waits=[(vv, evac_done[idx])])
                    h_store = h_dma_t[idx]
            P.run(nc, sems)

        ctxAC.close()

        s_sb = sb("s_sb", [128, NT * D], BF16)
        ctxDE = ExitStack()
        ctxDE.__enter__()
        q_stage = sb("q_stage", [128, H * 512], BF16, ctxDE)
        ctxD = ExitStack()
        with ctxD:
            O_all = sb("O_all", [128, 2 * 16 * 528], F32, ctxD)
            kv_sb = [sb(f"kv_sb{i}", [128, KVW], BF16, ctxD) for i in range(2)]
            P_sb = [sb(f"P_sb{i}", [128, 512], BF16, ctxD) for i in range(3)]
            ctxDP = ExitStack()
            ctxDP.__enter__()
            pS = [psum(f"pS{i}", [128, 512], F32, ctxDP) for i in range(3)]
            pO = psum("pO", [128, 2048], F32, ctxDP)

            P = Phase(fence_all=FENCE_ALL["D"])
            P.want_regs("gpsimd", ["kr"])
            P.want_regs("sync", ["qr"])

            v_init = P.step("vector", lambda e, r: e.memset(O_all[:, :], 0.0),
                            inc=(vv, 1))

            tmD0 = tm.v
            pair_mm = [tmD0 + 320 * (i + 1) + (1 if i < NPAIR - 1 else 0)
                       for i in range(NPAIR)]
            kv_done = [0] * NPAIR
            last_exp = [0, 0, 0]
            last_av = [0, 0, 0]
            last_oacc = v_init

            pair_last_st = [tmD0 + 320 * i + 312 for i in range(NPAIR)]
            q_done = [0] * NPAIR
            for i in range(NPAIR):
                par = i % 2
                waits = [(sL, v_tab), (cc, 1)]
                if i >= 2:
                    waits.append((tm, pair_mm[i - 2]))
                P.step("gpsimd", (lambda e, r, i=i: e.reg_load(
                    r["kr"], bass.AP(tab_sb, TAB_K + i, [[128, 1], [1, 1]]))),
                    waits=waits)
                kv_done[i] = P.step("gpsimd", (lambda e, r, par=par: e.dma_start(
                    out=kv_sb[par][:, :],
                    in_=bass.AP(ag_out, r["kr"], [[KVW, 128], [1, KVW]]))),
                    inc=(sG, 16))
            for i in range(NPAIR):
                q_waits = [(sL, v_tab), (sS, st_q_last), (cc, 1)]
                if i >= 1:
                    q_waits.append((tm, pair_last_st[i - 1]))
                if i < 2:
                    q_done[i] = P.dma("sync", sL, q_stage[:, :], q_dram.ap()[i],
                                      waits=q_waits)
                else:
                    P.step("sync", (lambda e, r, i=i: e.reg_load(
                        r["qr"], bass.AP(tab_sb, TAB_Q + i, [[128, 1], [1, 1]]))),
                        waits=q_waits)
                    q_done[i] = P.step("sync", (lambda e, r: e.dma_start(
                        out=q_stage[:, :],
                        in_=bass.AP(q_dram, r["qr"], [[H * 512, 128], [1, H * 512]]))),
                        inc=(sL, 16))

            tiles = [(i, h, kts) for i in range(NPAIR)
                     for h in range(H) for kts in range(4)]
            NTILE = len(tiles)
            exp_of = [0] * NTILE
            for g in range(NTILE + 1):
                if g < NTILE:
                    i, h, kts = tiles[g]
                    par = i % 2
                    b = g % 3
                    waits = [(sG, kv_done[i]), (sL, q_done[i]),
                             (aa, last_exp[b]), (vv, v_init)]
                    t_st = P.step("tensor", (lambda e, r, par=par, h=h, kts=kts, b=b:
                        e.matmul(pS[b][:, :],
                                 kv_sb[par][:, h * 512 + kts * 128:
                                            h * 512 + (kts + 1) * 128],
                                 q_stage[:, h * 512:(h + 1) * 512],
                                 start=True, stop=True)),
                        waits=waits, inc=(tm, 1))
                    if h == 15 and kts == 3:
                        assert tm.v == pair_last_st[i], (i, tm.v, pair_last_st[i])
                    if i < 2:
                        v_m = P.step("vector", (lambda e, r, b=b, kts=kts:
                            e.tensor_tensor(pS[b][:, :], pS[b][:, :],
                                            mask_sb[:, kts * 512:(kts + 1) * 512],
                                            OP.add)),
                            waits=[(tm, t_st), (sG, v_mask)], inc=(vv, 1))
                        exp_waits = [(vv, v_m)]
                    else:
                        exp_waits = [(tm, t_st)]
                    exp_waits.append((tm, last_av[b]))
                    a_e = P.step("scalar", (lambda e, r, b=b: e.activation(
                        P_sb[b][:, :], pS[b][:, :], AF.Exp, bias=EXPC,
                        scale=SC_QK)), waits=exp_waits, inc=(aa, 1),
                        fence=DEXP)
                    last_exp[b] = a_e
                    exp_of[g] = a_e
                if g >= 1:
                    i2, h2, kts2 = tiles[g - 1]
                    par2 = i2 % 2
                    b2 = (g - 1) % 3
                    av_waits = [(aa, exp_of[g - 1])]
                    if kts2 == 0:
                        av_waits.append((vv, last_oacc))
                    t_av = None
                    for qs in range(4):
                        t_av = P.step("tensor", (lambda e, r, b=b2, qs=qs, par=par2,
                                                 h=h2, kts=kts2:
                            e.matmul(pO[:, qs * 512:qs * 512 + 129],
                                     P_sb[b][:, qs * 128:(qs + 1) * 128],
                                     kv_sb[par][:, 8192 + (h * 4 + kts) * 132:
                                                8192 + (h * 4 + kts) * 132 + 129],
                                     start=(kts == 0), stop=(kts == 3),
                                     skip_group_check=True)),
                            waits=av_waits if qs == 0 else [], inc=(tm, 1))
                    last_av[b2] = t_av
                    if kts2 == 3:
                        pO_ap = bass.AP(pO, 0, [[2048, 128], [512, 4], [1, 129]])
                        for half in range(2):
                            oap = bass.AP(O_all, half * 8448 + h2 * 528,
                                          [[2 * 16 * 528, 128], [132, 4], [1, 129]])
                            st_ = P.step("vector", (lambda e, r, oap=oap, i=i2,
                                                    half=half:
                                e.scalar_tensor_tensor(
                                    oap, pO_ap,
                                    tabm_sb[:, 2 * i + half:2 * i + half + 1],
                                    oap, OP.mult, OP.add)),
                                waits=[(tm, last_av[0]), (tm, last_av[1]),
                                       (tm, last_av[2]), (sL, v_tabm)],
                                inc=(vv, 1) if half == 1 else None, fence=DSTT)
                        last_oacc = st_
                        if h2 == 15:
                            assert tm.v == pair_mm[i2], (i2, tm.v, pair_mm[i2])
            P.run(nc, sems)
            ctxDP.close()

            o_fm = kv_sb[0]
            pE1ctx = ExitStack()
            with pE1ctx:
                o_nrm = [sb(f"o_nrm{i}", [128, 128], BF16, pE1ctx) for i in range(2)]
                pOT = [psum(f"pOT{i}", [128, 128], BF16, pE1ctx) for i in range(2)]
                P = Phase(fence_all=FENCE_ALL["E1"])
                P.step("vector", lambda e, r: e.reciprocal(
                    bass.AP(O_all, 128, [[2 * 16 * 528, 128], [132, 128], [1, 1]]),
                    bass.AP(O_all, 128, [[2 * 16 * 528, 128], [132, 128], [1, 1]])),
                    fence=True)
                last_tr = [0, 0]
                last_cp = [0, 0]
                for h in range(H):
                    for qt in range(8):
                        b = (h * 8 + qt) % 2
                        off = (qt // 4) * 8448 + h * 528 + (qt % 4) * 132
                        v_n = P.step("vector", (lambda e, r, off=off, b=b:
                            e.tensor_scalar(o_nrm[b][:, :],
                                            O_all[:, off:off + 128],
                                            O_all[:, off + 128:off + 129],
                                            None, OP.mult)),
                            waits=[(tm, last_tr[b])], inc=(vv, 1), fence=E1VN)
                        t_t = P.step("tensor", (lambda e, r, b=b: e.transpose(
                            pOT[b][:, :], o_nrm[b][:, :], id_sb[:, :])),
                            waits=[(vv, v_n), (aa, last_cp[b])], inc=(tm, 1))
                        last_tr[b] = t_t
                        off2 = h * 1024 + qt * 128
                        last_cp[b] = P.step("scalar", (lambda e, r, off2=off2, b=b:
                            e.copy(o_fm[:, off2:off2 + 128], pOT[b][:, :])),
                            waits=[(tm, t_t)], inc=(aa, 1), fence=E1CP)
                P.run(nc, sems)

            ctxE = ExitStack()
            with ctxE:
                woc = [sb(f"woc{i}", [128, 512], BF16, ctxE) for i in range(2)]
                pSP = [psum(f"pSP{i}", [128, 512], F32, ctxE) for i in range(8)]
                P = Phase(fence_all=FENCE_ALL["E"])
                v_se = [0] * 8
                wo_last = [0, 0]
                for dc in range(4):
                    t_s = [None] * 8
                    for s_ in range(16):
                        wpar = s_ % 2
                        wo_waits = [(tm, wo_last[wpar])] if (dc > 0 or s_ >= 2) else []
                        wo_done = P.dma(
                            "gpsimd", sG, woc[wpar][:, :],
                            bass.AP(woutT, s_ * 128 * D + dc * 512,
                                    [[D, 128], [1, 512]]),
                            waits=wo_waits)
                        for qt in range(8):
                            waits = [(sG, wo_done)]
                            if s_ == 0:
                                waits.append((vv, v_se[qt]))
                            t_s[qt] = P.step("tensor", (lambda e, r, s_=s_, qt=qt,
                                                        dc=dc, wpar=wpar:
                                e.matmul(pSP[qt][:, :],
                                         o_fm[:, s_ * 1024 + qt * 128:
                                              s_ * 1024 + (qt + 1) * 128],
                                         woc[wpar][:, :],
                                         start=(s_ == 0), stop=(s_ == 15))),
                                waits=waits,
                                inc=(tm, 1) if qt == 7 else None)
                        wo_last[wpar] = t_s[7]
                    for qt in range(8):
                        v_se[qt] = P.step("vector", (lambda e, r, qt=qt, dc=dc:
                            e.tensor_copy(s_sb[:, qt * D + dc * 512:
                                               qt * D + (dc + 1) * 512],
                                          pSP[qt][:, :])),
                            waits=[(tm, t_s[7])], inc=(vv, 1))
                P.run(nc, sems)

        ctxDE.close()

        ctxF = ExitStack()
        with ctxF:
            hst = [sb(f"hst{i}", [128, DFF], F32, ctxF) for i in range(2)]
            hq_fm = sb("hq_fm", [128, 64 * 512], BF16, ctxF)
            hq_tm = sb("hq_tm", [128, 2048], BF16, ctxF)
            i32f = sb("i32f", [128, 2048], I32, ctxF)
            w3t = [sb(f"w3t{i}", [128, 4 * 512], BF16, ctxF) for i in range(2)]
            g_b = sb("g_b", [128, 512], BF16, ctxF)
            xre = [sb(f"xre{i}", [128, 512], F32, ctxF) for i in range(2)]
            ob = [sb(f"ob{i}", [128, 512], F32, ctxF) for i in range(2)]
            hsq = sb("hsq", [128, 2048], F32, ctxF)
            acc4 = sb("acc4", [128, 4], F32, ctxF)
            pG = [psum(f"pG{i}", [128, 512], F32, ctxF) for i in range(4)]
            pFT = [psum(f"pFT{i}", [128, 128], BF16, ctxF) for i in range(4)]

            P = Phase(fence_all=FENCE_ALL["F"])

            def newtonF(e, src_ap, eps):
                e.reciprocal(tmp1[:, 1:2], tmp1[:, 0:1])
                e.tensor_mul(tmp1[:, 2:3], tmp1[:, 1:2], tmp1[:, 1:2])
                e.tensor_scalar(tmp1[:, 3:4], src_ap, 1.0, float(eps), OP.mult, OP.add)
                e.tensor_mul(tmp1[:, 2:3], tmp1[:, 2:3], tmp1[:, 3:4])
                e.tensor_scalar(tmp1[:, 2:3], tmp1[:, 2:3], -0.5, 1.5, OP.mult, OP.add)
                return e.tensor_mul(tmp1[:, 1:2], tmp1[:, 1:2], tmp1[:, 2:3])

            h_loaded = [0] * NT
            h_loaded[0] = P.dma("sync", sL, hst[0][:, :], h_dram.ap()[0],
                                waits=[(sL, h_store)])
            h_loaded[1] = P.dma("sync", sL, hst[1][:, :], h_dram.ap()[1])

            t_trF_used = [0, 0, 0, 0]
            a_cpF = [0, 0, 0, 0]
            storm_done = 0
            st_out = 0
            x_loaded = {}
            g_done = [0] * 4
            w3t_last = [0, 0]
            last_vo = 0
            last_vo_bo = [0, 0]
            st_out_bo = [0, 0]
            a_hq = 0
            for tg in range(2):
                for tl in range(4):
                    tt = tg * 4 + tl
                    par = tt % 2
                    for q4 in range(4):
                        sl = slice(q4 * 2048, (q4 + 1) * 2048)
                        P.step("vector", (lambda e, r, par=par, sl=sl:
                            e.tensor_mul(hsq[:, :], hst[par][:, sl], hst[par][:, sl])),
                            waits=[(sL, h_loaded[tt])])
                        P.step("vector", (lambda e, r, q4=q4: e.reduce_sum(
                            acc4[:, q4:q4 + 1], hsq[:, :], axis=AX.X)), fence=True)
                    def vbnF(e, r, F):
                        F(e.reduce_sum(tmp2[:, 5:6], acc4[:, :], axis=AX.X))
                        return e.tensor_scalar(tmp2[:, 0:1], tmp2[:, 5:6], 1.0 / DFF,
                                               None, OP.mult)
                    v_bn = P.step("vector", vbnF, inc=(vv, 1), fence=True)
                    a1 = P.step("scalar", lambda e, r: e.activation(
                        tmp1[:, 0:1], tmp2[:, 0:1], AF.Sqrt, bias=np.float32(RMS_EPS)),
                        waits=[(vv, v_bn)], inc=(aa, 1))
                    P.step("vector", (lambda e, r: newtonF(e, tmp2[:, 0:1], RMS_EPS)),
                           waits=[(aa, a1)])
                    P.step("vector", (lambda e, r, par=par: e.reduce_max(
                        tmp1[:, 6:7], hst[par][:, :], axis=AX.X,
                        apply_absolute_value=True)), fence=True)

                    def scalesF(e, r, F, tt=tt):
                        F(e.tensor_mul(tmp1[:, 7:8], tmp1[:, 1:2], tmp1[:, 6:7]))
                        F(e.tensor_scalar(tmp1[:, 7:8], tmp1[:, 7:8], float(Q_EPS),
                                          None, OP.max))
                        F(e.tensor_scalar(tmp2[:, 3:4], tmp1[:, 7:8], 1.0 / 127.0,
                                          None, OP.mult))
                        e.tensor_scalar(f3t[:, tt:tt + 1], tmp2[:, 3:4], mh_b[:, 2:3],
                                        None, OP.mult)
                        F(e.reciprocal(tmp2[:, 1:2], tmp1[:, 7:8]))
                        F(e.tensor_scalar(tmp2[:, 1:2], tmp2[:, 1:2], 127.0, None,
                                          OP.mult))
                        return e.tensor_mul(tmp2[:, 2:3], tmp2[:, 1:2], tmp1[:, 1:2])
                    P.step("vector", scalesF, fence=True)
                    for q4 in range(4):
                        sl = slice(q4 * 2048, (q4 + 1) * 2048)
                        P.step("vector", (lambda e, r, par=par, sl=sl: e.tensor_scalar(
                            hst[par][:, sl], hst[par][:, sl], tmp2[:, 2:3], MAGIC,
                            OP.mult, OP.add)), fence=True)
                        v_u = P.step("vector", (lambda e, r, par=par, sl=sl, q4=q4:
                            e.tensor_scalar(hst[par][:, sl], hst[par][:, sl], MAGIC,
                                            0.0, OP.subtract, OP.add,
                                            accum_out=acc4[:, q4:q4 + 1])),
                            inc=(vv, 1))
                        P.step("scalar", (lambda e, r, par=par, sl=sl: e.copy(
                            i32f[:, :], hst[par][:, sl])),
                            waits=[(vv, v_u), (tm, t_trF_used[0]),
                                   (tm, t_trF_used[1]), (tm, t_trF_used[2]),
                                   (tm, t_trF_used[3])], fence=True)
                        a_hq = P.step("scalar", lambda e, r: e.copy(hq_tm[:, :],
                                                                    i32f[:, :]),
                                      inc=(aa, 1))
                        for s2 in range(16):
                            ds = q4 * 16 + s2
                            btr = ds % 4
                            t_t = P.step("tensor", (lambda e, r, s2=s2, btr=btr:
                                e.transpose(pFT[btr][:, :],
                                            hq_tm[:, s2 * 128:(s2 + 1) * 128],
                                            id_sb[:, :])),
                                waits=[(aa, a_hq), (vv, a_cpF[btr]),
                                       (tm, storm_done)], inc=(tm, 1))
                            t_trF_used[btr] = t_t
                            a_cpF[btr] = P.step("vector", (lambda e, r, ds=ds, tl=tl,
                                                           btr=btr:
                                e.tensor_copy(hq_fm[:, ds * 512 + tl * 128:
                                                    ds * 512 + (tl + 1) * 128],
                                              pFT[btr][:, :])),
                                waits=[(tm, t_t)], inc=(vv, 1), fence=True)
                    def vrsF(e, r, F, tt=tt):
                        F(e.reduce_sum(rsh[:, tt:tt + 1], acc4[:, :], axis=AX.X))
                        F(e.tensor_mul(tmp2[:, 4:5], rsh[:, tt:tt + 1],
                                       f3t[:, tt:tt + 1]))
                        return e.tensor_scalar(b3t[:, tt:tt + 1], tmp2[:, 4:5], -1.0,
                                               None, OP.mult)
                    v_rs = P.step("vector", vrsF, inc=(vv, 1), fence=True)
                    nxt = tt + 2
                    if nxt < NT:
                        h_loaded[nxt] = P.dma("sync", sL, hst[par][:, :],
                                              h_dram.ap()[nxt],
                                              waits=[(vv, v_rs), (aa, a_hq)])
                hq_ready = vv.v

                for dq in range(4):
                    w3_done = [0] * 16
                    for s4 in range(2):
                        w3_done[s4] = P.dma(
                            "scalar", sS,
                            bass.AP(w3t[s4 % 2], 0, [[4 * 512, 128], [512, 4],
                                                     [1, 512]]),
                            bass.AP(w3q_dram, s4 * 4 * 128 * D + dq * 512,
                                    [[D, 128], [128 * D, 4], [1, 512]]),
                            waits=[(sS, w3q_store), (tm, w3t_last[s4 % 2])])
                    ds_mm = [0] * 64
                    for s4 in range(16):
                        wpar = s4 % 2
                        for dsl in range(4):
                            ds = s4 * 4 + dsl
                            for tl in range(4):
                                waits = [(sS, w3_done[s4]), (vv, hq_ready)]
                                if ds == 0:
                                    waits.append((aa, g_done[tl]))
                                ds_mm[ds] = P.step("tensor", (lambda e, r, ds=ds,
                                                              tl=tl, wpar=wpar,
                                                              dsl=dsl:
                                    e.matmul(pG[tl][:, :],
                                             hq_fm[:, ds * 512 + tl * 128:
                                                   ds * 512 + (tl + 1) * 128],
                                             w3t[wpar][:, dsl * 512:
                                                       (dsl + 1) * 512],
                                             start=(ds == 0), stop=(ds == 63))),
                                    waits=waits,
                                    inc=(tm, 1) if tl == 3 else None)
                        w3t_last[wpar] = ds_mm[s4 * 4 + 3]
                        if s4 + 2 < 16:
                            w3_done[s4 + 2] = P.dma(
                                "scalar", sS,
                                bass.AP(w3t[wpar], 0, [[4 * 512, 128], [512, 4],
                                                       [1, 512]]),
                                bass.AP(w3q_dram, (s4 + 2) * 4 * 128 * D + dq * 512,
                                        [[D, 128], [128 * D, 4], [1, 512]]),
                                waits=[(tm, ds_mm[s4 * 4 + 3])])
                    storm_done = ds_mm[63]
                    for tl in range(4):
                        tt = tg * 4 + tl
                        bo = tl % 2
                        xk = (tg, dq, tl)
                        x_loaded[xk] = P.dma(
                            "sync", sL, xre[bo][:, :],
                            bass.AP(xin, (tt * 128) * D + dq * 512,
                                    [[D, 128], [1, 512]]),
                            waits=[(vv, last_vo_bo[bo])])
                        a_g = P.step("scalar", (lambda e, r, tl=tl, tt=tt:
                            e.activation(g_b[:, :], pG[tl][:, :], AF.Silu,
                                         bias=b3t[:, tt:tt + 1],
                                         scale=f3t[:, tt:tt + 1])),
                            waits=[(tm, storm_done), (vv, last_vo)], inc=(aa, 1),
                            fence=True)
                        g_done[tl] = a_g
                        def vcomb(e, r, F, tt=tt, dq=dq, bo=bo):
                            F(e.tensor_mul(g_b[:, :],
                                           s_sb[:, tt * D + dq * 512:
                                                tt * D + (dq + 1) * 512],
                                           g_b[:, :]))
                            return e.tensor_add(ob[bo][:, :], xre[bo][:, :],
                                                g_b[:, :])
                        v_o = P.step("vector", vcomb,
                            waits=[(aa, a_g), (sL, x_loaded[xk]),
                                   (sL, st_out_bo[bo])], inc=(vv, 1), fence=True)
                        last_vo = v_o
                        last_vo_bo[bo] = v_o
                        st_out_bo[bo] = P.dma(
                            "sync", sL,
                            bass.AP(out, (tt * 128) * D + dq * 512,
                                    [[D, 128], [1, 512]]),
                            ob[bo][:, :], waits=[(vv, v_o)])
                        st_out = st_out_bo[bo]
            P.step("sync", None, waits=[(sL, st_out)])
            P.run(nc, sems)

    return nc



def _host_prep(L0_attn_x, L0_attn_w_in, L0_attn_w_out, L0_ffn_w1, L0_ffn_w2,
               L0_ffn_w3):
    c = np.ascontiguousarray
    x = np.asarray(L0_attn_x, np.float32).reshape(S, D)
    w_in = np.asarray(L0_attn_w_in, np.float32)
    w_out = np.asarray(L0_attn_w_out, np.float32)
    w1 = np.asarray(L0_ffn_w1, np.float32)
    w2 = np.asarray(L0_ffn_w2, np.float32)
    w3 = np.asarray(L0_ffn_w3, np.float32)

    wqkv = c(w_in.reshape(H, 3, DH, D).transpose(1, 0, 3, 2))
    woutT = c(w_out.T.reshape(16, 128, D))
    w1c = c(w1.T.reshape(D, 32, 256).transpose(1, 0, 2))
    w2c = c(w2.T.reshape(D, 32, 256).transpose(1, 0, 2))
    w3c = c(w3.T.reshape(64, 128, D))

    kk = np.arange(512).reshape(4, 128)
    qq = np.arange(512)
    masks = np.where(kk[:, :, None] <= qq[None, None, :], 0.0, MASKVAL
                     ).astype(np.float32)
    ident = np.eye(128, dtype=np.float32)

    shared = dict(wqkv=wqkv, woutT=woutT, w1c=w1c, w2c=w2c, w3c=w3c,
                  masks=masks, ident=ident)

    def rowof(g):
        return 2 * g if g <= 7 else 2 * (15 - g) + 1

    in_maps = []
    for core in range(NCORE):
        chA, chB = core, 15 - core
        xc = c(np.concatenate([x[CH * chA:CH * (chA + 1)],
                               x[CH * chB:CH * (chB + 1)]], axis=0))
        w1s_ = c(w1[1024 * core:1024 * (core + 1)].reshape(2, 128, 8192))
        w2s_ = c(w2[1024 * core:1024 * (core + 1)].reshape(2, 128, 8192))
        w3s_ = c(w3[256 * core:256 * (core + 1)].reshape(2, 128, 8192))
        pairs = [(0, chA), (1, chB)]
        pairs += [(0, j) for j in range(chA)]
        pairs += [(1, j) for j in range(chB)]
        assert len(pairs) == NPAIR
        tabv = np.zeros(128, np.int32)
        tabmv = np.zeros((128, 64), np.float32)
        for i, (qsel, kch) in enumerate(pairs):
            tabv[TAB_K + i] = rowof(kch) * ROW
            tabv[TAB_Q + i] = qsel * (128 * H * 512)
            tabmv[:, 2 * i + qsel] = 1.0
        in_maps.append(dict(shared, xin=xc, w1s=w1s_, w2s=w2s_, w3s=w3s_,
                            tabi=tabv.reshape(1, 128), tabm=tabmv))
    return in_maps


def kernel(L0_attn_x, L0_attn_w_in, L0_attn_w_out, L0_ffn_w1, L0_ffn_w2,
           L0_ffn_w3):
    in_maps = _host_prep(L0_attn_x, L0_attn_w_in, L0_attn_w_out, L0_ffn_w1,
                         L0_ffn_w2, L0_ffn_w3)
    nc = build_program()
    res = run_bass_kernel_spmd(nc, in_maps, core_ids=list(range(NCORE)))
    full = np.zeros((1, S, D), np.float32)
    for core in range(NCORE):
        o = np.asarray(res.results[core]["outt"], np.float32)
        chA, chB = core, 15 - core
        full[0, CH * chA:CH * (chA + 1)] = o[0:CH]
        full[0, CH * chB:CH * (chB + 1)] = o[CH:2 * CH]
    return full


if __name__ == "__main__":
    nc = build_program()
    print("build OK; instructions:", nc.n_instructions())


# version 25 (final)
# speedup vs baseline: 1.2101x; 1.0394x over previous
from contextlib import ExitStack

import numpy as np
import concourse.bass as bass
import concourse.mybir as mybir
from concourse.bass_utils import run_bass_kernel_spmd

F32 = mybir.dt.float32
BF16 = mybir.dt.bfloat16
I32 = mybir.dt.int32
AF = mybir.ActivationFunctionType
OP = mybir.AluOpType
AX = mybir.AxisListType

NCORE = 8
S = 8192
D = 2048
H = 16
DH = 128
DFF = 8192
T = 1024
NT = 8
CH = 512

KVW = 8192 + 8448
KV_AREA = 128 * KVW
STATS = 1024
ROW = KV_AREA + STATS

NPAIR = 17
TAB_K, TAB_V, TAB_Q, TAB_O = 0, 17, 34, 51

MAGIC = 12582912.0
SC_QK = np.float32(1.0 / np.sqrt(128.0))
EXPC = np.float32(-15.0)
MASKVAL = -1.0e6
LN_EPS = 1.0e-5
RMS_EPS = 1.0e-6
Q_EPS = 1.0e-5

ENGS = ["sync", "gpsimd", "tensor", "vector", "scalar"]


class Sem:
    __slots__ = ("name", "v")

    def __init__(self, name):
        self.name = name
        self.v = 0

    def inc(self, n=1):
        self.v += n
        return self.v


FCNT = {e: 0 for e in ENGS}
FSEM = {"vector": "Fv", "scalar": "Fa", "gpsimd": "Fg", "tensor": "Ft",
        "sync": "Fs"}
FENCE_ALL = {"A": False, "B": False, "C": False, "D": False, "E1": False,
             "E": True, "F": True}
E1VN = True
E1CP = False
DEXP = False
DSTT = True


class Phase:

    def __init__(self, fence_all=False):
        self.fence_all = fence_all
        self.plan = {e: [] for e in ENGS}
        self.regs = {e: [] for e in ENGS}

    def want_regs(self, eng, names):
        self.regs[eng] = list(names)

    def step(self, eng, act, waits=(), inc=None, fence=False):
        if self.fence_all and eng in ("vector", "scalar"):
            fence = True
        inc_val = inc[0].inc(inc[1]) if inc is not None else None
        self.plan[eng].append((list(waits), act, inc, fence, inc_val))
        return inc_val

    def dma(self, eng, sem, out_ap, in_ap, waits=(), **kw):
        return self.step(
            eng,
            (lambda e, r, o=out_ap, i=in_ap, kw=kw: e.dma_start(out=o, in_=i, **kw)),
            waits=waits, inc=(sem, 16), fence=False)

    def run(self, nc, sems):
        def mk(engname):
            steps = self.plan[engname]
            regnames = self.regs[engname]
            fsem = sems[FSEM[engname]]

            def closure(eng):
                def F(inst):
                    FCNT[engname] += 1
                    inst.then_inc(fsem, 1)
                    eng.wait_ge(fsem, FCNT[engname])
                    return inst

                with ExitStack() as rs:
                    regs = {n: rs.enter_context(eng.register(n)) for n in regnames}
                    last = {}
                    for si, (waits, act, inc, fence, inc_val) in enumerate(steps):
                        for s, v in waits:
                            if v > 0 and last.get(s.name, -1) < v:
                                eng.wait_ge(sems[s.name], v)
                                last[s.name] = v
                        try:
                            if act is None:
                                inst = None
                            elif (act.__code__.co_argcount >= 3
                                  and act.__code__.co_varnames[2] == "F"):
                                inst = act(eng, regs, F)
                            else:
                                inst = act(eng, regs)
                        except Exception:
                            import traceback as _tb
                            print(f"STEP FAIL eng={engname} idx={si}")
                            _tb.print_exc()
                            raise
                        if inc is not None:
                            inst.then_inc(sems[inc[0].name], inc[1])
                            if fence:
                                eng.wait_ge(sems[inc[0].name], inc_val)
                        elif fence and inst is not None:
                            F(inst)
            return closure

        with nc.Block() as block:
            for engname in ENGS:
                if self.plan[engname]:
                    getattr(block, engname)(mk(engname))


def build_program():
    for k in FCNT:
        FCNT[k] = 0
    nc = bass.Bass("TRN2", target_bir_lowering=False, detect_race_conditions=False)

    xin = nc.dram_tensor("xin", [T, D], F32, kind="ExternalInput")
    wqkv = nc.dram_tensor("wqkv", [3, H, D, DH], F32, kind="ExternalInput")
    woutT = nc.dram_tensor("woutT", [16, 128, D], F32, kind="ExternalInput")
    w1c = nc.dram_tensor("w1c", [32, D, 256], F32, kind="ExternalInput")
    w2c = nc.dram_tensor("w2c", [32, D, 256], F32, kind="ExternalInput")
    w3c = nc.dram_tensor("w3c", [64, 128, D], F32, kind="ExternalInput")
    w1s = nc.dram_tensor("w1s", [2, 128, 8192], F32, kind="ExternalInput")
    w2s = nc.dram_tensor("w2s", [2, 128, 8192], F32, kind="ExternalInput")
    w3s = nc.dram_tensor("w3s", [2, 128, 8192], F32, kind="ExternalInput")
    masks = nc.dram_tensor("masks", [4, 128, 512], F32, kind="ExternalInput")
    ident = nc.dram_tensor("ident", [128, 128], F32, kind="ExternalInput")
    tab = nc.dram_tensor("tabi", [1, 128], I32, kind="ExternalInput")
    tabm = nc.dram_tensor("tabm", [128, 64], F32, kind="ExternalInput")
    out = nc.dram_tensor("outt", [T, D], F32, kind="ExternalOutput")

    ag_in = nc.dram_tensor("ag_in", [2, ROW], BF16, kind="Internal")
    ag_out = nc.dram_tensor("ag_out", [16, ROW], BF16, kind="Internal")
    q_dram = nc.dram_tensor("q_dram", [2, 128, H * 512], BF16, kind="Internal")
    h_dram = nc.dram_tensor("h_dram", [NT, 128, DFF], F32, kind="Internal")
    w3q_dram = nc.dram_tensor("w3q_dram", [64, 128, D], BF16, kind="Internal")

    sems = {}
    S_ = {}
    gctx = ExitStack()
    with gctx:
        for nm in ["sL", "sS", "sG", "cc", "tm", "vv", "aa", "gg",
                   "Fv", "Fa", "Fg", "Ft", "Fs"]:
            sems[nm] = gctx.enter_context(nc.semaphore(nm))
            S_[nm] = Sem(nm)
        sL, sS, sG, cc, tm, vv, aa, gg = (
            S_[k] for k in ["sL", "sS", "sG", "cc", "tm", "vv", "aa", "gg"])

        def sb(name, shape, dtype=F32, ctx=gctx):
            return ctx.enter_context(nc.sbuf_tensor(name, shape, dtype))

        def psum(name, shape, dtype=F32, ctx=gctx):
            return ctx.enter_context(nc.psum_tensor(name, shape, dtype))

        tab_sb = sb("tab_sb", [1, 128], I32)
        tabm_sb = sb("tabm_sb", [128, 64], F32)
        mask_sb = sb("mask_sb", [128, 4 * 512], BF16)
        id_sb = sb("id_sb", [128, 128], BF16)
        ones_row = sb("ones_row", [1, 128], F32)
        onesc = sb("onesc", [128, 1], F32)
        rsx = sb("rsx", [128, NT], F32)
        inv_sx = sb("inv_sx", [128, NT], F32)
        f1t = sb("f1t", [128, NT], F32)
        f2t = sb("f2t", [128, NT], F32)
        b1t = sb("b1t", [128, NT], F32)
        b2t = sb("b2t", [128, NT], F32)
        f3t = sb("f3t", [128, NT], F32)
        b3t = sb("b3t", [128, NT], F32)
        rsh = sb("rsh", [128, NT], F32)
        sw_b = sb("sw_b", [128, 3], F32)
        mh_b = sb("mh_b", [128, 3], F32)
        stats_f = sb("stats_f", [1, 16], F32)
        stats_bf = sb("stats_bf", [1, 16], BF16)
        tmp1 = sb("tmp1", [128, 8], F32)
        tmp2 = sb("tmp2", [128, 8], F32)

        ctxAC = ExitStack()
        ctxAC.__enter__()
        xq_fm = sb("xq_fm", [128, 16 * 1024], BF16, ctxAC)
        ctxAB = ExitStack()
        with ctxAB:
            xn_fm = sb("xn_fm", [128, 16 * 1024], BF16, ctxAB)

            ctxA = ExitStack()
            with ctxA:
                x_sb = sb("x_sb", [128, NT * D], F32, ctxA)
                xn_f = sb("xn_f", [128, D], F32, ctxA)
                xn_b = sb("xn_b", [128, D], BF16, ctxA)
                xq_b = sb("xq_b", [128, D], BF16, ctxA)
                i32b = sb("i32b", [128, D], I32, ctxA)
                strb = sb("strb", [128, 8192], F32, ctxA)
                bnst = sb("bnst", [128, 6], F32, ctxA)
                bnag = sb("bnag", [128, 2], F32, ctxA)
                strp = sb("strp", [128, 3], F32, ctxA)
                ztail = sb("ztail", [1, 1024], BF16, ctxA)
                pT = psum("pT", [128, 128], BF16, ctxA)
                pT2 = psum("pT2", [128, 128], BF16, ctxA)
                pR = psum("pR", [1, 8], F32, ctxA)

                P = Phase(fence_all=FENCE_ALL["A"])
                v_x = P.dma("sync", sL,
                            bass.AP(x_sb, 0, [[NT * D, 128], [D, NT], [1, D]]),
                            xin.ap().rearrange("(t p) d -> p t d", p=128))
                v_tab = P.dma("sync", sL, tab_sb[:, :], tab.ap())
                v_tabm = P.dma("sync", sL, tabm_sb[:, :], tabm.ap())
                v_mask = P.dma("gpsimd", sG,
                               bass.AP(mask_sb, 0, [[4 * 512, 128], [512, 4], [1, 512]]),
                               masks.ap().transpose([1, 0, 2]))
                v_id = P.dma("gpsimd", sG, id_sb[:, :], ident.ap())
                P.step("gpsimd", lambda e, r: e.memset(ones_row[:, :], 1.0))
                g_ones = P.step("gpsimd", lambda e, r: e.memset(onesc[:, :], 1.0),
                                inc=(gg, 1))

                P.step("vector", lambda e, r: e.memset(strp[:, :], 0.0))
                v_red = 0
                first = True
                for (wsrc, j) in [(w1s, 0), (w2s, 1), (w3s, 2)]:
                    for b in range(2):
                        waits = [] if first else [(vv, v_red)]
                        first = False
                        t_l = P.dma("sync", sL, strb[:, :], wsrc.ap()[b], waits=waits)
                        P.step("vector",
                               lambda e, r: e.reduce_sum(tmp1[:, 0:1], strb[:, :],
                                                         axis=AX.X,
                                                         apply_absolute_value=True),
                               waits=[(sL, t_l)], fence=True)
                        v_red = P.step(
                            "vector",
                            (lambda e, r, j=j: e.tensor_add(strp[:, j:j + 1],
                                                            strp[:, j:j + 1],
                                                            tmp1[:, 0:1])),
                            inc=(vv, 1), fence=True)
                t_red = None
                for j in range(3):
                    t_red = P.step(
                        "tensor",
                        (lambda e, r, j=j: e.matmul(pR[0:1, j:j + 1], strp[:, j:j + 1],
                                                    onesc[:, :], start=(j == 0),
                                                    stop=(j == 2),
                                                    skip_group_check=True)),
                        waits=[(vv, v_red), (gg, g_ones)],
                        inc=(tm, 1) if j == 2 else None)
                a_st = P.step("scalar", lambda e, r: e.copy(stats_f[:, 0:3], pR[0:1, 0:3]),
                              waits=[(tm, t_red)], inc=(aa, 1))

                def enc(e, r, F):
                    e.memset(stats_bf[:, :], 0.0)
                    F(e.tensor_copy(stats_bf[:, 0:3], stats_f[:, 0:3]))
                    F(e.tensor_copy(stats_f[:, 4:7], stats_bf[:, 0:3]))
                    F(e.tensor_sub(stats_f[:, 8:11], stats_f[:, 0:3], stats_f[:, 4:7]))
                    F(e.tensor_copy(stats_bf[:, 3:6], stats_f[:, 8:11]))
                    F(e.tensor_copy(stats_f[:, 4:7], stats_bf[:, 3:6]))
                    F(e.tensor_sub(stats_f[:, 8:11], stats_f[:, 8:11],
                                   stats_f[:, 4:7]))
                    return e.tensor_copy(stats_bf[:, 6:9], stats_f[:, 8:11])
                v_enc = P.step("vector", enc, waits=[(aa, a_st)], inc=(vv, 1))
                v_zt = P.step("vector", lambda e, r: e.memset(ztail[:, :], 0.0),
                              inc=(vv, 1), fence=True)
                z_t = 0
                for half in range(2):
                    z_t = P.dma("sync", sL,
                                bass.AP(ag_in, half * ROW + KV_AREA,
                                        [[ROW, 1], [1, 1024]]),
                                ztail[0:1, :], waits=[(vv, v_zt)])
                P.dma("gpsimd", sG,
                      bass.AP(ag_in, KV_AREA, [[ROW, 1], [1, 16]]),
                      stats_bf[0:1, 0:16], waits=[(vv, v_enc), (sL, z_t)])
                g_stats = sG.v

                def newton(e, F, src_ap, eps):
                    F(e.reciprocal(tmp1[:, 1:2], tmp1[:, 0:1]))
                    e.tensor_mul(tmp1[:, 2:3], tmp1[:, 1:2], tmp1[:, 1:2])
                    F(e.tensor_scalar(tmp1[:, 3:4], src_ap, 1.0, float(eps),
                                      OP.mult, OP.add))
                    F(e.tensor_mul(tmp1[:, 2:3], tmp1[:, 2:3], tmp1[:, 3:4]))
                    F(e.tensor_scalar(tmp1[:, 2:3], tmp1[:, 2:3], -0.5, 1.5,
                                      OP.mult, OP.add))
                    return e.tensor_mul(tmp1[:, 1:2], tmp1[:, 1:2], tmp1[:, 2:3])

                a_xn_copy = 0
                v_xq_copy = 0
                t_xnb_used = 0
                t_xqb_used = 0
                a_xq_prev = 0
                for tt in range(NT):
                    xs_ap = x_sb[:, tt * D:(tt + 1) * D]
                    P.step("vector", (lambda e, r, xs=xs_ap: e.tensor_mul(
                        xn_f[:, :], xs, xs)),
                        waits=[(sL, v_x), (tm, t_xnb_used), (aa, a_xq_prev)])
                    P.step("vector", lambda e, r: e.reduce_sum(
                        tmp2[:, 5:6], xn_f[:, :], axis=AX.X), fence=True)
                    P.step("vector", (lambda e, r, xs=xs_ap: e.reduce_sum(
                        tmp1[:, 4:5], xs, axis=AX.X)), fence=True)
                    def vbn(e, r, F):
                        e.tensor_scalar(tmp1[:, 4:5], tmp1[:, 4:5], 1.0 / D, None,
                                        OP.mult)
                        F(e.tensor_scalar(tmp1[:, 5:6], tmp2[:, 5:6], 1.0 / D, None,
                                          OP.mult))
                        F(e.tensor_mul(tmp1[:, 6:7], tmp1[:, 4:5], tmp1[:, 4:5]))
                        return e.tensor_sub(tmp2[:, 0:1], tmp1[:, 5:6], tmp1[:, 6:7])
                    v_bn = P.step("vector", vbn, inc=(vv, 1), fence=True)
                    a1 = P.step("scalar", lambda e, r: e.activation(
                        tmp1[:, 0:1], tmp2[:, 0:1], AF.Sqrt, bias=np.float32(LN_EPS)),
                        waits=[(vv, v_bn)], inc=(aa, 1))
                    P.step("vector", (lambda e, r, F: newton(e, F, tmp2[:, 0:1],
                                                             LN_EPS)),
                           waits=[(aa, a1)])
                    P.step("vector", lambda e, r: e.tensor_scalar(
                        tmp1[:, 4:5], tmp1[:, 4:5], -1.0, None, OP.mult), fence=True)
                    P.step("vector", (lambda e, r, xs=xs_ap: e.tensor_scalar(
                        xn_f[:, :], xs, tmp1[:, 4:5], tmp1[:, 1:2], OP.add, OP.mult)))
                    v_xnb = P.step("vector",
                                   lambda e, r: e.tensor_copy(xn_b[:, :], xn_f[:, :]),
                                   inc=(vv, 1))
                    P.step("vector", (lambda e, r, xs=xs_ap: e.tensor_mul(
                        xs, xn_f[:, :], xn_f[:, :])))
                    P.step("vector", (lambda e, r, xs=xs_ap: e.reduce_sum(
                        tmp2[:, 5:6], xs, axis=AX.X)), fence=True)
                    v_ms = P.step("vector", lambda e, r: e.tensor_scalar(
                        tmp2[:, 0:1], tmp2[:, 5:6], 1.0 / D, None, OP.mult),
                        inc=(vv, 1), fence=True)
                    a2 = P.step("scalar", lambda e, r: e.activation(
                        tmp1[:, 0:1], tmp2[:, 0:1], AF.Sqrt, bias=np.float32(RMS_EPS)),
                        waits=[(vv, v_ms)], inc=(aa, 1))
                    P.step("vector", (lambda e, r, F: newton(e, F, tmp2[:, 0:1],
                                                             RMS_EPS)),
                           waits=[(aa, a2)])
                    P.step("vector", lambda e, r: e.reduce_max(
                        tmp1[:, 6:7], xn_f[:, :], axis=AX.X,
                        apply_absolute_value=True), fence=True)

                    def scales(e, r, F, tt=tt):
                        F(e.tensor_mul(tmp1[:, 7:8], tmp1[:, 1:2], tmp1[:, 6:7]))
                        F(e.tensor_scalar(tmp1[:, 7:8], tmp1[:, 7:8], float(Q_EPS),
                                          None, OP.max))
                        e.tensor_scalar(inv_sx[:, tt:tt + 1], tmp1[:, 7:8], 1.0 / 127.0,
                                        None, OP.mult)
                        F(e.reciprocal(tmp2[:, 1:2], tmp1[:, 7:8]))
                        F(e.tensor_scalar(tmp2[:, 1:2], tmp2[:, 1:2], 127.0, None,
                                          OP.mult))
                        return e.tensor_mul(tmp2[:, 2:3], tmp2[:, 1:2], tmp1[:, 1:2])
                    P.step("vector", scales, fence=True)
                    P.step("vector", lambda e, r: e.tensor_scalar(
                        xn_f[:, :], xn_f[:, :], tmp2[:, 2:3], MAGIC, OP.mult,
                        OP.add), fence=True)
                    v_u = P.step("vector", (lambda e, r, tt=tt: e.tensor_scalar(
                        xn_f[:, :], xn_f[:, :], MAGIC, 0.0, OP.subtract, OP.add,
                        accum_out=rsx[:, tt:tt + 1])), inc=(vv, 1))
                    P.step("scalar", lambda e, r: e.copy(i32b[:, :], xn_f[:, :]),
                           waits=[(vv, v_u), (tm, t_xqb_used)], fence=True)
                    a_xq = P.step("scalar", lambda e, r: e.copy(xq_b[:, :], i32b[:, :]),
                                  inc=(aa, 1))
                    a_xq_prev = a_xq
                    for s_ in range(16):
                        t1 = P.step("tensor", (lambda e, r, s_=s_: e.transpose(
                            pT[:, :], xn_b[:, s_ * 128:(s_ + 1) * 128], id_sb[:, :])),
                            waits=[(vv, v_xnb), (sG, v_id), (aa, a_xn_copy)],
                            inc=(tm, 1))
                        if s_ == 15:
                            t_xnb_used = t1
                        a_xn_copy = P.step("scalar", (lambda e, r, s_=s_, tt=tt: e.copy(
                            xn_fm[:, s_ * 1024 + tt * 128: s_ * 1024 + (tt + 1) * 128],
                            pT[:, :])), waits=[(tm, t1)], inc=(aa, 1))
                        t2 = P.step("tensor", (lambda e, r, s_=s_: e.transpose(
                            pT2[:, :], xq_b[:, s_ * 128:(s_ + 1) * 128], id_sb[:, :])),
                            waits=[(aa, a_xq), (vv, v_xq_copy)], inc=(tm, 1))
                        if s_ == 15:
                            t_xqb_used = t2
                        v_xq_copy = P.step("vector", (lambda e, r, s_=s_, tt=tt:
                            e.tensor_copy(
                                xq_fm[:, s_ * 1024 + tt * 128: s_ * 1024 + (tt + 1) * 128],
                                pT2[:, :])), waits=[(tm, t2)], inc=(vv, 1))
                P.run(nc, sems)

            ctxB = ExitStack()
            with ctxB:
                wbuf = [sb(f"wbuf{i}", [128, 3 * 16 * 128], BF16, ctxB) for i in range(2)]
                kst = [sb(f"kst{i}", [128, H * 512], BF16, ctxB) for i in range(2)]
                vst = [sb(f"vst{i}", [128, H * 4 * 132], BF16, ctxB) for i in range(2)]
                qev = sb("qev", [128, 512], BF16, ctxB)
                vtm = sb("vtm", [128, 512], BF16, ctxB)
                pQ = psum("pQ", [128, 512], F32, ctxB)
                pK = psum("pK", [128, 512], F32, ctxB)
                pV = psum("pV", [128, 512], F32, ctxB)
                pVT = psum("pVT", [128, 128], BF16, ctxB)

                P = Phase(fence_all=FENCE_ALL["B"])
                for half in range(2):
                    P.step("vector", (lambda e, r, half=half: e.memset(
                        bass.AP(vst[half], 128, [[H * 4 * 132, 128], [132, 64], [1, 4]]),
                        1.0)))
                w_done = []
                tmB0 = tm.v
                head_mm_done = [tmB0 + 14 * (h + 1) for h in range(H)]
                for h in range(H):
                    par = h % 2
                    waits = [(tm, head_mm_done[h - 2])] if h >= 2 else []
                    tgt = None
                    for p_ in range(3):
                        tgt = P.dma(
                            "gpsimd", sG,
                            bass.AP(wbuf[par], p_ * 2048,
                                    [[3 * 16 * 128, 128], [128, 16], [1, 128]]),
                            wqkv.ap()[p_, h].rearrange("(s p) c -> p s c", p=128),
                            waits=waits if p_ == 0 else [])
                    w_done.append(tgt)

                a_qe = 0
                v_ke = 0
                v_ve = 0
                t_vt_used = 0
                st_q_last = 0
                for h in range(H):
                    par = h % 2
                    for half in range(2):
                        t_q = None
                        for s_ in range(16):
                            waits = [(sG, w_done[h])]
                            if s_ == 0:
                                waits.append((aa, a_qe))
                            t_q = P.step("tensor", (lambda e, r, s_=s_, half=half, par=par:
                                e.matmul(pQ[:, :],
                                         wbuf[par][:, 0 * 2048 + s_ * 128:(s_ + 1) * 128],
                                         xn_fm[:, s_ * 1024 + half * 512:
                                               s_ * 1024 + half * 512 + 512],
                                         start=(s_ == 0), stop=(s_ == 15))),
                                waits=waits, inc=(tm, 1) if s_ == 15 else None)
                        a_qe = P.step("scalar", lambda e, r: e.copy(qev[:, :], pQ[:, :]),
                                      waits=[(tm, t_q)], inc=(aa, 1))
                        st_q_last = P.dma(
                            "scalar", sS,
                            bass.AP(q_dram, half * 128 * H * 512 + h * 512,
                                    [[H * 512, 128], [1, 512]]),
                            qev[:, :], waits=[(aa, a_qe)])
                        t_k = None
                        for s_ in range(16):
                            waits = [(vv, v_ke)] if s_ == 0 else []
                            t_k = P.step("tensor", (lambda e, r, s_=s_, half=half, par=par:
                                e.matmul(pK[:, :],
                                         wbuf[par][:, 1 * 2048 + s_ * 128:
                                                   1 * 2048 + (s_ + 1) * 128],
                                         xn_fm[:, s_ * 1024 + half * 512:
                                               s_ * 1024 + half * 512 + 512],
                                         start=(s_ == 0), stop=(s_ == 15))),
                                waits=waits, inc=(tm, 1) if s_ == 15 else None)
                        v_ke = P.step("vector", (lambda e, r, h=h, half=half:
                            e.tensor_copy(kst[half][:, h * 512:(h + 1) * 512], pK[:, :])),
                            waits=[(tm, t_k)], inc=(vv, 1))
                        t_v = None
                        for s_ in range(16):
                            waits = [(vv, v_ve)] if s_ == 0 else []
                            t_v = P.step("tensor", (lambda e, r, s_=s_, half=half, par=par:
                                e.matmul(pV[:, :],
                                         wbuf[par][:, 2 * 2048 + s_ * 128:
                                                   2 * 2048 + (s_ + 1) * 128],
                                         xn_fm[:, s_ * 1024 + half * 512:
                                               s_ * 1024 + half * 512 + 512],
                                         start=(s_ == 0), stop=(s_ == 15))),
                                waits=waits, inc=(tm, 1) if s_ == 15 else None)
                        v_vt = P.step("vector", lambda e, r: e.tensor_copy(vtm[:, :],
                                                                           pV[:, :]),
                                      waits=[(tm, t_v), (tm, t_vt_used)], inc=(vv, 1))
                        for kts in range(4):
                            t_tr = P.step("tensor", (lambda e, r, kts=kts: e.transpose(
                                pVT[:, :], vtm[:, kts * 128:(kts + 1) * 128], id_sb[:, :])),
                                waits=[(vv, v_vt), (vv, v_ve)], inc=(tm, 1))
                            if kts == 3:
                                t_vt_used = t_tr
                            v_ve = P.step("vector", (lambda e, r, h=h, half=half, kts=kts:
                                e.tensor_copy(
                                    vst[half][:, (h * 4 + kts) * 132:
                                              (h * 4 + kts) * 132 + 128],
                                    pVT[:, :])), waits=[(tm, t_tr)], inc=(vv, 1))
                    assert tm.v == head_mm_done[h], (h, tm.v, head_mm_done[h])

                st_kv = 0
                for half in range(2):
                    for (srcb, base, width) in [(kst[half], 0, 8192),
                                                (vst[half], 8192, 8448)]:
                        st_kv = P.dma(
                            "scalar", sS,
                            bass.AP(ag_in, half * ROW + base,
                                    [[KVW, 128], [1, width]]),
                            srcb[:, :], waits=[(vv, v_ve), (vv, v_ke)])
                P.step("gpsimd", lambda e, r: e.collective_compute(
                    "AllGather", OP.bypass, replica_groups=[list(range(NCORE))],
                    ins=[ag_in.ap().opt()], outs=[ag_out.ap().opt()]),
                    waits=[(sS, st_kv), (sG, g_stats)], inc=(cc, 1))
                P.run(nc, sems)

        ctxC = ExitStack()
        with ctxC:
            w1st = [sb(f"w1st{i}", [128, 16 * 256], F32, ctxC) for i in range(2)]
            w2st = [sb(f"w2st{i}", [128, 16 * 256], F32, ctxC) for i in range(2)]
            wq1 = [sb(f"wq1_{i}", [128, 16 * 256], BF16, ctxC) for i in range(2)]
            wq2 = [sb(f"wq2_{i}", [128, 16 * 256], BF16, ctxC) for i in range(2)]
            w3st = [sb(f"w3st{i}", [128, D], F32, ctxC) for i in range(2)]
            wq3 = [sb(f"wq3_{i}", [128, D], BF16, ctxC) for i in range(2)]
            wgtb = sb("wgtb", [128, 16 * 256], BF16, ctxC)
            silb = [sb(f"silb{i}", [128, 256], F32, ctxC) for i in range(2)]
            h2b = sb("h2b", [128, 256], F32, ctxC)
            hbuf = [sb(f"hbuf{i}", [128, 256], F32, ctxC) for i in range(2)]
            gath = sb("gath", [1, 256], F32, ctxC)
            gath_b = sb("gath_b", [1, 128], BF16, ctxC)
            pH1 = [psum(f"pH1_{i}", [128, 256], F32, ctxC) for i in range(3)]
            pH2 = [psum(f"pH2_{i}", [128, 256], F32, ctxC) for i in range(3)]
            pB = psum("pBb", [128, 8], F32, ctxC)

            P = Phase(fence_all=FENCE_ALL["C"])
            g_st = P.dma("gpsimd", sG,
                         bass.AP(gath_b, 0, [[128, 1], [16, 8], [1, 16]]),
                         bass.AP(ag_out, KV_AREA, [[1, 1], [2 * ROW, 8], [1, 16]]),
                         waits=[(cc, 1)])
            P.step("vector", lambda e, r: e.tensor_copy(gath[0:1, 0:128],
                                                        gath_b[0:1, :]),
                   waits=[(sG, g_st)], fence=True)

            def dec(e, r, F):
                last = None
                for j in range(3):
                    F(e.tensor_reduce(gath[0:1, 160:168],
                                      bass.AP(gath, j, [[256, 1], [16, 8], [3, 3]]),
                                      AX.X, OP.add))
                    last = e.reduce_sum(stats_f[0:1, 8 + j:9 + j],
                                        gath[0:1, 160:168], axis=AX.X)
                    if j < 2:
                        F(last)
                return last
            P.step("vector", dec, fence=True)
            NW = [float(DFF * D), float(DFF * D), float(D * DFF)]

            def scl(e, r, F):
                last = None
                for j in range(3):
                    F(e.tensor_scalar(stats_f[0:1, 8 + j:9 + j],
                                      stats_f[0:1, 8 + j:9 + j],
                                      1.0 / NW[j], None, OP.mult))
                    F(e.tensor_scalar(stats_f[0:1, 8 + j:9 + j],
                                      stats_f[0:1, 8 + j:9 + j],
                                      float(Q_EPS), None, OP.max))
                    e.tensor_scalar(stats_f[0:1, 12 + j:13 + j],
                                    stats_f[0:1, 8 + j:9 + j], 0.5, None, OP.mult)
                    last = e.reciprocal(stats_f[0:1, j:j + 1],
                                        stats_f[0:1, 8 + j:9 + j])
                return last
            v_scl = P.step("vector", scl, inc=(vv, 1), fence=True)
            t_bc = None
            for j in range(6):
                so = j if j < 3 else (12 + j - 3)
                t_bc = P.step("tensor", (lambda e, r, j=j, so=so: e.matmul(
                    pB[:, j:j + 1], ones_row[0:1, :], stats_f[0:1, so:so + 1],
                    start=(j == 0), stop=(j == 5), skip_group_check=True)),
                    waits=[(vv, v_scl)], inc=(tm, 1) if j == 5 else None)
            a_bc = P.step("scalar", lambda e, r: (
                e.copy(sw_b[:, 0:3], pB[:, 0:3]),
                e.copy(mh_b[:, 0:3], pB[:, 3:6]))[-1], waits=[(tm, t_bc)],
                inc=(aa, 1), fence=True)

            def ffd(e, r, F):
                e.tensor_scalar(f1t[:, :], inv_sx[:, :], mh_b[:, 0:1], None, OP.mult)
                F(e.tensor_scalar(f2t[:, :], inv_sx[:, :], mh_b[:, 1:2], None,
                                  OP.mult))
                e.tensor_mul(b1t[:, :], rsx[:, :], f1t[:, :])
                F(e.tensor_mul(b2t[:, :], rsx[:, :], f2t[:, :]))
                e.tensor_scalar(b1t[:, :], b1t[:, :], -1.0, None, OP.mult)
                return e.tensor_scalar(b2t[:, :], b2t[:, :], -1.0, None, OP.mult)
            v_ffd = P.step("vector", ffd, waits=[(aa, a_bc)], inc=(vv, 1),
                           fence=True)

            w3q_done = [0] * 64
            w3_tl = [0] * 64
            w3q_store = 0

            def emit_w3_load(P, j3):
                par = j3 % 2
                waits = [(vv, w3q_done[j3 - 2])] if j3 >= 2 else []
                w3_tl[j3] = P.dma("sync", sL, w3st[par][:, :], w3c.ap()[j3],
                                  waits=waits)

            def emit_w3_quant(P, j3):
                nonlocal w3q_store
                par = j3 % 2
                a_s3 = P.step("scalar", (lambda e, r, par=par: e.activation(
                    wq3[par][:, :], w3st[par][:, :], AF.Sign, bias=np.float32(0.5),
                    scale=sw_b[:, 2:3])), waits=[(sL, w3_tl[j3]), (aa, a_bc)],
                    inc=(aa, 1))
                P.step("vector", (lambda e, r, par=par: e.tensor_scalar(
                    wgtb[:, 0:D], w3st[par][:, :], sw_b[:, 2:3], 0.5, OP.mult,
                    OP.is_gt)), waits=[(aa, a_s3)], fence=True)
                w3q_done[j3] = P.step("vector", (lambda e, r, par=par:
                    e.scalar_tensor_tensor(wq3[par][:, :], wgtb[:, 0:D], 2.0,
                                           wq3[par][:, :], OP.mult, OP.add)),
                    inc=(vv, 1))
                w3q_store = P.dma("scalar", sS, w3q_dram.ap()[j3], wq3[par][:, :],
                                  waits=[(vv, w3q_done[j3])])
                if j3 + 2 < 64:
                    emit_w3_load(P, j3 + 2)

            w1done = [0] * 32
            w2done = [0] * 32
            wq_consume = [0] * 32
            h_mm_done = [0] * 32
            evac_done = [0] * (32 * NT)
            h_dma_t = [0] * (32 * NT)
            h_store = 0

            def emit_wload(P, j):
                par = j % 2
                waits = [(vv, wq_consume[j - 2])] if j >= 2 else []
                w1done[j] = P.dma(
                    "sync", sL,
                    bass.AP(w1st[par], 0, [[16 * 256, 128], [256, 16], [1, 256]]),
                    w1c.ap()[j].rearrange("(s p) c -> p s c", p=128), waits=waits)
                w2done[j] = P.dma(
                    "scalar", sS,
                    bass.AP(w2st[par], 0, [[16 * 256, 128], [256, 16], [1, 256]]),
                    w2c.ap()[j].rearrange("(s p) c -> p s c", p=128), waits=waits)

            def emit_quant(P, j):
                par = j % 2
                waits_q = [(sL, w1done[j]), (aa, a_bc)]
                if j >= 2:
                    waits_q.append((tm, h_mm_done[j - 2]))
                a_s1 = P.step("scalar", (lambda e, r, par=par: e.activation(
                    wq1[par][:, :], w1st[par][:, :], AF.Sign, bias=np.float32(0.5),
                    scale=sw_b[:, 0:1])), waits=waits_q, inc=(aa, 1))
                P.step("vector", (lambda e, r, par=par: e.tensor_scalar(
                    wgtb[:, :], w1st[par][:, :], sw_b[:, 0:1], 0.5, OP.mult,
                    OP.is_gt)), waits=[(aa, a_s1)], fence=True)
                P.step("vector", (lambda e, r, par=par: e.scalar_tensor_tensor(
                    wq1[par][:, :], wgtb[:, :], 2.0, wq1[par][:, :], OP.mult,
                    OP.add)))
                waits_q2 = [(sS, w2done[j]), (aa, a_bc)]
                if j >= 2:
                    waits_q2.append((tm, h_mm_done[j - 2]))
                a_s2 = P.step("scalar", (lambda e, r, par=par: e.activation(
                    wq2[par][:, :], w2st[par][:, :], AF.Sign, bias=np.float32(0.5),
                    scale=sw_b[:, 1:2])), waits=waits_q2, inc=(aa, 1))
                emit_w3_quant(P, 2 * j)
                P.step("vector", (lambda e, r, par=par: e.tensor_scalar(
                    wgtb[:, :], w2st[par][:, :], sw_b[:, 1:2], 0.5, OP.mult,
                    OP.is_gt)), waits=[(aa, a_s2)], fence=True)
                v_qw = P.step("vector", (lambda e, r, par=par: e.scalar_tensor_tensor(
                    wq2[par][:, :], wgtb[:, :], 2.0, wq2[par][:, :], OP.mult,
                    OP.add)), inc=(vv, 1))
                wq_consume[j] = v_qw
                emit_w3_quant(P, 2 * j + 1)
                if j + 2 < 32:
                    emit_wload(P, j + 2)

            emit_wload(P, 0)
            emit_wload(P, 1)
            emit_w3_load(P, 0)
            emit_w3_load(P, 1)
            emit_quant(P, 0)
            emit_quant(P, 1)
            for j in range(32):
                par = j % 2
                v_qw = wq_consume[j]
                t_h = None
                for tt in range(NT):
                    idx = j * NT + tt
                    ppar = idx % 3
                    for s_ in range(16):
                        waits = [(vv, v_qw)]
                        if s_ == 0 and idx >= 3:
                            waits.append((vv, evac_done[idx - 3]))
                        P.step("tensor", (lambda e, r, s_=s_, tt=tt, par=par, ppar=ppar:
                            e.matmul(pH1[ppar][:, :],
                                     xq_fm[:, s_ * 1024 + tt * 128:
                                           s_ * 1024 + (tt + 1) * 128],
                                     wq1[par][:, s_ * 256:(s_ + 1) * 256],
                                     start=(s_ == 0), stop=(s_ == 15))),
                            waits=waits)
                    for s_ in range(16):
                        t_h = P.step("tensor", (lambda e, r, s_=s_, tt=tt, par=par,
                                                ppar=ppar:
                            e.matmul(pH2[ppar][:, :],
                                     xq_fm[:, s_ * 1024 + tt * 128:
                                           s_ * 1024 + (tt + 1) * 128],
                                     wq2[par][:, s_ * 256:(s_ + 1) * 256],
                                     start=(s_ == 0), stop=(s_ == 15))),
                            inc=(tm, 1) if s_ == 15 else None)
                    sil_waits = [(tm, t_h), (vv, v_ffd)]
                    if idx >= 2:
                        sil_waits.append((vv, evac_done[idx - 2]))
                    spar = idx % 2
                    a_sil = P.step("scalar", (lambda e, r, tt=tt, ppar=ppar, spar=spar:
                        e.activation(silb[spar][:, :], pH1[ppar][:, :], AF.Silu,
                                     bias=b1t[:, tt:tt + 1], scale=f1t[:, tt:tt + 1])),
                        waits=sil_waits, inc=(aa, 1))
                    P.step("vector", (lambda e, r, tt=tt, ppar=ppar: e.tensor_scalar(
                        h2b[:, :], pH2[ppar][:, :], f2t[:, tt:tt + 1], b2t[:, tt:tt + 1],
                        OP.mult, OP.add)), waits=[(tm, t_h), (vv, v_ffd)],
                        fence=True)
                    hb = hbuf[idx % 2]
                    ev_waits = [(aa, a_sil)]
                    if idx >= 2:
                        ev_waits.append((sL, h_dma_t[idx - 2]))
                    evac_done[idx] = P.step("vector", (lambda e, r, hb=hb, spar=spar:
                        e.tensor_mul(
                        hb[:, :], silb[spar][:, :], h2b[:, :])), waits=ev_waits,
                        inc=(vv, 1))
                    h_mm_done[j] = t_h
                    h_dma_t[idx] = P.dma(
                        "sync", sL,
                        bass.AP(h_dram, tt * 128 * DFF + j * 256, [[DFF, 128], [1, 256]]),
                        hb[:, :], waits=[(vv, evac_done[idx])])
                    h_store = h_dma_t[idx]
                if j + 2 < 32:
                    emit_quant(P, j + 2)
            P.run(nc, sems)

        ctxAC.close()

        s_sb = sb("s_sb", [128, NT * D], BF16)
        ctxDE = ExitStack()
        ctxDE.__enter__()
        q_stage = sb("q_stage", [128, H * 512], BF16, ctxDE)
        ctxD = ExitStack()
        with ctxD:
            O_all = sb("O_all", [128, 2 * 16 * 528], F32, ctxD)
            kv_sb = [sb(f"kv_sb{i}", [128, KVW], BF16, ctxD) for i in range(2)]
            P_sb = [sb(f"P_sb{i}", [128, 512], BF16, ctxD) for i in range(4)]
            ctxDP = ExitStack()
            ctxDP.__enter__()
            pS = [psum(f"pS{i}", [128, 512], F32, ctxDP) for i in range(4)]
            pO = psum("pO", [128, 2048], F32, ctxDP)

            P = Phase(fence_all=FENCE_ALL["D"])
            P.want_regs("gpsimd", ["kr"])
            P.want_regs("sync", ["qr"])

            v_init = P.step("vector", lambda e, r: e.memset(O_all[:, :], 0.0),
                            inc=(vv, 1))

            tmD0 = tm.v
            pair_mm = [tmD0 + 320 * (i + 1) + (2 if i < NPAIR - 1 else 0)
                       for i in range(NPAIR)]
            kv_done = [0] * NPAIR
            last_exp = [0, 0, 0, 0]
            last_av = [0, 0, 0, 0]
            last_oacc = v_init

            pair_last_st = [tmD0 + 320 * i + 308 for i in range(NPAIR)]
            q_done = [0] * NPAIR
            for i in range(NPAIR):
                par = i % 2
                waits = [(sL, v_tab), (cc, 1)]
                if i >= 2:
                    waits.append((tm, pair_mm[i - 2]))
                P.step("gpsimd", (lambda e, r, i=i: e.reg_load(
                    r["kr"], bass.AP(tab_sb, TAB_K + i, [[128, 1], [1, 1]]))),
                    waits=waits)
                kv_done[i] = P.step("gpsimd", (lambda e, r, par=par: e.dma_start(
                    out=kv_sb[par][:, :],
                    in_=bass.AP(ag_out, r["kr"], [[KVW, 128], [1, KVW]]))),
                    inc=(sG, 16))
            for i in range(NPAIR):
                q_waits = [(sL, v_tab), (sS, st_q_last), (cc, 1)]
                if i >= 1:
                    q_waits.append((tm, pair_last_st[i - 1]))
                if i < 2:
                    q_done[i] = P.dma("sync", sL, q_stage[:, :], q_dram.ap()[i],
                                      waits=q_waits)
                else:
                    P.step("sync", (lambda e, r, i=i: e.reg_load(
                        r["qr"], bass.AP(tab_sb, TAB_Q + i, [[128, 1], [1, 1]]))),
                        waits=q_waits)
                    q_done[i] = P.step("sync", (lambda e, r: e.dma_start(
                        out=q_stage[:, :],
                        in_=bass.AP(q_dram, r["qr"], [[H * 512, 128], [1, H * 512]]))),
                        inc=(sL, 16))

            tiles = [(i, h, kts) for i in range(NPAIR)
                     for h in range(H) for kts in range(4)]
            NTILE = len(tiles)
            exp_of = [0] * NTILE
            for g in range(NTILE + 2):
                if g < NTILE:
                    i, h, kts = tiles[g]
                    par = i % 2
                    b = g % 4
                    waits = [(sG, kv_done[i]), (sL, q_done[i]),
                             (aa, last_exp[b]), (vv, v_init)]
                    t_st = P.step("tensor", (lambda e, r, par=par, h=h, kts=kts, b=b:
                        e.matmul(pS[b][:, :],
                                 kv_sb[par][:, h * 512 + kts * 128:
                                            h * 512 + (kts + 1) * 128],
                                 q_stage[:, h * 512:(h + 1) * 512],
                                 start=True, stop=True)),
                        waits=waits, inc=(tm, 1))
                    if h == 15 and kts == 3:
                        assert tm.v == pair_last_st[i], (i, tm.v, pair_last_st[i])
                    if i < 2:
                        v_m = P.step("vector", (lambda e, r, b=b, kts=kts:
                            e.tensor_tensor(pS[b][:, :], pS[b][:, :],
                                            mask_sb[:, kts * 512:(kts + 1) * 512],
                                            OP.add)),
                            waits=[(tm, t_st), (sG, v_mask)], inc=(vv, 1))
                        exp_waits = [(vv, v_m)]
                    else:
                        exp_waits = [(tm, t_st)]
                    exp_waits.append((tm, last_av[b]))
                    a_e = P.step("scalar", (lambda e, r, b=b: e.activation(
                        P_sb[b][:, :], pS[b][:, :], AF.Exp, bias=EXPC,
                        scale=SC_QK)), waits=exp_waits, inc=(aa, 1),
                        fence=DEXP)
                    last_exp[b] = a_e
                    exp_of[g] = a_e
                if g >= 2:
                    i2, h2, kts2 = tiles[g - 2]
                    par2 = i2 % 2
                    b2 = (g - 2) % 4
                    av_waits = [(aa, exp_of[g - 2])]
                    if kts2 == 0:
                        av_waits.append((vv, last_oacc))
                    t_av = None
                    for qs in range(4):
                        t_av = P.step("tensor", (lambda e, r, b=b2, qs=qs, par=par2,
                                                 h=h2, kts=kts2:
                            e.matmul(pO[:, qs * 512:qs * 512 + 129],
                                     P_sb[b][:, qs * 128:(qs + 1) * 128],
                                     kv_sb[par][:, 8192 + (h * 4 + kts) * 132:
                                                8192 + (h * 4 + kts) * 132 + 129],
                                     start=(kts == 0), stop=(kts == 3),
                                     skip_group_check=True)),
                            waits=av_waits if qs == 0 else [], inc=(tm, 1))
                    last_av[b2] = t_av
                    if kts2 == 3:
                        pO_ap = bass.AP(pO, 0, [[2048, 128], [512, 4], [1, 129]])
                        for half in range(2):
                            oap = bass.AP(O_all, half * 8448 + h2 * 528,
                                          [[2 * 16 * 528, 128], [132, 4], [1, 129]])
                            st_ = P.step("vector", (lambda e, r, oap=oap, i=i2,
                                                    half=half:
                                e.scalar_tensor_tensor(
                                    oap, pO_ap,
                                    tabm_sb[:, 2 * i + half:2 * i + half + 1],
                                    oap, OP.mult, OP.add)),
                                waits=[(tm, last_av[0]), (tm, last_av[1]),
                                       (tm, last_av[2]), (tm, last_av[3]),
                                       (sL, v_tabm)],
                                inc=(vv, 1) if half == 1 else None, fence=DSTT)
                        last_oacc = st_
                        if h2 == 15:
                            assert tm.v == pair_mm[i2], (i2, tm.v, pair_mm[i2])
            P.run(nc, sems)
            ctxDP.close()

            o_fm = kv_sb[0]
            pE1ctx = ExitStack()
            with pE1ctx:
                o_nrm = [sb(f"o_nrm{i}", [128, 128], BF16, pE1ctx) for i in range(2)]
                pOT = [psum(f"pOT{i}", [128, 128], BF16, pE1ctx) for i in range(2)]
                P = Phase(fence_all=FENCE_ALL["E1"])
                P.step("vector", lambda e, r: e.reciprocal(
                    bass.AP(O_all, 128, [[2 * 16 * 528, 128], [132, 128], [1, 1]]),
                    bass.AP(O_all, 128, [[2 * 16 * 528, 128], [132, 128], [1, 1]])),
                    fence=True)
                last_tr = [0, 0]
                last_cp = [0, 0]
                for h in range(H):
                    for qt in range(8):
                        b = (h * 8 + qt) % 2
                        off = (qt // 4) * 8448 + h * 528 + (qt % 4) * 132
                        v_n = P.step("vector", (lambda e, r, off=off, b=b:
                            e.tensor_scalar(o_nrm[b][:, :],
                                            O_all[:, off:off + 128],
                                            O_all[:, off + 128:off + 129],
                                            None, OP.mult)),
                            waits=[(tm, last_tr[b])], inc=(vv, 1), fence=E1VN)
                        t_t = P.step("tensor", (lambda e, r, b=b: e.transpose(
                            pOT[b][:, :], o_nrm[b][:, :], id_sb[:, :])),
                            waits=[(vv, v_n), (aa, last_cp[b])], inc=(tm, 1))
                        last_tr[b] = t_t
                        off2 = h * 1024 + qt * 128
                        last_cp[b] = P.step("scalar", (lambda e, r, off2=off2, b=b:
                            e.copy(o_fm[:, off2:off2 + 128], pOT[b][:, :])),
                            waits=[(tm, t_t)], inc=(aa, 1), fence=E1CP)
                P.run(nc, sems)

            ctxE = ExitStack()
            with ctxE:
                woc = [sb(f"woc{i}", [128, 512], BF16, ctxE) for i in range(2)]
                pSP = [psum(f"pSP{i}", [128, 512], F32, ctxE) for i in range(8)]
                P = Phase(fence_all=FENCE_ALL["E"])
                v_se = [0] * 8
                wo_last = [0, 0]
                for dc in range(4):
                    t_s = [None] * 8
                    for s_ in range(16):
                        wpar = s_ % 2
                        wo_waits = [(tm, wo_last[wpar])] if (dc > 0 or s_ >= 2) else []
                        wo_done = P.dma(
                            "gpsimd", sG, woc[wpar][:, :],
                            bass.AP(woutT, s_ * 128 * D + dc * 512,
                                    [[D, 128], [1, 512]]),
                            waits=wo_waits)
                        for qt in range(8):
                            waits = [(sG, wo_done)]
                            if s_ == 0:
                                waits.append((vv, v_se[qt]))
                            t_s[qt] = P.step("tensor", (lambda e, r, s_=s_, qt=qt,
                                                        dc=dc, wpar=wpar:
                                e.matmul(pSP[qt][:, :],
                                         o_fm[:, s_ * 1024 + qt * 128:
                                              s_ * 1024 + (qt + 1) * 128],
                                         woc[wpar][:, :],
                                         start=(s_ == 0), stop=(s_ == 15))),
                                waits=waits,
                                inc=(tm, 1) if qt == 7 else None)
                        wo_last[wpar] = t_s[7]
                    for qt in range(8):
                        v_se[qt] = P.step("vector", (lambda e, r, qt=qt, dc=dc:
                            e.tensor_copy(s_sb[:, qt * D + dc * 512:
                                               qt * D + (dc + 1) * 512],
                                          pSP[qt][:, :])),
                            waits=[(tm, t_s[7])], inc=(vv, 1))
                P.run(nc, sems)

        ctxDE.close()

        ctxF = ExitStack()
        with ctxF:
            hst = [sb(f"hst{i}", [128, DFF], F32, ctxF) for i in range(2)]
            hq_fm = sb("hq_fm", [128, 64 * 512], BF16, ctxF)
            hq_tm = sb("hq_tm", [128, 2048], BF16, ctxF)
            i32f = sb("i32f", [128, 2048], I32, ctxF)
            w3t = [sb(f"w3t{i}", [128, 4 * 512], BF16, ctxF) for i in range(2)]
            g_b = sb("g_b", [128, 512], BF16, ctxF)
            xre = [sb(f"xre{i}", [128, 512], F32, ctxF) for i in range(2)]
            ob = [sb(f"ob{i}", [128, 512], F32, ctxF) for i in range(2)]
            hsq = sb("hsq", [128, 2048], F32, ctxF)
            acc4 = sb("acc4", [128, 4], F32, ctxF)
            pG = [psum(f"pG{i}", [128, 512], F32, ctxF) for i in range(4)]
            pFT = [psum(f"pFT{i}", [128, 128], BF16, ctxF) for i in range(4)]

            P = Phase(fence_all=FENCE_ALL["F"])

            def newtonF(e, src_ap, eps):
                e.reciprocal(tmp1[:, 1:2], tmp1[:, 0:1])
                e.tensor_mul(tmp1[:, 2:3], tmp1[:, 1:2], tmp1[:, 1:2])
                e.tensor_scalar(tmp1[:, 3:4], src_ap, 1.0, float(eps), OP.mult, OP.add)
                e.tensor_mul(tmp1[:, 2:3], tmp1[:, 2:3], tmp1[:, 3:4])
                e.tensor_scalar(tmp1[:, 2:3], tmp1[:, 2:3], -0.5, 1.5, OP.mult, OP.add)
                return e.tensor_mul(tmp1[:, 1:2], tmp1[:, 1:2], tmp1[:, 2:3])

            h_loaded = [0] * NT
            h_loaded[0] = P.dma("sync", sL, hst[0][:, :], h_dram.ap()[0],
                                waits=[(sL, h_store)])
            h_loaded[1] = P.dma("sync", sL, hst[1][:, :], h_dram.ap()[1])

            t_trF_used = [0, 0, 0, 0]
            a_cpF = [0, 0, 0, 0]
            storm_done = 0
            st_out = 0
            x_loaded = {}
            g_done = [0] * 4
            w3t_last = [0, 0]
            last_vo = 0
            last_vo_bo = [0, 0]
            st_out_bo = [0, 0]
            a_hq = 0
            for tg in range(2):
                for tl in range(4):
                    tt = tg * 4 + tl
                    par = tt % 2
                    for q4 in range(4):
                        sl = slice(q4 * 2048, (q4 + 1) * 2048)
                        P.step("vector", (lambda e, r, par=par, sl=sl:
                            e.tensor_mul(hsq[:, :], hst[par][:, sl], hst[par][:, sl])),
                            waits=[(sL, h_loaded[tt])])
                        P.step("vector", (lambda e, r, q4=q4: e.reduce_sum(
                            acc4[:, q4:q4 + 1], hsq[:, :], axis=AX.X)), fence=True)
                    def vbnF(e, r, F):
                        F(e.reduce_sum(tmp2[:, 5:6], acc4[:, :], axis=AX.X))
                        return e.tensor_scalar(tmp2[:, 0:1], tmp2[:, 5:6], 1.0 / DFF,
                                               None, OP.mult)
                    v_bn = P.step("vector", vbnF, inc=(vv, 1), fence=True)
                    a1 = P.step("scalar", lambda e, r: e.activation(
                        tmp1[:, 0:1], tmp2[:, 0:1], AF.Sqrt, bias=np.float32(RMS_EPS)),
                        waits=[(vv, v_bn)], inc=(aa, 1))
                    P.step("vector", (lambda e, r: newtonF(e, tmp2[:, 0:1], RMS_EPS)),
                           waits=[(aa, a1)])
                    P.step("vector", (lambda e, r, par=par: e.reduce_max(
                        tmp1[:, 6:7], hst[par][:, :], axis=AX.X,
                        apply_absolute_value=True)), fence=True)

                    def scalesF(e, r, F, tt=tt):
                        F(e.tensor_mul(tmp1[:, 7:8], tmp1[:, 1:2], tmp1[:, 6:7]))
                        F(e.tensor_scalar(tmp1[:, 7:8], tmp1[:, 7:8], float(Q_EPS),
                                          None, OP.max))
                        F(e.tensor_scalar(tmp2[:, 3:4], tmp1[:, 7:8], 1.0 / 127.0,
                                          None, OP.mult))
                        e.tensor_scalar(f3t[:, tt:tt + 1], tmp2[:, 3:4], mh_b[:, 2:3],
                                        None, OP.mult)
                        F(e.reciprocal(tmp2[:, 1:2], tmp1[:, 7:8]))
                        F(e.tensor_scalar(tmp2[:, 1:2], tmp2[:, 1:2], 127.0, None,
                                          OP.mult))
                        return e.tensor_mul(tmp2[:, 2:3], tmp2[:, 1:2], tmp1[:, 1:2])
                    P.step("vector", scalesF, fence=True)
                    for q4 in range(4):
                        sl = slice(q4 * 2048, (q4 + 1) * 2048)
                        P.step("vector", (lambda e, r, par=par, sl=sl: e.tensor_scalar(
                            hst[par][:, sl], hst[par][:, sl], tmp2[:, 2:3], MAGIC,
                            OP.mult, OP.add)), fence=True)
                        v_u = P.step("vector", (lambda e, r, par=par, sl=sl, q4=q4:
                            e.tensor_scalar(hst[par][:, sl], hst[par][:, sl], MAGIC,
                                            0.0, OP.subtract, OP.add,
                                            accum_out=acc4[:, q4:q4 + 1])),
                            inc=(vv, 1))
                        P.step("scalar", (lambda e, r, par=par, sl=sl: e.copy(
                            i32f[:, :], hst[par][:, sl])),
                            waits=[(vv, v_u), (tm, t_trF_used[0]),
                                   (tm, t_trF_used[1]), (tm, t_trF_used[2]),
                                   (tm, t_trF_used[3])], fence=True)
                        a_hq = P.step("scalar", lambda e, r: e.copy(hq_tm[:, :],
                                                                    i32f[:, :]),
                                      inc=(aa, 1))
                        for s2 in range(16):
                            ds = q4 * 16 + s2
                            btr = ds % 4
                            t_t = P.step("tensor", (lambda e, r, s2=s2, btr=btr:
                                e.transpose(pFT[btr][:, :],
                                            hq_tm[:, s2 * 128:(s2 + 1) * 128],
                                            id_sb[:, :])),
                                waits=[(aa, a_hq), (vv, a_cpF[btr]),
                                       (tm, storm_done)], inc=(tm, 1))
                            t_trF_used[btr] = t_t
                            a_cpF[btr] = P.step("vector", (lambda e, r, ds=ds, tl=tl,
                                                           btr=btr:
                                e.tensor_copy(hq_fm[:, ds * 512 + tl * 128:
                                                    ds * 512 + (tl + 1) * 128],
                                              pFT[btr][:, :])),
                                waits=[(tm, t_t)], inc=(vv, 1), fence=True)
                    def vrsF(e, r, F, tt=tt):
                        F(e.reduce_sum(rsh[:, tt:tt + 1], acc4[:, :], axis=AX.X))
                        F(e.tensor_mul(tmp2[:, 4:5], rsh[:, tt:tt + 1],
                                       f3t[:, tt:tt + 1]))
                        return e.tensor_scalar(b3t[:, tt:tt + 1], tmp2[:, 4:5], -1.0,
                                               None, OP.mult)
                    v_rs = P.step("vector", vrsF, inc=(vv, 1), fence=True)
                    nxt = tt + 2
                    if nxt < NT:
                        h_loaded[nxt] = P.dma("sync", sL, hst[par][:, :],
                                              h_dram.ap()[nxt],
                                              waits=[(vv, v_rs), (aa, a_hq)])
                hq_ready = vv.v

                for dq in range(4):
                    w3_done = [0] * 16
                    for s4 in range(2):
                        w3_done[s4] = P.dma(
                            "scalar", sS,
                            bass.AP(w3t[s4 % 2], 0, [[4 * 512, 128], [512, 4],
                                                     [1, 512]]),
                            bass.AP(w3q_dram, s4 * 4 * 128 * D + dq * 512,
                                    [[D, 128], [128 * D, 4], [1, 512]]),
                            waits=[(sS, w3q_store), (tm, w3t_last[s4 % 2])])
                    ds_mm = [0] * 64
                    for s4 in range(16):
                        wpar = s4 % 2
                        for dsl in range(4):
                            ds = s4 * 4 + dsl
                            for tl in range(4):
                                waits = [(sS, w3_done[s4]), (vv, hq_ready)]
                                if ds == 0:
                                    waits.append((aa, g_done[tl]))
                                ds_mm[ds] = P.step("tensor", (lambda e, r, ds=ds,
                                                              tl=tl, wpar=wpar,
                                                              dsl=dsl:
                                    e.matmul(pG[tl][:, :],
                                             hq_fm[:, ds * 512 + tl * 128:
                                                   ds * 512 + (tl + 1) * 128],
                                             w3t[wpar][:, dsl * 512:
                                                       (dsl + 1) * 512],
                                             start=(ds == 0), stop=(ds == 63))),
                                    waits=waits,
                                    inc=(tm, 1) if tl == 3 else None)
                        w3t_last[wpar] = ds_mm[s4 * 4 + 3]
                        if s4 + 2 < 16:
                            w3_done[s4 + 2] = P.dma(
                                "scalar", sS,
                                bass.AP(w3t[wpar], 0, [[4 * 512, 128], [512, 4],
                                                       [1, 512]]),
                                bass.AP(w3q_dram, (s4 + 2) * 4 * 128 * D + dq * 512,
                                        [[D, 128], [128 * D, 4], [1, 512]]),
                                waits=[(tm, ds_mm[s4 * 4 + 3])])
                    storm_done = ds_mm[63]
                    for tl in range(4):
                        tt = tg * 4 + tl
                        bo = tl % 2
                        xk = (tg, dq, tl)
                        x_loaded[xk] = P.dma(
                            "sync", sL, xre[bo][:, :],
                            bass.AP(xin, (tt * 128) * D + dq * 512,
                                    [[D, 128], [1, 512]]),
                            waits=[(vv, last_vo_bo[bo])])
                        a_g = P.step("scalar", (lambda e, r, tl=tl, tt=tt:
                            e.activation(g_b[:, :], pG[tl][:, :], AF.Silu,
                                         bias=b3t[:, tt:tt + 1],
                                         scale=f3t[:, tt:tt + 1])),
                            waits=[(tm, storm_done), (vv, last_vo)], inc=(aa, 1),
                            fence=True)
                        g_done[tl] = a_g
                        def vcomb(e, r, F, tt=tt, dq=dq, bo=bo):
                            F(e.tensor_mul(g_b[:, :],
                                           s_sb[:, tt * D + dq * 512:
                                                tt * D + (dq + 1) * 512],
                                           g_b[:, :]))
                            return e.tensor_add(ob[bo][:, :], xre[bo][:, :],
                                                g_b[:, :])
                        v_o = P.step("vector", vcomb,
                            waits=[(aa, a_g), (sL, x_loaded[xk]),
                                   (sL, st_out_bo[bo])], inc=(vv, 1), fence=True)
                        last_vo = v_o
                        last_vo_bo[bo] = v_o
                        st_out_bo[bo] = P.dma(
                            "sync", sL,
                            bass.AP(out, (tt * 128) * D + dq * 512,
                                    [[D, 128], [1, 512]]),
                            ob[bo][:, :], waits=[(vv, v_o)])
                        st_out = st_out_bo[bo]
            P.step("sync", None, waits=[(sL, st_out)])
            P.run(nc, sems)

    return nc



def _host_prep(L0_attn_x, L0_attn_w_in, L0_attn_w_out, L0_ffn_w1, L0_ffn_w2,
               L0_ffn_w3):
    c = np.ascontiguousarray
    x = np.asarray(L0_attn_x, np.float32).reshape(S, D)
    w_in = np.asarray(L0_attn_w_in, np.float32)
    w_out = np.asarray(L0_attn_w_out, np.float32)
    w1 = np.asarray(L0_ffn_w1, np.float32)
    w2 = np.asarray(L0_ffn_w2, np.float32)
    w3 = np.asarray(L0_ffn_w3, np.float32)

    wqkv = c(w_in.reshape(H, 3, DH, D).transpose(1, 0, 3, 2))
    woutT = c(w_out.T.reshape(16, 128, D))
    w1c = c(w1.T.reshape(D, 32, 256).transpose(1, 0, 2))
    w2c = c(w2.T.reshape(D, 32, 256).transpose(1, 0, 2))
    w3c = c(w3.T.reshape(64, 128, D))

    kk = np.arange(512).reshape(4, 128)
    qq = np.arange(512)
    masks = np.where(kk[:, :, None] <= qq[None, None, :], 0.0, MASKVAL
                     ).astype(np.float32)
    ident = np.eye(128, dtype=np.float32)

    shared = dict(wqkv=wqkv, woutT=woutT, w1c=w1c, w2c=w2c, w3c=w3c,
                  masks=masks, ident=ident)

    def rowof(g):
        return 2 * g if g <= 7 else 2 * (15 - g) + 1

    in_maps = []
    for core in range(NCORE):
        chA, chB = core, 15 - core
        xc = c(np.concatenate([x[CH * chA:CH * (chA + 1)],
                               x[CH * chB:CH * (chB + 1)]], axis=0))
        w1s_ = c(w1[1024 * core:1024 * (core + 1)].reshape(2, 128, 8192))
        w2s_ = c(w2[1024 * core:1024 * (core + 1)].reshape(2, 128, 8192))
        w3s_ = c(w3[256 * core:256 * (core + 1)].reshape(2, 128, 8192))
        pairs = [(0, chA), (1, chB)]
        pairs += [(0, j) for j in range(chA)]
        pairs += [(1, j) for j in range(chB)]
        assert len(pairs) == NPAIR
        tabv = np.zeros(128, np.int32)
        tabmv = np.zeros((128, 64), np.float32)
        for i, (qsel, kch) in enumerate(pairs):
            tabv[TAB_K + i] = rowof(kch) * ROW
            tabv[TAB_Q + i] = qsel * (128 * H * 512)
            tabmv[:, 2 * i + qsel] = 1.0
        in_maps.append(dict(shared, xin=xc, w1s=w1s_, w2s=w2s_, w3s=w3s_,
                            tabi=tabv.reshape(1, 128), tabm=tabmv))
    return in_maps


def kernel(L0_attn_x, L0_attn_w_in, L0_attn_w_out, L0_ffn_w1, L0_ffn_w2,
           L0_ffn_w3):
    in_maps = _host_prep(L0_attn_x, L0_attn_w_in, L0_attn_w_out, L0_ffn_w1,
                         L0_ffn_w2, L0_ffn_w3)
    nc = build_program()
    res = run_bass_kernel_spmd(nc, in_maps, core_ids=list(range(NCORE)))
    full = np.zeros((1, S, D), np.float32)
    for core in range(NCORE):
        o = np.asarray(res.results[core]["outt"], np.float32)
        chA, chB = core, 15 - core
        full[0, CH * chA:CH * (chA + 1)] = o[0:CH]
        full[0, CH * chB:CH * (chB + 1)] = o[CH:2 * CH]
    return full


if __name__ == "__main__":
    nc = build_program()
    print("build OK; instructions:", nc.n_instructions())
